# version 11
# baseline (speedup 1.0000x reference)
import numpy as np
import concourse.bass as bass
import concourse.mybir as mybir
from concourse.bass_utils import run_bass_kernel_spmd

F32 = mybir.dt.float32
BF16 = mybir.dt.bfloat16
U8 = mybir.dt.uint8
U32 = mybir.dt.uint32
AF = mybir.ActivationFunctionType
OP = mybir.AluOpType
AX = mybir.AxisListType

S, D = 8192, 1024
T = 2048
NT = T // 128
NTF = S // 128
KC = D // 128
I3 = 2816
ISL = 352
NIS = I3 // ISL
NI128 = I3 // 128
DH = 64
C = 64
NCH = S // C
CLIP = 60.0

ATTN = True
DEBUG = False


class KB:
    def __init__(self):
        self.nc = bass.Bass("TRN2", target_bir_lowering=False)
        self.sems = {}
        self.cnt = {}

    def sem(self, name):
        if name not in self.sems:
            self.sems[name] = self.nc.alloc_semaphore(name)
            self.cnt[name] = 0
        return self.sems[name]

    def val(self, name):
        return self.cnt.get(name, 0)

    def inc(self, name, by=16):
        self.sem(name)
        self.cnt[name] += by
        return self.sems[name], self.cnt[name]

    def dma(self, eng, out, in_, name="dml", transpose=False):
        s, _ = self.inc(name)
        eng.dma_start(out, in_, transpose=transpose).then_inc(s, 16)

    def wait(self, eng, name, val=None):
        if name in self.sems:
            eng.wait_ge(self.sems[name], self.cnt[name] if val is None else val)

    def flush(self, eng, inst, name="esync"):
        s, n = self.inc(name, 1)
        inst.then_inc(s, 1)
        eng.wait_ge(s, n)


def build():
    kb = KB()
    nc = kb.nc
    _scope = [None, 0]

    def phase(name):
        if _scope[0] is not None:
            nc.leave_named_scope(_scope[0], _scope[1], False)
            _scope[0] = None
        if name is not None:
            _scope[1], _ = nc.enter_named_scope(name, False)
            _scope[0] = name

    x_in = nc.dram_tensor("x", [T, D], F32, kind="ExternalInput").ap()
    wkvqab_in = nc.dram_tensor("wkvqab", [D, 776], F32, kind="ExternalInput").ap()
    bab_in = nc.dram_tensor("bab", [1, 8], F32, kind="ExternalInput").ap()
    wg_in = nc.dram_tensor("wgm", [D, 256], F32, kind="ExternalInput").ap()
    wo_in = nc.dram_tensor("wom", [256, D], F32, kind="ExternalInput").ap()
    gn_in = nc.dram_tensor("gnm", [1, 256], F32, kind="ExternalInput").ap()
    wgate_in = nc.dram_tensor("wgate", [I3 // 8, D], F32, kind="ExternalInput").ap()
    wup_in = nc.dram_tensor("wup", [I3 // 8, D], F32, kind="ExternalInput").ap()
    wdown_in = nc.dram_tensor("wdown", [D // 8, I3], F32, kind="ExternalInput").ap()
    mlow_in = nc.dram_tensor("mlow", [128, 128], F32, kind="ExternalInput").ap()
    tri_in = nc.dram_tensor("tri", [128, 64], F32, kind="ExternalInput").ap()
    mupp_in = nc.dram_tensor("mupp", [128, 128], F32, kind="ExternalInput").ap()
    mupi_in = nc.dram_tensor("mupi", [128, 128], F32, kind="ExternalInput").ap()
    out_ext = nc.dram_tensor("out", [T, D], F32, kind="ExternalOutput").ap()
    if DEBUG:
        dbg_kt = nc.dram_tensor("dbg_kt", [S, 256], BF16, kind="ExternalOutput").ap()
        dbg_qt = nc.dram_tensor("dbg_qt", [S, 256], BF16, kind="ExternalOutput").ap()
        dbg_vt = nc.dram_tensor("dbg_vt", [S, 256], BF16, kind="ExternalOutput").ap()
        dbg_sc = nc.dram_tensor("dbg_sc", [S, 16], F32, kind="ExternalOutput").ap()
        dbg_stk = nc.dram_tensor("dbg_stk", [128, 2 * 6 * NCH], F32,
                                 kind="ExternalOutput").ap()
        dbg_y = nc.dram_tensor("dbg_y", [S, 256], F32, kind="ExternalOutput").ap()
        dbg_kf = nc.dram_tensor("dbg_kf", [256, S], BF16, kind="ExternalOutput").ap()
        dbg_kw = nc.dram_tensor("dbg_kw", [128, 4096], BF16, kind="ExternalOutput").ap()
        dbg_ffn = nc.dram_tensor("dbg_ffn", [T, D], F32, kind="ExternalOutput").ap()
        dbg_sg = nc.dram_tensor("dbg_sg", [S, 256], F32, kind="ExternalOutput").ap()
        dbg_rstd = nc.dram_tensor("dbg_rstd", [S, 1], F32, kind="ExternalOutput").ap()
        dbg_atr = nc.dram_tensor("dbg_atr", [T, D], BF16, kind="ExternalOutput").ap()
        dbg_hq = nc.dram_tensor("dbg_hq", [I3, T], BF16, kind="ExternalOutput").ap()
        dbg_sq = nc.dram_tensor("dbg_sq", [S, 1], F32, kind="ExternalOutput").ap()
        dbg_atp = nc.dram_tensor("dbg_atp", [S, D], BF16, kind="ExternalOutput").ap()

    dt = nc.dram_tensor
    ag_xnf_i = dt("ag_xnf_i", [D, T], BF16)
    ag_xnf_o = dt("ag_xnf_o", [4 * D, T], BF16)
    wqg_i = dt("wqg_i", [I3 // 8, D], BF16)
    wqu_i = dt("wqu_i", [I3 // 8, D], BF16)
    wqd_i = dt("wqd_i", [D // 8, I3], BF16)
    wqg_o = dt("wqg_o", [I3, D], BF16)
    wqu_o = dt("wqu_o", [I3, D], BF16)
    wqd_o = dt("wqd_o", [D, I3], BF16)
    hqT_d = dt("hqT_d", [I3, T], BF16)
    ffn_d = dt("ffn_d", [T, D], F32)
    sg_d = dt("sg_d", [S, 256], F32)
    sq_i = dt("sq_i", [S, 1], F32)
    sq_o = dt("sq_o", [S, 1], F32)
    atp_d = dt("atp_d", [S, D], BF16)
    atr_d = dt("atr_d", [T, D], BF16)
    kt_d = dt("kt_d", [S, 256], BF16)
    qt_d = dt("qt_d", [S, 256], BF16)
    vt_d = dt("vt_d", [S, 256], BF16)
    kf_d = dt("kf_d", [256, S], BF16)
    kw_d = dt("kw_d", [256, S], BF16)
    kr_d = dt("kr_d", [256, S], BF16)
    qf_d = dt("qf_d", [256, S], BF16)
    qm_d = dt("qm_d", [256, S], BF16)
    scal_d = dt("scal_d", [S, 16], F32)
    scal2_d = dt("scal2_d", [S, 32], F32)
    gct_d = dt("gct_d", [128, 8], F32)
    y_my = dt("y_my", [S, 256], F32)


    A = nc.alloc_sbuf_tensor
    xt = A("xt", [128, I3], F32)
    xnt = A("xnt", [128, D], F32)
    xnb = A("xnb", [128, D], BF16)
    stat = A("stat", [128, 8], F32)
    q_fa = A("q_fa", [128, 1024], F32)
    q_fb = A("q_fb", [128, 1024], F32)
    q_fc = A("q_fc", [128, 1024], F32)
    q_fd = A("q_fd", [128, 1024], F32)
    q_fs = A("q_fs", [128, 1024], F32)
    q_m8 = A("q_m8", [128, 1024], U8)
    q_s1 = A("q_s1", [128, 64], F32)
    q_s2 = A("q_s2", [128, 64], F32)
    q_s3 = A("q_s3", [128, 64], F32)
    qb = A("qb", [128, 1024], BF16)
    tp1 = A("tp1", [128, 128], BF16)
    tp2 = A("tp2", [128, 128], BF16)
    xqf_r = A("xqf_r", [128, KC * T], BF16)
    wgu = A("wgu", [128, KC * 2 * ISL], BF16)
    wdq = A("wdq", [128, NI128 * 1024], BF16)
    wbig = A("wbig", [128, KC * 1024], BF16)
    hqf_t = A("hqf", [128, 2 * 128], BF16)
    hqf = [hqf_t[:, 0:128], hqf_t[:, 128:256]]
    gu_s = A("gu_s", [128, 2 * ISL], F32)
    hfa = A("hfa", [128, 2 * ISL], F32)
    yrow_t = A("yrow", [128, 2 * D], F32)
    yrow = [yrow_t[:, 0:D], yrow_t[:, D:2 * D]]
    ps = [nc.alloc_psum_tensor(f"ps{i}", [128, 512], F32) for i in range(8)]
    if ATTN:
        wkv_bf = A("wkv_bf", [128, KC * 776], BF16)
        bab_bf = A("bab_bf", [1, 8], BF16)
        ktile = A("ktile", [128, 776], BF16)
        abt = A("abt", [128, 8], F32)
        sctile = A("sctile", [128, 16], F32)
        vart = A("vart", [128, 768], BF16)
        rawh = A("rawh", [128, 1024], F32)
        der = A("der", [128, 4 * 8 * 64], F32)
        tmp8 = A("tmp8", [128, 8 * 64], F32)
        rowm = A("rowm", [1, 256], F32)
        tri64 = A("tri64", [128, 64], F32)
        ones1 = A("ones1", [1, 128], F32)
        mlow = A("mlows", [128, 128], F32)
        mupp = A("mupps", [128, 128], F32)
        mupi = A("mupis", [128, 128], F32)
        onesb = A("onesb", [1, 128], BF16)
        trow = A("trow", [128, 10 * 128], BF16)
        stk = A("stk", [128, 2 * 6 * NCH], F32)
        kfs = A("kfs", [128, 2 * 5 * C], BF16)
        kts = A("kts", [128, 2 * 128], BF16)
        vts = A("vts", [128, 2 * C], BF16)
        Lb = A("Lb", [128, 128], BF16)
        LTb = A("LTb", [128, 128], BF16)
        MTb = A("MTb", [128, 128], BF16)
        Pw = A("Pw", [128, 2 * 128], BF16)
        PwT = A("PwT", [128, 2 * 128], BF16)
        Yw = A("Yw", [128, 2 * C], BF16)
        Usb = A("Usb", [128, C], BF16)
        S0 = A("S0", [128, 4 * C], BF16)
        evs = A("evs", [128, 128], F32)
        evs2 = A("evs2", [128, 128], F32)
        ytile = A("ytile", [128, 128], F32)
        n2sq = q_fa

        gnbt = gu_s
        sgt2 = q_fd

    x_t = x_in.rearrange("(n p) d -> n p d", p=128)
    xnf_stage = qb[:, 0:KC * 128]
    ffn_t = ffn_d.ap().rearrange("(n p) d -> n p d", p=128)
    sg_t = sg_d.ap().rearrange("(n p) d -> n p d", p=128)
    out_t = out_ext.rearrange("(n p) d -> n p d", p=128)

    def q_sign(s, src_ap, rows, cols):
        s.activation(q_fs[0:rows, 0:cols], src_ap, AF.Sign)

    retq = [None]

    def q_main_nosign(v, src_ap, out_ap, rows, cols):
        q_main(v, src_ap, out_ap, rows, cols)

    def q_main(v, src_ap, out_ap, rows, cols):
        nb = cols // 16
        a3 = src_ap.rearrange("p (g k) -> p g k", k=16)
        kb.flush(v, v.tensor_reduce(q_s1[0:rows, 0:nb], a3, AX.X, OP.max,
                                    apply_absolute_value=True))
        kb.flush(v, v.tensor_scalar(q_s2[0:rows, 0:nb], q_s1[0:rows, 0:nb],
                                    1.0 / 6.0, 1e-12, OP.mult, OP.max))
        kb.flush(v, v.reciprocal(q_s3[0:rows, 0:nb], q_s2[0:rows, 0:nb]))
        v.tensor_scalar(q_fa[0:rows, 0:cols].bitcast(U32), src_ap.bitcast(U32),
                        0x7FFFFFFF, None, OP.bitwise_and)
        v.tensor_tensor(q_fb[0:rows, 0:cols].rearrange("p (g k) -> p g k", k=16),
                        q_fa[0:rows, 0:cols].rearrange("p (g k) -> p g k", k=16),
                        q_s3[0:rows, 0:nb].unsqueeze(2).broadcast_to([rows, nb, 16]),
                        OP.mult)
        v.tensor_scalar(q_fa[0:rows, 0:cols].bitcast(U32),
                        q_fb[0:rows, 0:cols].bitcast(U32), 0x001FFFFF, None, OP.add)
        v.tensor_scalar(q_fc[0:rows, 0:cols].bitcast(U32),
                        q_fa[0:rows, 0:cols].bitcast(U32), 0xFFC00000, None,
                        OP.bitwise_and)
        v.tensor_scalar(q_fa[0:rows, 0:cols].bitcast(U32),
                        q_fb[0:rows, 0:cols].bitcast(U32), 0x003FFFFF, None, OP.add)
        v.tensor_scalar(q_fd[0:rows, 0:cols].bitcast(U32),
                        q_fa[0:rows, 0:cols].bitcast(U32), 0xFF800000, None,
                        OP.bitwise_and)
        v.tensor_scalar(q_fa[0:rows, 0:cols], q_fd[0:rows, 0:cols], 0.5, None, OP.max)
        v.tensor_scalar(q_m8[0:rows, 0:cols], q_fb[0:rows, 0:cols], 1.0, None, OP.is_lt)
        v.select(q_fd[0:rows, 0:cols], q_m8[0:rows, 0:cols],
                 q_fa[0:rows, 0:cols], q_fc[0:rows, 0:cols], add_drain=True)
        v.tensor_scalar(q_fa[0:rows, 0:cols], q_fb[0:rows, 0:cols], 0.25, None, OP.is_gt)
        v.tensor_tensor(q_fb[0:rows, 0:cols], q_fd[0:rows, 0:cols],
                        q_fa[0:rows, 0:cols], OP.mult)
        v.tensor_tensor(q_fa[0:rows, 0:cols], q_fb[0:rows, 0:cols],
                        q_fs[0:rows, 0:cols], OP.mult)
        retq[0] = v.tensor_tensor(
            out_ap.rearrange("p (g k) -> p g k", k=16),
            q_fa[0:rows, 0:cols].rearrange("p (g k) -> p g k", k=16),
            q_s2[0:rows, 0:nb].unsqueeze(2).broadcast_to([rows, nb, 16]),
            OP.mult)

    phase('p1')
    for it in range(NT):
        with nc.Block() as blk:
            @blk.sync
            def _(sync, it=it):
                kb.dma(sync, xt[:, 0:D], x_t[it])
                kb.wait(sync, "dml")
        with nc.Block() as blk:
            @blk.vector
            def _(v):
                kb.flush(v, v.tensor_reduce(stat[:, 0:1], xt[:, 0:D], AX.X, OP.add))
                kb.flush(v, v.tensor_scalar(stat[:, 0:1], stat[:, 0:1], 1.0 / D,
                                            None, OP.mult))
            @blk.scalar
            def _(s):
                s.activation(q_fa[:, 0:D], xt[:, 0:D], AF.Square)
        with nc.Block() as blk:
            @blk.vector
            def _(v):
                kb.flush(v, v.tensor_reduce(stat[:, 1:2], q_fa[:, 0:D], AX.X, OP.add))
                kb.flush(v, v.tensor_scalar(stat[:, 1:2], stat[:, 1:2], 1.0 / D,
                                            None, OP.mult))
                kb.flush(v, v.tensor_tensor(stat[:, 2:3], stat[:, 0:1], stat[:, 0:1],
                                            OP.mult))
                kb.flush(v, v.tensor_tensor(stat[:, 3:4], stat[:, 1:2], stat[:, 2:3],
                                            OP.subtract))
                kb.flush(v, v.tensor_scalar(stat[:, 3:4], stat[:, 3:4], 1e-5,
                                            None, OP.add))
        with nc.Block() as blk:
            @blk.scalar
            def _(s):
                s.activation(stat[:, 4:5], stat[:, 3:4], AF.Sqrt)
        with nc.Block() as blk:
            @blk.vector
            def _(v):
                kb.flush(v, v.reciprocal(stat[:, 5:6], stat[:, 4:5]))
                kb.flush(v, v.tensor_tensor(stat[:, 6:7], stat[:, 0:1], stat[:, 5:6],
                                            OP.mult))
                kb.flush(v, v.tensor_scalar(stat[:, 6:7], stat[:, 6:7], -1.0,
                                            None, OP.mult))
        with nc.Block() as blk:
            @blk.vector
            def _(v):
                kb.flush(v, v.tensor_scalar(stat[:, 7:8], stat[:, 0:1], -1.0,
                                            None, OP.mult))
                v.tensor_scalar(xnt[:], xt[:, 0:D], stat[:, 7:8], stat[:, 5:6],
                                OP.add, OP.mult)
        with nc.Block() as blk:
            @blk.vector
            def _(v):
                v.tensor_copy(xnb[:], xnt[:])
            @blk.scalar
            def _(s):
                q_sign(s, xnt[:], 128, D)
        with nc.Block() as blk:
            @blk.vector
            def _(v):
                q_main(v, xnt[:], qb[:, 0:D], 128, D)
        with nc.Block() as blk:
            @blk.sync
            def _(sync, it=it):
                for kc in range(KC):
                    kb.dma(sync, tp1[:], xnb[:, kc * 128:(kc + 1) * 128],
                           name="dtr", transpose=True)
                    kb.wait(sync, "dtr")
                    kb.dma(sync, ag_xnf_i.ap()[kc * 128:(kc + 1) * 128,
                                               it * 128:(it + 1) * 128], tp1[:])
                    kb.wait(sync, "dml")
                    kb.dma(sync, xqf_r[:, kc * T + it * 128:kc * T + (it + 1) * 128],
                           qb[:, kc * 128:(kc + 1) * 128], name="dtr", transpose=True)
                    kb.wait(sync, "dtr")

    phase('wq')
    with nc.Block() as blk:
        @blk.gpsimd
        def _(g):
            cs, _ = kb.inc("coll", 1)
            g.collective_compute("AllGather", OP.bypass,
                                 replica_groups=[[0, 1, 2, 3], [4, 5, 6, 7]],
                                 ins=[ag_xnf_i.ap().opt()],
                                 outs=[ag_xnf_o.ap().opt()]).then_inc(cs, 1)

    for (win, wout, rows, cols) in ((wgate_in, wqg_i, I3 // 8, D),
                                    (wup_in, wqu_i, I3 // 8, D),
                                    (wdown_in, wqd_i, D // 8, I3)):
        nrt = (rows + 127) // 128
        ncs = (cols + 1023) // 1024
        for rt in range(nrt):
            r0 = rt * 128
            pr = min(128, rows - r0)
            with nc.Block() as blk:
                @blk.sync
                def _(sync, win=win, r0=r0, pr=pr, cols=cols):
                    kb.dma(sync, xt[0:pr, 0:cols], win[r0:r0 + pr, :])
                    kb.wait(sync, "dml")
            for csl in range(ncs):
                c0 = csl * 1024
                pc = min(1024, cols - c0)
                with nc.Block() as blk:
                    @blk.scalar
                    def _(s, r0=r0, pr=pr, c0=c0, pc=pc):
                        q_sign(s, xt[0:pr, c0:c0 + pc], pr, pc)
                with nc.Block() as blk:
                    @blk.vector
                    def _(v, r0=r0, pr=pr, c0=c0, pc=pc):
                        q_main(v, xt[0:pr, c0:c0 + pc], qb[0:pr, 0:pc], pr, pc)
                with nc.Block() as blk:
                    @blk.sync
                    def _(sync, wout=wout, r0=r0, pr=pr, c0=c0, pc=pc):
                        kb.dma(sync, wout.ap()[r0:r0 + pr, c0:c0 + pc], qb[0:pr, 0:pc])
                        kb.wait(sync, "dml")
    with nc.Block() as blk:
        @blk.gpsimd
        def _(g):
            for (i_, o_) in ((wqg_i, wqg_o), (wqu_i, wqu_o), (wqd_i, wqd_o)):
                cs, _ = kb.inc("coll", 1)
                g.collective_compute("AllGather", OP.bypass,
                                     replica_groups=[list(range(8))],
                                     ins=[i_.ap().opt()],
                                     outs=[o_.ap().opt()]).then_inc(cs, 1)
            g.wait_ge(kb.sems["coll"], kb.cnt["coll"])

    phase('f1')
    with nc.Block() as blk:
        @blk.gpsimd
        def _(g):
            g.wait_ge(kb.sems["coll"], kb.cnt["coll"])

    kb.sem("f1m"); kb.sem("f1a"); kb.sem("f1h"); kb.sem("f1g"); kb.sem("f1q")
    kb.sem("f1s")
    for isl in range(NIS):
        i0 = isl * ISL
        with nc.Block() as blk:
            @blk.sync
            def _(sync, i0=i0):
                for kc in range(KC):
                    for (srcw, off) in ((wqg_o, 0), (wqu_o, ISL)):
                        for j in range(3):
                            w0 = j * 128
                            wN = min(128, ISL - w0)
                            kb.dma(sync, wgu[:, kc * 2 * ISL + off + w0:
                                            kc * 2 * ISL + off + w0 + wN],
                                   srcw.ap()[i0 + w0:i0 + w0 + wN,
                                             kc * 128:(kc + 1) * 128],
                                   name="p4w", transpose=True)
                kb.wait(sync, "p4w")
        kb.sem("f1so"); kb.sem("f1t0"); kb.sem("f1t1")
        c0 = {k: kb.cnt[k] for k in ("f1m", "f1a", "f1h", "f1g", "f1q", "f1s",
                                     "f1so", "f1t0", "f1t1")}
        with nc.Block() as blk:
            @blk.tensor
            def _(te, isl=isl, c0=c0):
                for it in range(NT):
                    pi = it % 2
                    if it >= 2:
                        te.wait_ge(kb.sems["f1h"], c0["f1h"] + it - 1)
                    for kc in range(KC):
                        lhs = xqf_r[:, kc * T + it * 128:kc * T + (it + 1) * 128]
                        te.matmul(ps[pi][:, 0:ISL], lhs,
                                  wgu[:, kc * 2 * ISL:kc * 2 * ISL + ISL],
                                  start=(kc == 0), stop=(kc == KC - 1))
                        mm = te.matmul(ps[2 + pi][:, 0:ISL], lhs,
                                       wgu[:, kc * 2 * ISL + ISL:
                                           kc * 2 * ISL + 2 * ISL],
                                       start=(kc == 0), stop=(kc == KC - 1))
                    mm.then_inc(kb.sems["f1m"], 1)
            @blk.scalar
            def _(s, isl=isl, c0=c0):
                for it in range(NT):
                    pi = it % 2
                    s.wait_ge(kb.sems["f1m"], c0["f1m"] + it + 1)
                    if it >= 2:
                        s.wait_ge(kb.sems["f1h"], c0["f1h"] + it - 1)
                    s.activation(gu_s[:, pi * ISL:(pi + 1) * ISL], ps[pi][:, 0:ISL],
                                 AF.Silu).then_inc(kb.sems["f1a"], 1)
                    s.wait_ge(kb.sems["f1h"], c0["f1h"] + it + 1)
                    s.activation(q_fs[:, 0:ISL], hfa[:, pi * ISL:(pi + 1) * ISL],
                                 AF.Sign).then_inc(kb.sems["f1g"], 1)
            @blk.vector
            def _(v, isl=isl, c0=c0):
                for it in range(NT):
                    pi = it % 2
                    v.wait_ge(kb.sems["f1a"], c0["f1a"] + it + 1)
                    v.tensor_tensor(hfa[:, pi * ISL:(pi + 1) * ISL],
                                    gu_s[:, pi * ISL:(pi + 1) * ISL],
                                    ps[2 + pi][:, 0:ISL], OP.mult)\
                        .then_inc(kb.sems["f1h"], 1)
                    v.wait_ge(kb.sems["f1g"], c0["f1g"] + it + 1)
                    if it >= 2:
                        v.wait_ge(kb.sem("f1so"), c0["f1so"] + it - 1)
                    q_main_nosign(v, hfa[:, pi * ISL:(pi + 1) * ISL],
                                  qb[:, pi * 512:pi * 512 + ISL], 128, ISL)
                    retq[0].then_inc(kb.sems["f1q"], 1)
            @blk.sync
            def _(sync, isl=isl, c0=c0):
                for it in range(NT):
                    pi = it % 2
                    sync.wait_ge(kb.sems["f1q"], c0["f1q"] + it + 1)
                    if it >= 2:
                        sync.wait_ge(kb.sem("f1so"), c0["f1so"] + it - 1)
                    for c in range(3):
                        sync.dma_start(
                            ktile[:, pi * 384 + c * 128:pi * 384 + (c + 1) * 128],
                            qb[:, pi * 512 + c * 128:pi * 512 + (c + 1) * 128],
                            transpose=True)\
                            .then_inc(kb.sem(f"f1t{pi}"), 16)
            @blk.gpsimd
            def _(g, isl=isl, i0=i0, c0=c0):
                for it in range(NT):
                    pi = it % 2
                    g.wait_ge(kb.sem(f"f1t{pi}"), c0[f"f1t{pi}"] + (it // 2 + 1) * 48)
                    for c in range(3):
                        rr = 96 if c == 2 else 128
                        g.dma_start(hqT_d.ap()[i0 + c * 128:i0 + c * 128 + rr,
                                               it * 128:(it + 1) * 128],
                                    ktile[0:rr, pi * 384 + c * 128:
                                          pi * 384 + c * 128 + rr * 0 + 128])\
                            .then_inc(kb.sems["f1s"], 16)
                    g.wait_ge(kb.sems["f1s"], c0["f1s"] + (it + 1) * 48)
                    g.sem_inc(kb.sem("f1so"), 1)
        for k, d in (("f1m", NT), ("f1a", NT), ("f1h", NT), ("f1g", NT),
                     ("f1q", NT), ("f1s", NT * 48), ("f1so", NT),
                     ("f1t0", (NT // 2) * 48), ("f1t1", (NT // 2) * 48)):
            kb.cnt[k] = c0[k] + d

    phase('p5')
    with nc.Block() as blk:
        @blk.sync
        def _(sync):
            for kc2 in range(NI128):
                for j in range(KC):
                    kb.dma(sync, wdq[:, kc2 * 1024 + j * 128:kc2 * 1024 + (j + 1) * 128],
                           wqd_o.ap()[j * 128:(j + 1) * 128,
                                      kc2 * 128:(kc2 + 1) * 128],
                           name="p5w", transpose=True)
            kb.wait(sync, "p5w")

    kb.sem("p5mm"); kb.sem("p5ev"); kb.sem("p5st")
    with nc.Block() as blk:
        @blk.sync
        def _(sync):
            for it in range(NT):
                for kc2 in range(NI128):
                    if it * NI128 + kc2 >= 2:
                        kb.wait(sync, "p5mm", (it * NI128 + kc2 - 1))
                    sync.dma_start(hqf[kc2 % 2],
                                   hqT_d.ap()[kc2 * 128:(kc2 + 1) * 128,
                                              it * 128:(it + 1) * 128])\
                        .then_inc(kb.sem(f"p5l{kc2 % 2}"), 16)
        @blk.tensor
        def _(te):
            for it in range(NT):
                for kc2 in range(NI128):
                    i = it * NI128 + kc2
                    te.wait_ge(kb.sem(f"p5l{i % 2}"), 16 * (i // 2 + 1))
                    lhs = hqf[kc2 % 2]
                    te.matmul(ps[2][:, 0:512], lhs, wdq[:, kc2 * 1024:kc2 * 1024 + 512],
                              start=(kc2 == 0), stop=(kc2 == NI128 - 1))
                    te.matmul(ps[3][:, 0:512], lhs,
                              wdq[:, kc2 * 1024 + 512:(kc2 + 1) * 1024],
                              start=(kc2 == 0), stop=(kc2 == NI128 - 1))\
                        .then_inc(kb.sems["p5mm"], 1)
                te.wait_ge(kb.sems["p5ev"], it + 1)
        @blk.vector
        def _(v):
            for it in range(NT):
                v.wait_ge(kb.sems["p5mm"], (it + 1) * NI128)
                if it >= 2:
                    v.wait_ge(kb.sem(f"p5st{it % 2}"), 16 * (it // 2))
                v.tensor_copy(yrow[it % 2][:, 0:512], ps[2][:, 0:512])
                v.tensor_copy(yrow[it % 2][:, 512:1024], ps[3][:, 0:512])\
                    .then_inc(kb.sems["p5ev"], 1)
        @blk.scalar
        def _(s):
            for it in range(NT):
                s.wait_ge(kb.sems["p5ev"], it + 1)
                s.dma_start(ffn_t[it], yrow[it % 2][:])\
                    .then_inc(kb.sem(f"p5st{it % 2}"), 16)
            s.wait_ge(kb.sem("p5st0"), (NT // 2) * 16)
            s.wait_ge(kb.sem("p5st1"), (NT // 2) * 16)
    kb.cnt["p5mm"] = NT * NI128
    kb.cnt["p5ev"] = NT
    kb.cnt["p5st"] = NT * 16

    if ATTN:
        phase('a0')
        with nc.Block() as blk:
            @blk.sync
            def _(sync):
                kb.dma(sync, mlow[:], mlow_in)
                kb.dma(sync, mupp[:], mupp_in)
                kb.dma(sync, mupi[:], mupi_in)
                kb.wait(sync, "dml")
            @blk.vector
            def _(v):
                v.memset(onesb[:], 1.0)
                v.memset(S0[:], 0.0)
        for kc in range(KC):
            with nc.Block() as blk:
                @blk.sync
                def _(sync, kc=kc):
                    kb.dma(sync, xt[:, 0:776],
                           wkvqab_in.rearrange("(n p) d -> n p d", p=128)[kc])
                    kb.wait(sync, "dml")
            with nc.Block() as blk:
                @blk.vector
                def _(v, kc=kc):
                    v.tensor_copy(wkv_bf[:, kc * 776:(kc + 1) * 776], xt[:, 0:776])
        with nc.Block() as blk:
            @blk.sync
            def _(sync):
                kb.dma(sync, stat[0:1, 0:8], bab_in)
                kb.wait(sync, "dml")
        with nc.Block() as blk:
            @blk.vector
            def _(v):
                v.tensor_copy(bab_bf[:], stat[0:1, 0:8])

        phase('a1')
        kb.sem("a1l"); kb.sem("a1m"); kb.sem("a1e"); kb.sem("a1s"); kb.sem("a1g")
        with nc.Block() as blk:
            @blk.sync
            def _(sync):
                nld = 0
                for tt in range(NTF):
                    if tt >= 1:
                        sync.wait_ge(kb.sems["a1e"], tt)
                    for kc in range(KC):
                        sync.dma_start(
                            xnf_stage[:, kc * 128:(kc + 1) * 128],
                            ag_xnf_o.ap()[(tt // 16) * D + kc * 128:
                                          (tt // 16) * D + (kc + 1) * 128,
                                          (tt % 16) * 128:(tt % 16) * 128 + 128]
                        ).then_inc(kb.sems["a1l"], 16)
                        nld += 16
            @blk.scalar
            def _(s):
                for tt in range(NTF):
                    s.wait_ge(kb.sems["a1m"], tt + 1)
                    if tt >= 1:
                        s.wait_ge(kb.sems["a1e"], tt)
                    s.activation(abt[:, 0:8], ps[1][:, 256:264], AF.Sigmoid)\
                        .then_inc(kb.sems["a1g"], 1)
                    s.activation(n2sq[:, 0:256], ps[0][:, 256:512], AF.Square)
                    s.activation(n2sq[:, 256:512], ps[1][:, 0:256], AF.Square)\
                        .then_inc(kb.sems["a1g"], 1)
            @blk.tensor
            def _(te):
                for tt in range(NTF):
                    te.wait_ge(kb.sems["a1l"], (tt + 1) * KC * 16)
                    if tt >= 1:
                        te.wait_ge(kb.sems["a1e"], tt)
                    for kc in range(KC):
                        lhs = xnf_stage[:, kc * 128:(kc + 1) * 128]
                        te.matmul(ps[0][:, 0:512], lhs,
                                  wkv_bf[:, kc * 776:kc * 776 + 512],
                                  start=(kc == 0), stop=(kc == KC - 1))
                        te.matmul(ps[1][:, 0:264], lhs,
                                  wkv_bf[:, kc * 776 + 512:kc * 776 + 776],
                                  start=(kc == 0), stop=False)
                    te.matmul(ps[1][:, 256:264], onesb[:], bab_bf[:],
                              start=False, stop=True).then_inc(kb.sems["a1m"], 1)
            @blk.vector
            def _(v):
                for tt in range(NTF):
                    v.wait_ge(kb.sems["a1m"], tt + 1)
                    v.wait_ge(kb.sems["a1g"], tt * 2 + 1)
                    if tt >= 1:
                        v.wait_ge(kb.sem("a1so"), tt)
                    for h in range(4):
                        v.tensor_scalar(ktile[:, h * 64:(h + 1) * 64],
                                        ps[0][:, h * 64:(h + 1) * 64],
                                        abt[:, 4 + h:5 + h], None, OP.mult)
                    v.tensor_copy(ktile[:, 256:512], ps[0][:, 256:512])
                    v.tensor_copy(ktile[:, 512:768], ps[1][:, 0:256])
                    v.wait_ge(kb.sems["a1g"], tt * 2 + 2)
                    kb.flush(v, v.tensor_reduce(
                        sctile[:, 8:12],
                        n2sq[:, 0:256].rearrange("p (h e) -> p h e", e=64),
                        AX.X, OP.add))
                    kb.flush(v, v.tensor_reduce(
                        sctile[:, 12:16],
                        n2sq[:, 256:512].rearrange("p (h e) -> p h e", e=64),
                        AX.X, OP.add))
                    v.tensor_copy(sctile[:, 0:8], abt[:, 0:8])\
                        .then_inc(kb.sems["a1e"], 1)
            @blk.gpsimd
            def _(g):
                for tt in range(NTF):
                    g.wait_ge(kb.sems["a1e"], tt + 1)
                    r0 = tt * 128
                    g.dma_start(vt_d.ap()[r0:r0 + 128, :], ktile[:, 0:256])\
                        .then_inc(kb.sems["a1s"], 16)
                    g.dma_start(kt_d.ap()[r0:r0 + 128, :], ktile[:, 256:512])\
                        .then_inc(kb.sems["a1s"], 16)
                    g.dma_start(qt_d.ap()[r0:r0 + 128, :], ktile[:, 512:768])\
                        .then_inc(kb.sems["a1s"], 16)
                    g.dma_start(scal_d.ap()[r0:r0 + 128, :], sctile[:])\
                        .then_inc(kb.sems["a1s"], 16)
                    g.wait_ge(kb.sems["a1s"], (tt + 1) * 64)
                    g.sem_inc(kb.sem("a1so"), 1)
        kb.cnt["a1l"] = NTF * KC * 16
        kb.cnt["a1m"] = NTF
        kb.cnt["a1g"] = NTF * 2
        kb.cnt["a1e"] = NTF
        kb.cnt["a1s"] = NTF * 64
        kb.cnt["a1so"] = NTF

        phase('a2')
        with nc.Block() as blk:
            @blk.sync
            def _(sync):
                kb.dma(sync, tri64[:], tri_in)
                kb.wait(sync, "dml")
            @blk.vector
            def _(v):
                v.memset(ones1[:], 1.0)
        QW, QR, QM, QSR, QSO, QSS, QGC = range(7)

        def dsl(h, q):
            return der[:, (h * 8 + q) * 64:(h * 8 + q) * 64 + 64]

        def tsl(q):
            return tmp8[:, q * 64:(q + 1) * 64]
        TLA, TLB, TLK, TLQ, TLG, TLG0, TREF, TLGC = range(8)
        sc2 = scal_d.ap().rearrange("(cp cpar tok) col -> cpar tok cp col",
                                    cpar=2, tok=64)
        for h in range(4):
            with nc.Block() as blk:
                @blk.sync
                def _(sync, h=h):
                    for cpar in range(2):
                        kb.dma(sync, rawh[cpar * 64:(cpar + 1) * 64, :]
                               .rearrange("p (cp col) -> p cp col", col=16),
                               sc2[cpar], name="a2l")
                    kb.wait(sync, "a2l")
            with nc.Block() as blk:
                @blk.scalar
                def _(s, h=h):
                    rw = rawh[:].rearrange("p (cp col) -> p cp col", col=16)
                    s.activation(tsl(TLA).rearrange("p c -> p c 1" ) if False else
                                 tsl(TLA).rearrange("p (c o) -> p c o", o=1),
                                 rw[:, :, 0 + h:1 + h], AF.Ln)
                    s.activation(tsl(TLB).rearrange("p (c o) -> p c o", o=1),
                                 rw[:, :, 4 + h:5 + h], AF.Ln)
            with nc.Block() as blk:
                @blk.vector
                def _(v, h=h):
                    rw = rawh[:].rearrange("p (cp col) -> p cp col", col=16)
                    v.tensor_scalar(tsl(TLK).rearrange("p (c o) -> p c o", o=1),
                                    rw[:, :, 8 + h:9 + h], 1e-24, None, OP.max)
                    v.tensor_scalar(tsl(TLQ).rearrange("p (c o) -> p c o", o=1),
                                    rw[:, :, 12 + h:13 + h], 1e-24, None, OP.max)
            with nc.Block() as blk:
                @blk.scalar
                def _(s, h=h):
                    s.activation(tsl(TLK), tsl(TLK), AF.Ln)
                    s.activation(tsl(TLQ), tsl(TLQ), AF.Ln)
            with nc.Block() as blk:
                @blk.vector
                def _(v, h=h):
                    v.tensor_scalar(tsl(TLK), tsl(TLK), -0.5, None, OP.mult)
                    v.tensor_scalar(tsl(TLQ), tsl(TLQ), -0.5, None, OP.mult)
            with nc.Block() as blk:
                @blk.tensor
                def _(te, h=h):
                    for cpar in range(2):
                        te.matmul(ps[0][cpar * 64:(cpar + 1) * 64, 0:64],
                                  tri64[cpar * 64:(cpar + 1) * 64, :],
                                  tsl(TLA)[cpar * 64:(cpar + 1) * 64, :],
                                  start=True, stop=True)
            with nc.Block() as blk:
                @blk.vector
                def _(v, h=h):
                    v.tensor_copy(tsl(TLG), ps[0][:, 0:64])
                    v.tensor_tensor(tsl(TLG0), tsl(TLG), tsl(TLA), OP.subtract)
            with nc.Block() as blk:
                @blk.sync
                def _(sync, h=h):
                    kb.dma(sync, rowm[0:1, 0:64], tsl(TLG0)[32:33, :], name="a2r")
                    kb.dma(sync, rowm[0:1, 64:128], tsl(TLG0)[96:97, :], name="a2r")
                    kb.dma(sync, rowm[0:1, 128:192], tsl(TLG)[63:64, :], name="a2r")
                    kb.dma(sync, rowm[0:1, 192:256], tsl(TLG)[127:128, :], name="a2r")
                    kb.wait(sync, "a2r")
            with nc.Block() as blk:
                @blk.tensor
                def _(te, h=h):
                    for cpar in range(2):
                        te.matmul(ps[1][cpar * 64:(cpar + 1) * 64, 0:64],
                                  ones1[:, 0:64], rowm[0:1, cpar * 64:(cpar + 1) * 64],
                                  start=True, stop=True)
                        te.matmul(ps[2][cpar * 64:(cpar + 1) * 64, 0:64],
                                  ones1[:, 0:64],
                                  rowm[0:1, 128 + cpar * 64:192 + cpar * 64],
                                  start=True, stop=True)
            with nc.Block() as blk:
                @blk.vector
                def _(v, h=h):
                    v.tensor_copy(tsl(TREF), ps[1][:, 0:64])
                    v.tensor_copy(tsl(TLGC), ps[2][:, 0:64])
            with nc.Block() as blk:
                @blk.vector
                def _(v, h=h):
                    v.tensor_tensor(dsl(h, QW), tsl(TLK), tsl(TLG), OP.subtract)
                    v.tensor_tensor(dsl(h, QW), dsl(h, QW), tsl(TREF), OP.add)
                    v.tensor_tensor(dsl(h, QSR), tsl(TLG0), tsl(TLB), OP.add)
                    v.tensor_tensor(dsl(h, QSR), dsl(h, QSR), tsl(TLK), OP.add)
                    v.tensor_tensor(dsl(h, QR), dsl(h, QSR), tsl(TREF), OP.subtract)
                    v.tensor_tensor(dsl(h, QSO), tsl(TLG), tsl(TLQ), OP.add)
                    v.tensor_tensor(dsl(h, QM), dsl(h, QSO), tsl(TREF), OP.subtract)
                    v.tensor_tensor(dsl(h, QSS), tsl(TLGC), tsl(TLG), OP.subtract)
                    v.tensor_tensor(dsl(h, QSS), dsl(h, QSS), tsl(TLK), OP.add)
                    v.tensor_copy(dsl(h, QGC), tsl(TLGC))
                    for q in range(7):
                        v.tensor_scalar(dsl(h, q), dsl(h, q), CLIP, None, OP.min)
            with nc.Block() as blk:
                @blk.scalar
                def _(s, h=h):
                    for q in range(7):
                        s.activation(dsl(h, q), dsl(h, q), AF.Exp)
            with nc.Block() as blk:
                @blk.vector
                def _(v, h=h):
                    v.tensor_scalar(dsl(h, QSR), dsl(h, QSR), -1.0, None, OP.mult)
        with nc.Block() as blk:
            @blk.sync
            def _(sync):
                qlist = (QSR, QSO, QSS, QR, QW, QGC)
                for pr in range(2):
                    for qi, qq in enumerate(qlist):
                        for hh in range(2):
                            h = pr * 2 + hh
                            for cpar in range(2):
                                kb.dma(sync,
                                       stk[hh * 64:(hh + 1) * 64,
                                           pr * 768 + qi * 128 + cpar * 64:
                                           pr * 768 + qi * 128 + (cpar + 1) * 64],
                                       dsl(h, qq)[cpar * 64:(cpar + 1) * 64, :],
                                       name="a2k")
                kb.wait(sync, "a2k")

        def stk_col(pr, qi, c):
            return stk[:, pr * 768 + qi * 128 + (c % 2) * 64 + c // 2:
                       pr * 768 + qi * 128 + (c % 2) * 64 + c // 2 + 1]

        phase('a3')
        kb.sem("a3l"); kb.sem("a3v"); kb.sem("a3t"); kb.sem("a3s")
        trow2 = ktile
        with nc.Block() as blk:
            @blk.sync
            def _(sync):
                def loads(tt):
                    pi = tt % 2
                    sync.dma_start(qb[:, pi * 512:pi * 512 + 256],
                                   kt_d.ap()[tt * 128:(tt + 1) * 128, :])\
                        .then_inc(kb.sem(f"a3l{pi}"), 16)
                    sync.dma_start(qb[:, pi * 512 + 256:pi * 512 + 512],
                                   qt_d.ap()[tt * 128:(tt + 1) * 128, :])\
                        .then_inc(kb.sem(f"a3l{pi}"), 16)
                loads(0)
                for tt in range(NTF):
                    pi = tt % 2
                    if tt + 1 < NTF:
                        if tt >= 1:
                            sync.wait_ge(kb.sems["a3v"], tt)
                            sync.wait_ge(kb.sem(f"a3t{(tt - 1) % 2}"),
                                         160 * ((tt - 1) // 2 + 1))
                        loads(tt + 1)
                    sync.wait_ge(kb.sems["a3v"], tt + 1)
                    if tt >= 1:
                        sync.wait_ge(kb.sem("a3so"), tt)
                    vsrc = vart if pi == 0 else ktile
                    srcs = [qb[:, pi * 512:pi * 512 + 128],
                            qb[:, pi * 512 + 128:pi * 512 + 256],
                            vsrc[:, 0:128], vsrc[:, 128:256],
                            vsrc[:, 256:384], vsrc[:, 384:512],
                            qb[:, pi * 512 + 256:pi * 512 + 384],
                            qb[:, pi * 512 + 384:pi * 512 + 512],
                            vsrc[:, 512:640], vsrc[:, 640:768]]
                    for j in range(10):
                        sync.dma_start(trow[:, j * 128:(j + 1) * 128], srcs[j],
                                       transpose=True)\
                            .then_inc(kb.sem(f"a3t{pi}"), 16)
            @blk.vector
            def _(v):
                for tt in range(NTF):
                    pi = tt % 2
                    v.wait_ge(kb.sem(f"a3l{pi}"), (tt // 2 + 1) * 32)
                    if tt >= 2:
                        v.wait_ge(kb.sem(f"a3t{pi}"), (tt // 2) * 160)
                    vdst = vart if pi == 0 else ktile
                    for h in range(4):
                        v.tensor_scalar(vdst[:, h * 64:(h + 1) * 64],
                                        qb[:, pi * 512 + h * 64:
                                           pi * 512 + (h + 1) * 64],
                                        dsl(h, QW)[:, tt:tt + 1], None, OP.mult)
                        v.tensor_scalar(vdst[:, 256 + h * 64:256 + (h + 1) * 64],
                                        qb[:, pi * 512 + h * 64:
                                           pi * 512 + (h + 1) * 64],
                                        dsl(h, QR)[:, tt:tt + 1], None, OP.mult)
                        vv = v.tensor_scalar(vdst[:, 512 + h * 64:512 + (h + 1) * 64],
                                             qb[:, pi * 512 + 256 + h * 64:
                                                pi * 512 + 256 + (h + 1) * 64],
                                             dsl(h, QM)[:, tt:tt + 1], None, OP.mult)
                    vv.then_inc(kb.sems["a3v"], 1)
            @blk.gpsimd
            def _(g):
                dsts = [kf_d, kf_d, kw_d, kw_d, kr_d, kr_d, qf_d, qf_d, qm_d, qm_d]
                for tt in range(NTF):
                    g.wait_ge(kb.sem(f"a3t{tt % 2}"), (tt // 2 + 1) * 160)
                    for j in range(10):
                        pr = j % 2
                        g.dma_start(dsts[j].ap()[pr * 128:(pr + 1) * 128,
                                                 tt * 128:(tt + 1) * 128],
                                    trow[:, j * 128:(j + 1) * 128])\
                            .then_inc(kb.sems["a3s"], 16)
                    g.wait_ge(kb.sems["a3s"], (tt + 1) * 160)
                    g.sem_inc(kb.sem("a3so"), 1)
        kb.cnt["a3l"] = NTF * 32
        kb.cnt["a3v"] = NTF
        kb.cnt["a3t"] = NTF * 160
        kb.cnt["a3s"] = NTF * 160
        kb.cnt["a3so"] = NTF

    phase('a4')
    if ATTN:
        NB = 9 * 16
        PG = 15
        ED = 23
        NU = NCH * 2
        kb.sem("dl"); kb.sem("dm"); kb.sem("de"); kb.sem("dy")
        KFV, KWV, KRV, QFV, QMV = range(5)

        def kfsl(pi, var, hh=None):
            basec = (pi * 5 + var) * 64
            if hh is None:
                return kfs[:, basec:basec + 64]
            return kfs[hh * 64:(hh + 1) * 64, basec:basec + 64]

        with nc.Block() as blk:
            @blk.sync
            def _(sync):
                for u in range(NU):
                    c, p = u // 2, u % 2
                    pi = u % 2
                    if u >= 2:
                        sync.wait_ge(kb.sems["dm"], PG * (u - 1))
                        sync.wait_ge(kb.sems["de"], ED * (u - 1))
                    for var, srcd in ((KFV, kf_d), (KWV, kw_d), (KRV, kr_d),
                                      (QFV, qf_d), (QMV, qm_d)):
                        sync.dma_start(kfsl(pi, var),
                                       srcd.ap()[p * 128:(p + 1) * 128,
                                                 c * 64:(c + 1) * 64])\
                            .then_inc(kb.sem(f"dl{pi}"), 16)
                    for hh in range(2):
                        sync.dma_start(kts[hh * 64:(hh + 1) * 64,
                                           pi * 128:(pi + 1) * 128],
                                       kt_d.ap()[c * 64:(c + 1) * 64,
                                                 p * 128:(p + 1) * 128])\
                            .then_inc(kb.sem(f"dl{pi}"), 16)
                        sync.dma_start(vts[hh * 64:(hh + 1) * 64,
                                           pi * 64:(pi + 1) * 64],
                                       vt_d.ap()[c * 64:(c + 1) * 64,
                                                 (p * 2 + hh) * 64:
                                                 (p * 2 + hh + 1) * 64])\
                            .then_inc(kb.sem(f"dl{pi}"), 16)

            @blk.tensor
            def _(te):
                for u in range(NU):
                    c, p = u // 2, u % 2
                    pi = u % 2
                    cpp = c % 2
                    dm0 = PG * u
                    de0 = ED * u
                    te.wait_ge(kb.sem(f"dl{u % 2}"), NB * (u // 2 + 1))
                    if u >= 1:
                        te.wait_ge(kb.sems["de"], ED * u)
                    for hh in range(2):
                        r = slice(hh * 64, hh * 64 + 64)
                        te.matmul(ps[0][r, hh * 64:hh * 64 + 64],
                                  kfsl(pi, KFV, hh), kfsl(pi, KWV, hh),
                                  start=True, stop=True)
                    for hh in range(2):
                        r = slice(hh * 64, hh * 64 + 64)
                        te.matmul(ps[1][r, hh * 64:hh * 64 + 64],
                                  kfsl(pi, KFV, hh), kfsl(pi, KRV, hh),
                                  start=True, stop=True)
                    for hh in range(2):
                        r = slice(hh * 64, hh * 64 + 64)
                        te.matmul(ps[2][r, hh * 64:hh * 64 + 64],
                                  kfsl(pi, KFV, hh), kfsl(pi, QMV, hh),
                                  start=True, stop=True)\
                            .then_inc(kb.sems["dm"], 1) if hh == 1 else \
                            te.matmul(ps[2][r, hh * 64:hh * 64 + 64],
                                      kfsl(pi, KFV, hh), kfsl(pi, QMV, hh),
                                      start=True, stop=True)
                    for hh in range(2):
                        r = slice(hh * 64, hh * 64 + 64)
                        mm = te.matmul(ps[3][r, 0:64], kfsl(pi, KFV, hh),
                                       S0[r, (p * 2 + cpp) * 64:
                                          (p * 2 + cpp) * 64 + 64],
                                       start=True, stop=True)
                        if hh == 1:
                            mm.then_inc(kb.sems["dm"], 1)
                    te.wait_ge(kb.sems["de"], de0 + 4)
                    te.matmul(ps[6][:, 0:64], LTb[:], Yw[:, 0:64],
                              start=True, stop=True).then_inc(kb.sems["dm"], 1)
                    for j in range(1, 6):
                        sl_new = j % 2
                        if j == 1:
                            lp, lpt = Lb[:], LTb[:]
                        else:
                            lp = Pw[:, (1 - sl_new) * 128:(2 - sl_new) * 128]
                            lpt = PwT[:, (1 - sl_new) * 128:(2 - sl_new) * 128]
                        if j >= 2:
                            te.wait_ge(kb.sems["de"], de0 + 4 + 3 * (j - 1))
                        if j < 5:
                            te.matmul(ps[4][:, 0:128], lpt, lp, start=True, stop=True)
                            te.matmul(ps[5][:, 0:128], lp, lpt, start=True,
                                      stop=True).then_inc(kb.sems["dm"], 1)
                        else:
                            te.matmul(ps[5][:, 0:128], lp, lpt, start=True,
                                      stop=True).then_inc(kb.sems["dm"], 1)
                        if j < 5:
                            te.wait_ge(kb.sems["de"], de0 + 4 + 3 * j)
                        else:
                            te.wait_ge(kb.sems["de"], de0 + 4 + 3 * 4 + 2)
                        ysl = j % 2
                        te.matmul(ps[6][:, 0:64],
                                  PwT[:, sl_new * 128:(sl_new + 1) * 128],
                                  Yw[:, ysl * 64:(ysl + 1) * 64],
                                  start=True, stop=True).then_inc(kb.sems["dm"], 1)
                    te.wait_ge(kb.sems["de"], de0 + 19)
                    for hh in range(2):
                        r = slice(hh * 64, hh * 64 + 64)
                        te.matmul(ps[0][r, 0:64], kfsl(pi, QFV, hh),
                                  S0[r, (p * 2 + cpp) * 64:(p * 2 + cpp) * 64 + 64],
                                  start=True, stop=True)
                    te.matmul(ps[1][:, 0:64], MTb[:], Yw[:, 0:64],
                              start=True, stop=True).then_inc(kb.sems["dm"], 1)
                    te.wait_ge(kb.sems["de"], de0 + 20)
                    for hh in range(2):
                        r = slice(hh * 64, hh * 64 + 64)
                        mm = te.matmul(ps[2][r, 0:64],
                                       kts[r, pi * 128 + hh * 64:
                                           pi * 128 + hh * 64 + 64],
                                       Usb[r, :], start=True, stop=True)
                        if hh == 1:
                            mm.then_inc(kb.sems["dm"], 1)

            @blk.vector
            def _(v):
                for u in range(NU):
                    c, p = u // 2, u % 2
                    pi = u % 2
                    cpp = c % 2
                    dm0 = PG * u
                    de0 = ED * u
                    v.wait_ge(kb.sems["dm"], dm0 + 1)
                    v.tensor_tensor(evs[:], ps[0][:, 0:128], mlow[:], OP.mult)
                    v.tensor_scalar(Lb[:], evs[:], stk_col(p, 3, c), None, OP.mult)\
                        .then_inc(kb.sems["de"], 1)
                    v.tensor_tensor(evs2[:], ps[1][:, 0:128], mupp[:], OP.mult)
                    v.tensor_scalar(LTb[:], evs2[:], stk_col(p, 4, c), None, OP.mult)\
                        .then_inc(kb.sems["de"], 1)
                    v.tensor_tensor(evs[:], ps[2][:, 0:128], mupi[:], OP.mult)
                    v.tensor_scalar(MTb[:], evs[:], stk_col(p, 4, c), None, OP.mult)\
                        .then_inc(kb.sems["de"], 1)
                    v.wait_ge(kb.sems["dm"], dm0 + 2)
                    v.wait_ge(kb.sem(f"dl{u % 2}"), NB * (u // 2 + 1))
                    v.scalar_tensor_tensor(Yw[:, 0:64], ps[3][:, 0:64],
                                           stk_col(p, 0, c),
                                           vts[:, pi * 64:(pi + 1) * 64],
                                           OP.mult, OP.add)\
                        .then_inc(kb.sems["de"], 1)
                    v.wait_ge(kb.sems["dm"], dm0 + 3)
                    v.tensor_tensor(Yw[:, 64:128], Yw[:, 0:64], ps[6][:, 0:64],
                                    OP.subtract).then_inc(kb.sems["de"], 1)
                    for j in range(1, 6):
                        sl_new = j % 2
                        v.wait_ge(kb.sems["dm"], dm0 + 3 + 2 * (j - 1) + 1)
                        if j < 5:
                            v.tensor_copy(Pw[:, sl_new * 128:(sl_new + 1) * 128],
                                          ps[4][:, 0:128]).then_inc(kb.sems["de"], 1)
                        v.tensor_copy(PwT[:, sl_new * 128:(sl_new + 1) * 128],
                                      ps[5][:, 0:128]).then_inc(kb.sems["de"], 1)
                        v.wait_ge(kb.sems["dm"], dm0 + 3 + 2 * (j - 1) + 2)
                        ydst = (j + 1) % 2
                        v.tensor_tensor(Yw[:, ydst * 64:(ydst + 1) * 64],
                                        Yw[:, (j % 2) * 64:(j % 2) * 64 + 64],
                                        ps[6][:, 0:64], OP.add)\
                            .then_inc(kb.sems["de"], 1)
                    v.tensor_scalar(Usb[:], Yw[:, 0:64], stk_col(p, 2, c), None,
                                    OP.mult).then_inc(kb.sems["de"], 1)
                    v.wait_ge(kb.sems["dm"], dm0 + 14)
                    if u >= 2:
                        v.wait_ge(kb.sem("dyo"), u - 1)
                    v.tensor_scalar(evs[:, 0:64], ps[0][:, 0:64],
                                    stk_col(p, 1, c), None, OP.mult)\
                        .then_inc(kb.sems["de"], 1)
                    v.tensor_tensor(ytile[:, pi * 64:(pi + 1) * 64], evs[:, 0:64],
                                    ps[1][:, 0:64], OP.add)\
                        .then_inc(kb.sems["de"], 1)
                    v.wait_ge(kb.sems["dm"], dm0 + 15)
                    v.scalar_tensor_tensor(
                        S0[:, (p * 2 + 1 - cpp) * 64:(p * 2 + 1 - cpp) * 64 + 64],
                        S0[:, (p * 2 + cpp) * 64:(p * 2 + cpp) * 64 + 64],
                        stk_col(p, 5, c), ps[2][:, 0:64], OP.mult, OP.add)\
                        .then_inc(kb.sems["de"], 1)

            @blk.gpsimd
            def _(g):
                for u in range(NU):
                    c, p = u // 2, u % 2
                    pi = u % 2
                    g.wait_ge(kb.sems["de"], ED * u + 22)
                    for hh in range(2):
                        g.dma_start(y_my.ap()[c * 64:(c + 1) * 64,
                                              (p * 2 + hh) * 64:(p * 2 + hh + 1) * 64],
                                    ytile[hh * 64:(hh + 1) * 64,
                                          pi * 64:(pi + 1) * 64])\
                            .then_inc(kb.sems["dy"], 16)
                    g.wait_ge(kb.sems["dy"], 32 * (u + 1))
                    g.sem_inc(kb.sem("dyo"), 1)
        kb.cnt["dl"] = NB * NU
        kb.cnt["dm"] = PG * NU
        kb.cnt["de"] = ED * NU
        kb.cnt["dy"] = 32 * NU
        kb.cnt["dyo"] = NU

        if DEBUG:
            with nc.Block() as blk:
                @blk.sync
                def _(sync):
                    for ri, (dsrc, ddst) in enumerate(
                            ((kt_d, dbg_kt), (qt_d, dbg_qt), (vt_d, dbg_vt))):
                        for r in range(16):
                            kb.dma(sync, qb[:, 0:256],
                                   dsrc.ap()[r * 128:(r + 1) * 128, 0:256],
                                   name="dbg")
                            kb.wait(sync, "dbg")
                            kb.dma(sync, ddst[r * 128:(r + 1) * 128, 0:256],
                                   qb[:, 0:256], name="dbg")
                            kb.wait(sync, "dbg")
                    for r in range(16):
                        kb.dma(sync, qb[:, 0:256],
                               ag_xnf_o.ap()[0:128, r * 128:r * 128 + 256]
                               if r < 8 else
                               kf_d.ap()[0:128, (r - 8) * 256:(r - 8) * 256 + 256],
                               name="dbg")
                        kb.wait(sync, "dbg")
                        kb.dma(sync, dbg_kf[0:128, r * 256:r * 256 + 256],
                               qb[:, 0:256], name="dbg")
                        kb.wait(sync, "dbg")
                    kb.dma(sync, dbg_kw[0:128, 0:776], wkv_bf[:, 0:776], name="dbg")
                    kb.wait(sync, "dbg")
                    kb.dma(sync, dbg_kw[0:128, 1024:1024 + 1024],
                           ag_xnf_i.ap()[0:128, 0:1024], name="dbg")
                    kb.wait(sync, "dbg")
                    for r in range(NTF):
                        kb.dma(sync, xt[:, 0:256], y_my.ap()[r * 128:(r + 1) * 128, :],
                               name="dbg")
                        kb.wait(sync, "dbg")
                        kb.dma(sync, dbg_y[r * 128:(r + 1) * 128, :], xt[:, 0:256],
                               name="dbg")
                        kb.wait(sync, "dbg")
                    for r in range(NTF):
                        kb.dma(sync, xt[:, 0:16], scal_d.ap()[r * 128:(r + 1) * 128, :],
                               name="dbg")
                        kb.wait(sync, "dbg")
                        kb.dma(sync, dbg_sc[r * 128:(r + 1) * 128, :], xt[:, 0:16],
                               name="dbg")
                        kb.wait(sync, "dbg")
                    kb.dma(sync, dbg_stk[:], stk[:], name="dbg")
                    kb.wait(sync, "dbg")


        phase('b5')
        for kc in range(KC):
            with nc.Block() as blk:
                @blk.sync
                def _(sync, kc=kc):
                    kb.dma(sync, xt[:, 0:256],
                           wg_in.rearrange("(n p) d -> n p d", p=128)[kc])
                    kb.wait(sync, "dml")
            with nc.Block() as blk:
                @blk.vector
                def _(v, kc=kc):
                    v.tensor_copy(wbig[:, kc * 256:(kc + 1) * 256], xt[:, 0:256])
        sg_t2 = sg_d.ap().rearrange("(n p) d -> n p d", p=128)
        kb.sem("b5l"); kb.sem("b5m"); kb.sem("b5e"); kb.sem("b5s")
        with nc.Block() as blk:
            @blk.sync
            def _(sync):
                for tt in range(NTF):
                    if tt >= 2:
                        sync.wait_ge(kb.sems["b5m"], tt - 1)
                    for kc in range(KC):
                        sync.dma_start(
                            xqf_r[:, (tt % 2) * 1024 + kc * 128:
                                  (tt % 2) * 1024 + (kc + 1) * 128],
                            ag_xnf_o.ap()[(tt // 16) * D + kc * 128:
                                          (tt // 16) * D + (kc + 1) * 128,
                                          (tt % 16) * 128:(tt % 16) * 128 + 128])\
                            .then_inc(kb.sem(f"b5l{tt % 2}"), 16)
            @blk.tensor
            def _(te):
                for tt in range(NTF):
                    te.wait_ge(kb.sem(f"b5l{tt % 2}"), (tt // 2 + 1) * KC * 16)
                    if tt >= 2:
                        te.wait_ge(kb.sems["b5e"], tt - 1)
                    for kc in range(KC):
                        mm = te.matmul(ps[tt % 2][:, 0:256],
                                       xqf_r[:, (tt % 2) * 1024 + kc * 128:
                                             (tt % 2) * 1024 + (kc + 1) * 128],
                                       wbig[:, kc * 256:(kc + 1) * 256],
                                       start=(kc == 0), stop=(kc == KC - 1))
                    mm.then_inc(kb.sems["b5m"], 1)
            @blk.scalar
            def _(s):
                def ysq_sq(u):
                    su = u % 4
                    s.wait_ge(kb.sem(f"b6q{su}"), (u // 4 + 1) * 16)
                    if u >= 4:
                        s.wait_ge(kb.sem("b6e2"), u - 3)
                    s.activation(q_fa[:, su * 256:(su + 1) * 256],
                                 q_fb[:, su * 256:(su + 1) * 256],
                                 AF.Square).then_inc(kb.sem("b6a2"), 1)

                def ysq_st(u):
                    su = u % 4
                    s.wait_ge(kb.sem("b6e2"), u + 1)
                    s.dma_start(sq_i.ap()[u * 128:(u + 1) * 128, :],
                                stat[:, su:su + 1])\
                        .then_inc(kb.sem(f"b6st{su}"), 16)
                for tt in range(NTF):
                    s.wait_ge(kb.sems["b5m"], tt + 1)
                    if tt >= 2:
                        s.wait_ge(kb.sem("b5so"), tt - 1)
                    s.activation(xnt[:, (tt % 2) * 256:(tt % 2) * 256 + 256],
                                 ps[tt % 2][:, 0:256], AF.Silu)\
                        .then_inc(kb.sems["b5e"], 1)
                    s.dma_start(q_fb[:, (tt % 4) * 256:(tt % 4 + 1) * 256],
                                y_my.ap()[tt * 128:(tt + 1) * 128, :])\
                        .then_inc(kb.sem(f"b6q{tt % 4}"), 16)
                    if tt >= 2:
                        ysq_sq(tt - 2)
                    if tt >= 5:
                        ysq_st(tt - 5)
                for u in (NTF - 2, NTF - 1):
                    ysq_sq(u)
                for u in range(NTF - 5, NTF):
                    ysq_st(u)
                for sl in range(4):
                    s.wait_ge(kb.sem(f"b6st{sl}"), (NTF // 4) * 16)
            @blk.gpsimd
            def _(g):
                for tt in range(NTF):
                    g.wait_ge(kb.sems["b5e"], tt + 1)
                    g.dma_start(sg_t2[tt], xnt[:, (tt % 2) * 256:(tt % 2) * 256 + 256])\
                        .then_inc(kb.sems["b5s"], 16)
                    g.wait_ge(kb.sems["b5s"], (tt + 1) * 16)
                    g.sem_inc(kb.sem("b5so"), 1)
            @blk.vector
            def _(v):
                for u in range(NTF):
                    su = u % 4
                    v.wait_ge(kb.sem("b6a2"), u + 1)
                    if u >= 4:
                        v.wait_ge(kb.sem(f"b6st{su}"), (u // 4) * 16)
                    v.tensor_reduce(stat[:, su:su + 1],
                                    q_fa[:, su * 256:(su + 1) * 256],
                                    AX.X, OP.add)\
                        .then_inc(kb.sem("b6e2"), 1)
        kb.cnt["b5l"] = NTF * KC * 16
        kb.cnt["b5m"] = NTF
        kb.cnt["b5e"] = NTF
        kb.cnt["b5s"] = NTF * 16
        kb.cnt["b5so"] = NTF
        phase('b6')
        with nc.Block() as blk:
            @blk.gpsimd
            def _(g):
                cs, _ = kb.inc("coll", 1)
                g.collective_compute("AllReduce", OP.add,
                                     replica_groups=[[0, 1, 2, 3], [4, 5, 6, 7]],
                                     ins=[sq_i.ap().opt()],
                                     outs=[sq_o.ap().opt()]).then_inc(cs, 1)
                g.wait_ge(kb.sems["coll"], kb.cnt["coll"])
        with nc.Block() as blk:
            @blk.sync
            def _(sync):
                kb.dma(sync, q_fb[0:1, 0:256], gn_in)
                kb.wait(sync, "dml")
        with nc.Block() as blk:
            @blk.tensor
            def _(te):
                te.matmul(ps[0][:, 0:256], ones1[:], q_fb[0:1, 0:256],
                          start=True, stop=True)
        with nc.Block() as blk:
            @blk.vector
            def _(v):
                v.tensor_copy(gnbt[:, 0:256], ps[0][:, 0:256])
        for kc2 in range(2):
            with nc.Block() as blk:
                @blk.sync
                def _(sync, kc2=kc2):
                    kb.dma(sync, xt[:, 0:D],
                           wo_in.rearrange("(n p) d -> n p d", p=128)[kc2])
                    kb.wait(sync, "dml")
            with nc.Block() as blk:
                @blk.vector
                def _(v, kc2=kc2):
                    v.tensor_copy(wbig[:, 4096 + kc2 * 1024:4096 + (kc2 + 1) * 1024],
                                  xt[:, 0:D])
        phase('rstd')
        phase('b7')
        atp_t = atp_d.ap().rearrange("(n p) d -> n p d", p=128)
        kb.sem("b7l"); kb.sem("b7z"); kb.sem("b7t"); kb.sem("b7m"); kb.sem("b7e")
        kb.sem("b7s")
        with nc.Block() as blk:
            @blk.sync
            def _(sync):
                def loads7(tt):
                    pi = tt % 2
                    sync.dma_start(yrow[pi][:, 0:256],
                                   y_my.ap()[tt * 128:(tt + 1) * 128, :])\
                        .then_inc(kb.sem(f"b7l{pi}"), 16)
                    sync.dma_start(q_fd[:, pi * 256:pi * 256 + 256], sg_t2[tt])\
                        .then_inc(kb.sem(f"b7l{pi}"), 16)
                    sync.dma_start(stat[:, 4 + pi:5 + pi],
                                   sq_o.ap()[tt * 128:(tt + 1) * 128, :])\
                        .then_inc(kb.sem(f"b7l{pi}"), 16)
                loads7(0)
                for tt in range(NTF):
                    pi = tt % 2
                    if tt + 1 < NTF:
                        if tt >= 1:
                            sync.wait_ge(kb.sems["b7z"], tt)
                        loads7(tt + 1)
                    sync.wait_ge(kb.sems["b7z"], tt + 1)
                    if tt >= 2:
                        sync.wait_ge(kb.sems["b7m"], tt - 1)
                    for j in range(2):
                        sync.dma_start(xqf_r[:, 2048 + pi * 256 + j * 128:
                                             2048 + pi * 256 + (j + 1) * 128],
                                       xnb[:, pi * 256 + j * 128:
                                           pi * 256 + (j + 1) * 128],
                                       transpose=True)\
                            .then_inc(kb.sem(f"b7t{pi}"), 16)
            @blk.vector
            def _(v):
                def zstep(tt):
                    pi = tt % 2
                    v.wait_ge(kb.sem(f"b7l{pi}"), (tt // 2 + 1) * 48)
                    if tt >= 2:
                        v.wait_ge(kb.sem(f"b7t{pi}"), (tt // 2) * 32)
                    v.tensor_scalar(stat[:, 6 + pi:7 + pi],
                                    stat[:, 4 + pi:5 + pi],
                                    1.0 / D, 1e-5, OP.mult, OP.add)\
                        .then_inc(kb.sem("b7v"), 1)
                    kb.flush(v, v.tensor_tensor(q_fa[:, pi * 256:pi * 256 + 256],
                                                q_fd[:, pi * 256:pi * 256 + 256],
                                                gnbt[:, 0:256], OP.mult))
                    v.wait_ge(kb.sem("b7a"), tt + 1)
                    kb.flush(v, v.reciprocal(stat[:, 6 + pi:7 + pi],
                                             stat[:, 6 + pi:7 + pi]))
                    kb.flush(v, v.tensor_scalar(q_fb[:, pi * 256:pi * 256 + 256],
                                                yrow[pi][:, 0:256],
                                                stat[:, 6 + pi:7 + pi],
                                                None, OP.mult))
                    v.tensor_tensor(xnb[:, pi * 256:pi * 256 + 256],
                                    q_fb[:, pi * 256:pi * 256 + 256],
                                    q_fa[:, pi * 256:pi * 256 + 256], OP.mult)\
                        .then_inc(kb.sems["b7z"], 1)

                def estep(tt):
                    pi = tt % 2
                    v.wait_ge(kb.sems["b7m"], tt + 1)
                    if tt >= 2:
                        v.wait_ge(kb.sem("b7so"), tt - 1)
                    v.tensor_copy(q_fc[:, 0:1024].bitcast(BF16)
                                  [:, pi * 1024:pi * 1024 + 512],
                                  ps[4 + 2 * pi][:, 0:512])
                    v.tensor_copy(q_fc[:, 0:1024].bitcast(BF16)
                                  [:, pi * 1024 + 512:pi * 1024 + 1024],
                                  ps[5 + 2 * pi][:, 0:512])\
                        .then_inc(kb.sems["b7e"], 1)
                for tt in range(NTF):
                    zstep(tt)
                    if tt >= 1:
                        estep(tt - 1)
                estep(NTF - 1)
            @blk.tensor
            def _(te):
                for tt in range(NTF):
                    pi = tt % 2
                    te.wait_ge(kb.sem(f"b7t{pi}"), (tt // 2 + 1) * 32)
                    if tt >= 2:
                        te.wait_ge(kb.sems["b7e"], tt - 1)
                    for half in range(2):
                        for j in range(2):
                            mm = te.matmul(ps[4 + 2 * pi + half][:, 0:512],
                                           xqf_r[:, 2048 + pi * 256 + j * 128:
                                                 2048 + pi * 256 + (j + 1) * 128],
                                           wbig[:, 4096 + j * 1024 + half * 512:
                                                4096 + j * 1024 + (half + 1) * 512],
                                           start=(j == 0), stop=(j == 1))
                    mm.then_inc(kb.sems["b7m"], 1)
            @blk.scalar
            def _(s):
                for tt in range(NTF):
                    pi = tt % 2
                    s.wait_ge(kb.sem("b7v"), tt + 1)
                    s.activation(stat[:, 6 + pi:7 + pi], stat[:, 6 + pi:7 + pi],
                                 AF.Sqrt).then_inc(kb.sem("b7a"), 1)
            @blk.gpsimd
            def _(g):
                for tt in range(NTF):
                    pi = tt % 2
                    g.wait_ge(kb.sems["b7e"], tt + 1)
                    g.dma_start(atp_t[tt],
                                q_fc[:, 0:1024].bitcast(BF16)
                                [:, pi * 1024:pi * 1024 + 1024])\
                        .then_inc(kb.sems["b7s"], 16)
                    g.wait_ge(kb.sems["b7s"], (tt + 1) * 16)
                    g.sem_inc(kb.sem("b7so"), 1)
        kb.cnt["b7l"] = NTF * 48
        kb.cnt["b7z"] = NTF
        kb.cnt["b7t"] = NTF * 32
        kb.cnt["b7m"] = NTF
        kb.cnt["b7e"] = NTF
        kb.cnt["b7s"] = NTF * 16
        kb.cnt["b7so"] = NTF
        kb.cnt["b7v"] = NTF
        kb.cnt["b7a"] = NTF
        phase('rs')
        with nc.Block() as blk:
            @blk.gpsimd
            def _(g):
                cs, _ = kb.inc("coll", 1)
                g.collective_compute("ReduceScatter", OP.add,
                                     replica_groups=[[0, 1, 2, 3], [4, 5, 6, 7]],
                                     ins=[atp_d.ap().opt()],
                                     outs=[atr_d.ap().opt()]).then_inc(cs, 1)
                g.wait_ge(kb.sems["coll"], kb.cnt["coll"])
        atr_t = atr_d.ap().rearrange("(n p) d -> n p d", p=128)
        phase('fin')
        for it in range(NT):
            with nc.Block() as blk:
                @blk.sync
                def _(sync, it=it):
                    kb.dma(sync, xt[:, 0:D], x_t[it], name="a5x")
                    kb.dma(sync, yrow[1][:], ffn_t[it], name="a5x")
                    kb.dma(sync, xnb[:, 0:D], atr_t[it], name="a5x")
                    kb.wait(sync, "a5x")
            with nc.Block() as blk:
                @blk.vector
                def _(v):
                    v.tensor_copy(q_fa[:, 0:D], xnb[:, 0:D])
                    v.tensor_tensor(xnt[:], xt[:, 0:D], yrow[1][:], OP.add)
                    v.tensor_tensor(xnt[:], xnt[:], q_fa[:, 0:D], OP.add)
            with nc.Block() as blk:
                @blk.sync
                def _(sync, it=it):
                    kb.dma(sync, out_t[it], xnt[:], name="dst")
                    kb.wait(sync, "dst")

    else:
        for it in range(NT):
            with nc.Block() as blk:
                @blk.sync
                def _(sync, it=it):
                    kb.dma(sync, xt[:, 0:D], x_t[it])
                    kb.dma(sync, yrow[0][:], ffn_t[it])
                    kb.wait(sync, "dml")
            with nc.Block() as blk:
                @blk.vector
                def _(v):
                    v.tensor_tensor(xnt[:], xt[:, 0:D], yrow[0][:], OP.add)
            with nc.Block() as blk:
                @blk.sync
                def _(sync, it=it):
                    kb.dma(sync, out_t[it], xnt[:], name="dst")
                    kb.wait(sync, "dst")

    phase(None)
    if DEBUG:
        with nc.Block() as blk:
            @blk.sync
            def _(sync):
                kb.dma(sync, dbg_ffn, ffn_d.ap(), name="dbg2")
                kb.dma(sync, dbg_sg, sg_d.ap(), name="dbg2")
                kb.dma(sync, dbg_atr, atr_d.ap(), name="dbg2")
                kb.dma(sync, dbg_hq, hqT_d.ap(), name="dbg2")
                kb.dma(sync, dbg_sq, sq_o.ap(), name="dbg2")
                kb.dma(sync, dbg_atp, atp_d.ap(), name="dbg2")
                kb.wait(sync, "dbg2")

    return kb


def make_consts():
    i = np.arange(128)
    hhp = i // 64
    tp = i % 64
    low = ((hhp[:, None] == hhp[None, :]) & (tp[None, :] < tp[:, None])).astype(np.float32)
    upp = ((hhp[:, None] == hhp[None, :]) & (tp[None, :] > tp[:, None])).astype(np.float32)
    upi = ((hhp[:, None] == hhp[None, :]) & (tp[None, :] >= tp[:, None])).astype(np.float32)
    return low, upp, upi


def kernel(**inputs):
    x = np.ascontiguousarray(np.asarray(inputs["L0_attn_x"], np.float32))
    Wq = np.asarray(inputs["L0_attn_Wq"], np.float32)
    Wk = np.asarray(inputs["L0_attn_Wk"], np.float32)
    Wv = np.asarray(inputs["L0_attn_Wv"], np.float32)
    Wa = np.asarray(inputs["L0_attn_Wa"], np.float32)
    ba = np.asarray(inputs["L0_attn_ba"], np.float32)
    Wb = np.asarray(inputs["L0_attn_Wb"], np.float32)
    bb = np.asarray(inputs["L0_attn_bb"], np.float32)
    Wg = np.ascontiguousarray(np.asarray(inputs["L0_attn_Wg"], np.float32))
    Wo = np.ascontiguousarray(np.asarray(inputs["L0_attn_Wo"], np.float32))
    gn = np.asarray(inputs["L0_attn_g_norm"], np.float32)
    wgate = np.asarray(inputs["L0_ffn_w_gate"], np.float32)
    wup = np.asarray(inputs["L0_ffn_w_up"], np.float32)
    wdown = np.asarray(inputs["L0_ffn_w_down"], np.float32)
    low, upp, upi = make_consts()
    trj = np.tril(np.ones((64, 64), np.float32)).T.copy()
    tri2 = np.ascontiguousarray(np.concatenate([trj, trj], 0))

    kb = build()
    in_maps = []
    for c in range(8):
        b, qq = c // 4, c % 4
        hc = slice(qq * 256, (qq + 1) * 256)
        wkvqab = np.concatenate([Wv[:, hc], Wk[:, hc], Wq[:, hc],
                                 Wa[:, 4 * qq:4 * qq + 4], Wb[:, 4 * qq:4 * qq + 4]], 1)
        bab = np.concatenate([ba[4 * qq:4 * qq + 4], bb[4 * qq:4 * qq + 4]])[None, :]
        in_maps.append({
            "x": np.ascontiguousarray(x[b, qq * T:(qq + 1) * T]),
            "wkvqab": np.ascontiguousarray(wkvqab),
            "bab": np.ascontiguousarray(bab),
            "wgm": np.ascontiguousarray(Wg[:, hc]),
            "wom": np.ascontiguousarray(Wo[qq * 256:(qq + 1) * 256, :]),
            "gnm": np.ascontiguousarray(gn[None, qq * 256:(qq + 1) * 256]),
            "wgate": np.ascontiguousarray(wgate[c * (I3 // 8):(c + 1) * (I3 // 8)]),
            "wup": np.ascontiguousarray(wup[c * (I3 // 8):(c + 1) * (I3 // 8)]),
            "wdown": np.ascontiguousarray(wdown[c * (D // 8):(c + 1) * (D // 8)]),
            "mlow": low, "mupp": upp, "mupi": upi, "tri": tri2,
        })
    res = run_bass_kernel_spmd(kb.nc, in_maps, core_ids=list(range(8)))
    out = np.zeros((2, S, D), np.float32)
    for c in range(8):
        b, qq = c // 4, c % 4
        out[b, qq * T:(qq + 1) * T] = res.results[c]["out"]
    return out




# version 12
# speedup vs baseline: 1.1509x; 1.1509x over previous
import numpy as np
import concourse.bass as bass
import concourse.mybir as mybir
from concourse.bass_utils import run_bass_kernel_spmd

F32 = mybir.dt.float32
BF16 = mybir.dt.bfloat16
U8 = mybir.dt.uint8
U32 = mybir.dt.uint32
AF = mybir.ActivationFunctionType
OP = mybir.AluOpType
AX = mybir.AxisListType

S, D = 8192, 1024
T = 2048
NT = T // 128
NTF = S // 128
KC = D // 128
I3 = 2816
ISL = 352
NIS = I3 // ISL
NI128 = I3 // 128
DH = 64
C = 64
NCH = S // C
CLIP = 60.0

ATTN = True
DEBUG = False


class KB:
    def __init__(self):
        self.nc = bass.Bass("TRN2", target_bir_lowering=False)
        self.sems = {}
        self.cnt = {}

    def sem(self, name):
        if name not in self.sems:
            self.sems[name] = self.nc.alloc_semaphore(name)
            self.cnt[name] = 0
        return self.sems[name]

    def val(self, name):
        return self.cnt.get(name, 0)

    def inc(self, name, by=16):
        self.sem(name)
        self.cnt[name] += by
        return self.sems[name], self.cnt[name]

    def dma(self, eng, out, in_, name="dml", transpose=False):
        s, _ = self.inc(name)
        eng.dma_start(out, in_, transpose=transpose).then_inc(s, 16)

    def wait(self, eng, name, val=None):
        if name in self.sems:
            eng.wait_ge(self.sems[name], self.cnt[name] if val is None else val)

    def flush(self, eng, inst, name="esync"):
        s, n = self.inc(name, 1)
        inst.then_inc(s, 1)
        eng.wait_ge(s, n)


def build():
    kb = KB()
    nc = kb.nc
    _scope = [None, 0]

    def phase(name):
        if _scope[0] is not None:
            nc.leave_named_scope(_scope[0], _scope[1], False)
            _scope[0] = None
        if name is not None:
            _scope[1], _ = nc.enter_named_scope(name, False)
            _scope[0] = name

    x_in = nc.dram_tensor("x", [T, D], F32, kind="ExternalInput").ap()
    wkvqab_in = nc.dram_tensor("wkvqab", [D, 776], F32, kind="ExternalInput").ap()
    bab_in = nc.dram_tensor("bab", [1, 8], F32, kind="ExternalInput").ap()
    wg_in = nc.dram_tensor("wgm", [D, 256], F32, kind="ExternalInput").ap()
    wo_in = nc.dram_tensor("wom", [256, D], F32, kind="ExternalInput").ap()
    gn_in = nc.dram_tensor("gnm", [1, 256], F32, kind="ExternalInput").ap()
    wgate_in = nc.dram_tensor("wgate", [I3 // 8, D], F32, kind="ExternalInput").ap()
    wup_in = nc.dram_tensor("wup", [I3 // 8, D], F32, kind="ExternalInput").ap()
    wdown_in = nc.dram_tensor("wdown", [D // 8, I3], F32, kind="ExternalInput").ap()
    mlow_in = nc.dram_tensor("mlow", [128, 128], F32, kind="ExternalInput").ap()
    tri_in = nc.dram_tensor("tri", [128, 64], F32, kind="ExternalInput").ap()
    mupp_in = nc.dram_tensor("mupp", [128, 128], F32, kind="ExternalInput").ap()
    mupi_in = nc.dram_tensor("mupi", [128, 128], F32, kind="ExternalInput").ap()
    out_ext = nc.dram_tensor("out", [T, D], F32, kind="ExternalOutput").ap()
    if DEBUG:
        dbg_kt = nc.dram_tensor("dbg_kt", [S, 256], BF16, kind="ExternalOutput").ap()
        dbg_qt = nc.dram_tensor("dbg_qt", [S, 256], BF16, kind="ExternalOutput").ap()
        dbg_vt = nc.dram_tensor("dbg_vt", [S, 256], BF16, kind="ExternalOutput").ap()
        dbg_sc = nc.dram_tensor("dbg_sc", [S, 16], F32, kind="ExternalOutput").ap()
        dbg_stk = nc.dram_tensor("dbg_stk", [128, 2 * 6 * NCH], F32,
                                 kind="ExternalOutput").ap()
        dbg_y = nc.dram_tensor("dbg_y", [S, 256], F32, kind="ExternalOutput").ap()
        dbg_kf = nc.dram_tensor("dbg_kf", [256, S], BF16, kind="ExternalOutput").ap()
        dbg_kw = nc.dram_tensor("dbg_kw", [128, 4096], BF16, kind="ExternalOutput").ap()
        dbg_ffn = nc.dram_tensor("dbg_ffn", [T, D], F32, kind="ExternalOutput").ap()
        dbg_sg = nc.dram_tensor("dbg_sg", [S, 256], F32, kind="ExternalOutput").ap()
        dbg_rstd = nc.dram_tensor("dbg_rstd", [S, 1], F32, kind="ExternalOutput").ap()
        dbg_atr = nc.dram_tensor("dbg_atr", [T, D], BF16, kind="ExternalOutput").ap()
        dbg_hq = nc.dram_tensor("dbg_hq", [I3, T], BF16, kind="ExternalOutput").ap()
        dbg_sq = nc.dram_tensor("dbg_sq", [S, 1], F32, kind="ExternalOutput").ap()
        dbg_atp = nc.dram_tensor("dbg_atp", [S, D], BF16, kind="ExternalOutput").ap()

    dt = nc.dram_tensor
    ag_xnf_i = dt("ag_xnf_i", [D, T], BF16)
    ag_xnf_o = dt("ag_xnf_o", [4 * D, T], BF16)
    wqg_i = dt("wqg_i", [I3 // 8, D], BF16)
    wqu_i = dt("wqu_i", [I3 // 8, D], BF16)
    wqd_i = dt("wqd_i", [D // 8, I3], BF16)
    wqg_o = dt("wqg_o", [I3, D], BF16)
    wqu_o = dt("wqu_o", [I3, D], BF16)
    wqd_o = dt("wqd_o", [D, I3], BF16)
    hqT_d = dt("hqT_d", [I3, T], BF16)
    ffn_d = dt("ffn_d", [T, D], F32)
    sg_d = dt("sg_d", [S, 256], F32)
    sq_i = dt("sq_i", [S, 1], F32)
    sq_o = dt("sq_o", [S, 1], F32)
    atp_d = dt("atp_d", [S, D], BF16)
    atr_d = dt("atr_d", [T, D], BF16)
    kt_d = dt("kt_d", [S, 256], BF16)
    qt_d = dt("qt_d", [S, 256], BF16)
    vt_d = dt("vt_d", [S, 256], BF16)
    kf_d = dt("kf_d", [256, S], BF16)
    kw_d = dt("kw_d", [256, S], BF16)
    kr_d = dt("kr_d", [256, S], BF16)
    qf_d = dt("qf_d", [256, S], BF16)
    qm_d = dt("qm_d", [256, S], BF16)
    scal_d = dt("scal_d", [S, 16], F32)
    scal2_d = dt("scal2_d", [S, 32], F32)
    gct_d = dt("gct_d", [128, 8], F32)
    y_my = dt("y_my", [S, 256], F32)


    A = nc.alloc_sbuf_tensor
    xt = A("xt", [128, I3], F32)
    xnt = A("xnt", [128, D], F32)
    xnb = A("xnb", [128, D], BF16)
    stat = A("stat", [128, 8], F32)
    q_fa = A("q_fa", [128, 1024], F32)
    q_fb = A("q_fb", [128, 1024], F32)
    q_fc = A("q_fc", [128, 1024], F32)
    q_fd = A("q_fd", [128, 1024], F32)
    q_fs = A("q_fs", [128, 1024], F32)
    q_m8 = A("q_m8", [128, 1024], U8)
    q_s1 = A("q_s1", [128, 64], F32)
    q_s2 = A("q_s2", [128, 64], F32)
    q_s3 = A("q_s3", [128, 64], F32)
    qb = A("qb", [128, 1024], BF16)
    tp1 = A("tp1", [128, 128], BF16)
    tp2 = A("tp2", [128, 128], BF16)
    xqf_r = A("xqf_r", [128, KC * T], BF16)
    wgu = A("wgu", [128, KC * 2 * ISL], BF16)
    wdq = A("wdq", [128, NI128 * 1024], BF16)
    wbig = A("wbig", [128, KC * 1024], BF16)
    hqf_t = A("hqf", [128, 2 * 128], BF16)
    hqf = [hqf_t[:, 0:128], hqf_t[:, 128:256]]
    gu_s = A("gu_s", [128, 2 * ISL], F32)
    hfa = A("hfa", [128, 2 * ISL], F32)
    yrow_t = A("yrow", [128, 2 * D], F32)
    yrow = [yrow_t[:, 0:D], yrow_t[:, D:2 * D]]
    ps = [nc.alloc_psum_tensor(f"ps{i}", [128, 512], F32) for i in range(8)]
    if ATTN:
        wkv_bf = A("wkv_bf", [128, KC * 776], BF16)
        bab_bf = A("bab_bf", [1, 8], BF16)
        ktile = A("ktile", [128, 776], BF16)
        abt = A("abt", [128, 8], F32)
        sctile = A("sctile", [128, 16], F32)
        vart = A("vart", [128, 768], BF16)
        rawh = A("rawh", [128, 1024], F32)
        der = A("der", [128, 4 * 8 * 64], F32)
        tmp8 = A("tmp8", [128, 8 * 64], F32)
        rowm = A("rowm", [1, 256], F32)
        tri64 = A("tri64", [128, 64], F32)
        ones1 = A("ones1", [1, 128], F32)
        mlow = A("mlows", [128, 128], F32)
        mupp = A("mupps", [128, 128], F32)
        mupi = A("mupis", [128, 128], F32)
        onesb = A("onesb", [1, 128], BF16)
        trow = A("trow", [128, 10 * 128], BF16)
        stk = A("stk", [128, 2 * 6 * NCH], F32)
        kfs = A("kfs", [128, 2 * 5 * C], BF16)
        kts = A("kts", [128, 2 * 128], BF16)
        vts = A("vts", [128, 2 * C], BF16)
        Lb = A("Lb", [128, 128], BF16)
        LTb = A("LTb", [128, 128], BF16)
        MTb = A("MTb", [128, 128], BF16)
        Pw = A("Pw", [128, 2 * 128], BF16)
        PwT = A("PwT", [128, 2 * 128], BF16)
        Yw = A("Yw", [128, 2 * C], BF16)
        Usb = A("Usb", [128, C], BF16)
        S0 = A("S0", [128, 4 * C], BF16)
        evs = A("evs", [128, 128], F32)
        evs2 = A("evs2", [128, 128], F32)
        ytile = A("ytile", [128, 128], F32)
        n2sq = q_fa

        gnbt = gu_s
        sgt2 = q_fd

    x_t = x_in.rearrange("(n p) d -> n p d", p=128)
    xnf_stage = qb[:, 0:KC * 128]
    ffn_t = ffn_d.ap().rearrange("(n p) d -> n p d", p=128)
    sg_t = sg_d.ap().rearrange("(n p) d -> n p d", p=128)
    out_t = out_ext.rearrange("(n p) d -> n p d", p=128)

    def q_sign(s, src_ap, rows, cols):
        s.activation(q_fs[0:rows, 0:cols], src_ap, AF.Sign)

    retq = [None]

    def q_main_nosign(v, src_ap, out_ap, rows, cols):
        q_main(v, src_ap, out_ap, rows, cols)

    def q_main(v, src_ap, out_ap, rows, cols):
        nb = cols // 16
        a3 = src_ap.rearrange("p (g k) -> p g k", k=16)
        kb.flush(v, v.tensor_reduce(q_s1[0:rows, 0:nb], a3, AX.X, OP.max,
                                    apply_absolute_value=True))
        kb.flush(v, v.tensor_scalar(q_s2[0:rows, 0:nb], q_s1[0:rows, 0:nb],
                                    1.0 / 6.0, 1e-12, OP.mult, OP.max))
        kb.flush(v, v.reciprocal(q_s3[0:rows, 0:nb], q_s2[0:rows, 0:nb]))
        v.tensor_scalar(q_fa[0:rows, 0:cols].bitcast(U32), src_ap.bitcast(U32),
                        0x7FFFFFFF, None, OP.bitwise_and)
        v.tensor_tensor(q_fb[0:rows, 0:cols].rearrange("p (g k) -> p g k", k=16),
                        q_fa[0:rows, 0:cols].rearrange("p (g k) -> p g k", k=16),
                        q_s3[0:rows, 0:nb].unsqueeze(2).broadcast_to([rows, nb, 16]),
                        OP.mult)
        v.tensor_scalar(q_fa[0:rows, 0:cols].bitcast(U32),
                        q_fb[0:rows, 0:cols].bitcast(U32), 0x001FFFFF, None, OP.add)
        v.tensor_scalar(q_fc[0:rows, 0:cols].bitcast(U32),
                        q_fa[0:rows, 0:cols].bitcast(U32), 0xFFC00000, None,
                        OP.bitwise_and)
        v.tensor_scalar(q_fa[0:rows, 0:cols].bitcast(U32),
                        q_fb[0:rows, 0:cols].bitcast(U32), 0x003FFFFF, None, OP.add)
        v.tensor_scalar(q_fd[0:rows, 0:cols].bitcast(U32),
                        q_fa[0:rows, 0:cols].bitcast(U32), 0xFF800000, None,
                        OP.bitwise_and)
        v.tensor_scalar(q_fa[0:rows, 0:cols], q_fd[0:rows, 0:cols], 0.5, None, OP.max)
        v.tensor_scalar(q_m8[0:rows, 0:cols], q_fb[0:rows, 0:cols], 1.0, None, OP.is_lt)
        v.select(q_fd[0:rows, 0:cols], q_m8[0:rows, 0:cols],
                 q_fa[0:rows, 0:cols], q_fc[0:rows, 0:cols], add_drain=True)
        v.tensor_scalar(q_fa[0:rows, 0:cols], q_fb[0:rows, 0:cols], 0.25, None, OP.is_gt)
        v.tensor_tensor(q_fb[0:rows, 0:cols], q_fd[0:rows, 0:cols],
                        q_fa[0:rows, 0:cols], OP.mult)
        v.tensor_tensor(q_fa[0:rows, 0:cols], q_fb[0:rows, 0:cols],
                        q_fs[0:rows, 0:cols], OP.mult)
        retq[0] = v.tensor_tensor(
            out_ap.rearrange("p (g k) -> p g k", k=16),
            q_fa[0:rows, 0:cols].rearrange("p (g k) -> p g k", k=16),
            q_s2[0:rows, 0:nb].unsqueeze(2).broadcast_to([rows, nb, 16]),
            OP.mult)

    phase('p1')
    for it in range(NT):
        with nc.Block() as blk:
            @blk.sync
            def _(sync, it=it):
                kb.dma(sync, xt[:, 0:D], x_t[it])
                kb.wait(sync, "dml")
        with nc.Block() as blk:
            @blk.vector
            def _(v):
                kb.flush(v, v.tensor_reduce(stat[:, 0:1], xt[:, 0:D], AX.X, OP.add))
                kb.flush(v, v.tensor_scalar(stat[:, 0:1], stat[:, 0:1], 1.0 / D,
                                            None, OP.mult))
            @blk.scalar
            def _(s):
                s.activation(q_fa[:, 0:D], xt[:, 0:D], AF.Square)
        with nc.Block() as blk:
            @blk.vector
            def _(v):
                kb.flush(v, v.tensor_reduce(stat[:, 1:2], q_fa[:, 0:D], AX.X, OP.add))
                kb.flush(v, v.tensor_scalar(stat[:, 1:2], stat[:, 1:2], 1.0 / D,
                                            None, OP.mult))
                kb.flush(v, v.tensor_tensor(stat[:, 2:3], stat[:, 0:1], stat[:, 0:1],
                                            OP.mult))
                kb.flush(v, v.tensor_tensor(stat[:, 3:4], stat[:, 1:2], stat[:, 2:3],
                                            OP.subtract))
                kb.flush(v, v.tensor_scalar(stat[:, 3:4], stat[:, 3:4], 1e-5,
                                            None, OP.add))
        with nc.Block() as blk:
            @blk.scalar
            def _(s):
                s.activation(stat[:, 4:5], stat[:, 3:4], AF.Sqrt)
        with nc.Block() as blk:
            @blk.vector
            def _(v):
                kb.flush(v, v.reciprocal(stat[:, 5:6], stat[:, 4:5]))
                kb.flush(v, v.tensor_tensor(stat[:, 6:7], stat[:, 0:1], stat[:, 5:6],
                                            OP.mult))
                kb.flush(v, v.tensor_scalar(stat[:, 6:7], stat[:, 6:7], -1.0,
                                            None, OP.mult))
        with nc.Block() as blk:
            @blk.vector
            def _(v):
                kb.flush(v, v.tensor_scalar(stat[:, 7:8], stat[:, 0:1], -1.0,
                                            None, OP.mult))
                v.tensor_scalar(xnt[:], xt[:, 0:D], stat[:, 7:8], stat[:, 5:6],
                                OP.add, OP.mult)
        with nc.Block() as blk:
            @blk.vector
            def _(v):
                v.tensor_copy(xnb[:], xnt[:])
            @blk.scalar
            def _(s):
                q_sign(s, xnt[:], 128, D)
        with nc.Block() as blk:
            @blk.vector
            def _(v):
                q_main(v, xnt[:], qb[:, 0:D], 128, D)
        with nc.Block() as blk:
            @blk.sync
            def _(sync, it=it):
                for kc in range(KC):
                    kb.dma(sync, tp1[:], xnb[:, kc * 128:(kc + 1) * 128],
                           name="dtr", transpose=True)
                    kb.wait(sync, "dtr")
                    kb.dma(sync, ag_xnf_i.ap()[kc * 128:(kc + 1) * 128,
                                               it * 128:(it + 1) * 128], tp1[:])
                    kb.wait(sync, "dml")
                    kb.dma(sync, xqf_r[:, kc * T + it * 128:kc * T + (it + 1) * 128],
                           qb[:, kc * 128:(kc + 1) * 128], name="dtr", transpose=True)
                    kb.wait(sync, "dtr")

    phase('wq')
    with nc.Block() as blk:
        @blk.gpsimd
        def _(g):
            cs, _ = kb.inc("coll", 1)
            g.collective_compute("AllGather", OP.bypass,
                                 replica_groups=[[0, 1, 2, 3], [4, 5, 6, 7]],
                                 ins=[ag_xnf_i.ap().opt()],
                                 outs=[ag_xnf_o.ap().opt()]).then_inc(cs, 1)

    for (win, wout, rows, cols) in ((wgate_in, wqg_i, I3 // 8, D),
                                    (wup_in, wqu_i, I3 // 8, D),
                                    (wdown_in, wqd_i, D // 8, I3)):
        nrt = (rows + 127) // 128
        ncs = (cols + 1023) // 1024
        for rt in range(nrt):
            r0 = rt * 128
            pr = min(128, rows - r0)
            with nc.Block() as blk:
                @blk.sync
                def _(sync, win=win, r0=r0, pr=pr, cols=cols):
                    kb.dma(sync, xt[0:pr, 0:cols], win[r0:r0 + pr, :])
                    kb.wait(sync, "dml")
            for csl in range(ncs):
                c0 = csl * 1024
                pc = min(1024, cols - c0)
                with nc.Block() as blk:
                    @blk.scalar
                    def _(s, r0=r0, pr=pr, c0=c0, pc=pc):
                        q_sign(s, xt[0:pr, c0:c0 + pc], pr, pc)
                with nc.Block() as blk:
                    @blk.vector
                    def _(v, r0=r0, pr=pr, c0=c0, pc=pc):
                        q_main(v, xt[0:pr, c0:c0 + pc], qb[0:pr, 0:pc], pr, pc)
                with nc.Block() as blk:
                    @blk.sync
                    def _(sync, wout=wout, r0=r0, pr=pr, c0=c0, pc=pc):
                        kb.dma(sync, wout.ap()[r0:r0 + pr, c0:c0 + pc], qb[0:pr, 0:pc])
                        kb.wait(sync, "dml")
    with nc.Block() as blk:
        @blk.gpsimd
        def _(g):
            for (i_, o_) in ((wqg_i, wqg_o), (wqu_i, wqu_o), (wqd_i, wqd_o)):
                cs, _ = kb.inc("coll", 1)
                g.collective_compute("AllGather", OP.bypass,
                                     replica_groups=[list(range(8))],
                                     ins=[i_.ap().opt()],
                                     outs=[o_.ap().opt()]).then_inc(cs, 1)
            g.wait_ge(kb.sems["coll"], kb.cnt["coll"])

    phase('f1')
    with nc.Block() as blk:
        @blk.gpsimd
        def _(g):
            g.wait_ge(kb.sems["coll"], kb.cnt["coll"])

    kb.sem("f1m"); kb.sem("f1a"); kb.sem("f1h"); kb.sem("f1g"); kb.sem("f1q")
    kb.sem("f1s")
    for isl in range(NIS):
        i0 = isl * ISL
        with nc.Block() as blk:
            @blk.sync
            def _(sync, i0=i0):
                for kc in range(KC):
                    for (srcw, off) in ((wqg_o, 0), (wqu_o, ISL)):
                        for j in range(3):
                            w0 = j * 128
                            wN = min(128, ISL - w0)
                            kb.dma(sync, wgu[:, kc * 2 * ISL + off + w0:
                                            kc * 2 * ISL + off + w0 + wN],
                                   srcw.ap()[i0 + w0:i0 + w0 + wN,
                                             kc * 128:(kc + 1) * 128],
                                   name="p4w", transpose=True)
                kb.wait(sync, "p4w")
        kb.sem("f1so"); kb.sem("f1t0"); kb.sem("f1t1"); kb.sem("f1t2")
        c0 = {k: kb.cnt[k] for k in ("f1m", "f1a", "f1h", "f1g", "f1q", "f1s",
                                     "f1so", "f1t0", "f1t1", "f1t2")}
        with nc.Block() as blk:
            @blk.tensor
            def _(te, isl=isl, c0=c0):
                for it in range(NT):
                    pi = it % 2
                    if it >= 2:
                        te.wait_ge(kb.sems["f1h"], c0["f1h"] + it - 1)
                    for kc in range(KC):
                        lhs = xqf_r[:, kc * T + it * 128:kc * T + (it + 1) * 128]
                        te.matmul(ps[pi][:, 0:ISL], lhs,
                                  wgu[:, kc * 2 * ISL:kc * 2 * ISL + ISL],
                                  start=(kc == 0), stop=(kc == KC - 1))
                        mm = te.matmul(ps[2 + pi][:, 0:ISL], lhs,
                                       wgu[:, kc * 2 * ISL + ISL:
                                           kc * 2 * ISL + 2 * ISL],
                                       start=(kc == 0), stop=(kc == KC - 1))
                    mm.then_inc(kb.sems["f1m"], 1)
            @blk.scalar
            def _(s, isl=isl, c0=c0):
                for it in range(NT):
                    pi = it % 2
                    s.wait_ge(kb.sems["f1m"], c0["f1m"] + it + 1)
                    if it >= 2:
                        s.wait_ge(kb.sems["f1h"], c0["f1h"] + it - 1)
                    s.activation(gu_s[:, pi * ISL:(pi + 1) * ISL], ps[pi][:, 0:ISL],
                                 AF.Silu).then_inc(kb.sems["f1a"], 1)
                    s.wait_ge(kb.sems["f1h"], c0["f1h"] + it + 1)
                    s.activation(q_fs[:, 0:ISL], hfa[:, pi * ISL:(pi + 1) * ISL],
                                 AF.Sign).then_inc(kb.sems["f1g"], 1)
            @blk.vector
            def _(v, isl=isl, c0=c0):
                for it in range(NT):
                    pi = it % 2
                    v.wait_ge(kb.sems["f1a"], c0["f1a"] + it + 1)
                    v.tensor_tensor(hfa[:, pi * ISL:(pi + 1) * ISL],
                                    gu_s[:, pi * ISL:(pi + 1) * ISL],
                                    ps[2 + pi][:, 0:ISL], OP.mult)\
                        .then_inc(kb.sems["f1h"], 1)
                    v.wait_ge(kb.sems["f1g"], c0["f1g"] + it + 1)
                    if it >= 2:
                        v.wait_ge(kb.sem("f1so"), c0["f1so"] + it - 1)
                    q_main_nosign(v, hfa[:, pi * ISL:(pi + 1) * ISL],
                                  qb[:, pi * 512:pi * 512 + ISL], 128, ISL)
                    retq[0].then_inc(kb.sems["f1q"], 1)
            @blk.sync
            def _(sync, isl=isl, c0=c0):
                for it in range(NT):
                    pi = it % 2
                    st = it % 3
                    sync.wait_ge(kb.sems["f1q"], c0["f1q"] + it + 1)
                    if it >= 3:
                        sync.wait_ge(kb.sem("f1so"), c0["f1so"] + it - 2)
                    for c in range(3):
                        sync.dma_start(
                            trow[:, st * 384 + c * 128:st * 384 + (c + 1) * 128],
                            qb[:, pi * 512 + c * 128:pi * 512 + (c + 1) * 128],
                            transpose=True)\
                            .then_inc(kb.sem(f"f1t{st}"), 16)
            @blk.gpsimd
            def _(g, isl=isl, i0=i0, c0=c0):
                for it in range(NT):
                    st = it % 3
                    g.wait_ge(kb.sem(f"f1t{st}"), c0[f"f1t{st}"] + (it // 3 + 1) * 48)
                    for c in range(3):
                        rr = 96 if c == 2 else 128
                        g.dma_start(hqT_d.ap()[i0 + c * 128:i0 + c * 128 + rr,
                                               it * 128:(it + 1) * 128],
                                    trow[0:rr, st * 384 + c * 128:
                                         st * 384 + c * 128 + 128])\
                            .then_inc(kb.sems["f1s"], 16)
                    g.wait_ge(kb.sems["f1s"], c0["f1s"] + (it + 1) * 48)
                    g.sem_inc(kb.sem("f1so"), 1)
        nt3 = [(NT + 2 - i) // 3 for i in range(3)]
        for k, d in (("f1m", NT), ("f1a", NT), ("f1h", NT), ("f1g", NT),
                     ("f1q", NT), ("f1s", NT * 48), ("f1so", NT),
                     ("f1t0", nt3[0] * 48), ("f1t1", nt3[1] * 48),
                     ("f1t2", nt3[2] * 48)):
            kb.cnt[k] = c0[k] + d

    phase('p5')
    with nc.Block() as blk:
        @blk.sync
        def _(sync):
            for kc2 in range(NI128):
                for j in range(KC):
                    kb.dma(sync, wdq[:, kc2 * 1024 + j * 128:kc2 * 1024 + (j + 1) * 128],
                           wqd_o.ap()[j * 128:(j + 1) * 128,
                                      kc2 * 128:(kc2 + 1) * 128],
                           name="p5w", transpose=True)
            kb.wait(sync, "p5w")

    kb.sem("p5mm"); kb.sem("p5ev"); kb.sem("p5st")
    with nc.Block() as blk:
        @blk.sync
        def _(sync):
            for it in range(NT):
                for kc2 in range(NI128):
                    i = it * NI128 + kc2
                    if i >= 4:
                        kb.wait(sync, "p5mm", (i - 3))
                    sync.dma_start(qb[:, (i % 4) * 128:(i % 4 + 1) * 128],
                                   hqT_d.ap()[kc2 * 128:(kc2 + 1) * 128,
                                              it * 128:(it + 1) * 128])\
                        .then_inc(kb.sem(f"p5l{i % 4}"), 16)
        @blk.tensor
        def _(te):
            for it in range(NT):
                for kc2 in range(NI128):
                    i = it * NI128 + kc2
                    te.wait_ge(kb.sem(f"p5l{i % 4}"), 16 * (i // 4 + 1))
                    lhs = qb[:, (i % 4) * 128:(i % 4 + 1) * 128]
                    te.matmul(ps[2][:, 0:512], lhs, wdq[:, kc2 * 1024:kc2 * 1024 + 512],
                              start=(kc2 == 0), stop=(kc2 == NI128 - 1))
                    te.matmul(ps[3][:, 0:512], lhs,
                              wdq[:, kc2 * 1024 + 512:(kc2 + 1) * 1024],
                              start=(kc2 == 0), stop=(kc2 == NI128 - 1))\
                        .then_inc(kb.sems["p5mm"], 1)
                te.wait_ge(kb.sems["p5ev"], it + 1)
        @blk.vector
        def _(v):
            for it in range(NT):
                v.wait_ge(kb.sems["p5mm"], (it + 1) * NI128)
                if it >= 2:
                    v.wait_ge(kb.sem(f"p5st{it % 2}"), 16 * (it // 2))
                v.tensor_copy(yrow[it % 2][:, 0:512], ps[2][:, 0:512])
                v.tensor_copy(yrow[it % 2][:, 512:1024], ps[3][:, 0:512])\
                    .then_inc(kb.sems["p5ev"], 1)
        @blk.scalar
        def _(s):
            for it in range(NT):
                s.wait_ge(kb.sems["p5ev"], it + 1)
                s.dma_start(ffn_t[it], yrow[it % 2][:])\
                    .then_inc(kb.sem(f"p5st{it % 2}"), 16)
            s.wait_ge(kb.sem("p5st0"), (NT // 2) * 16)
            s.wait_ge(kb.sem("p5st1"), (NT // 2) * 16)
    kb.cnt["p5mm"] = NT * NI128
    kb.cnt["p5ev"] = NT
    kb.cnt["p5st"] = NT * 16

    if ATTN:
        phase('a0')
        with nc.Block() as blk:
            @blk.sync
            def _(sync):
                kb.dma(sync, mlow[:], mlow_in)
                kb.dma(sync, mupp[:], mupp_in)
                kb.dma(sync, mupi[:], mupi_in)
                kb.wait(sync, "dml")
            @blk.vector
            def _(v):
                v.memset(onesb[:], 1.0)
                v.memset(S0[:], 0.0)
        for kc in range(KC):
            with nc.Block() as blk:
                @blk.sync
                def _(sync, kc=kc):
                    kb.dma(sync, xt[:, 0:776],
                           wkvqab_in.rearrange("(n p) d -> n p d", p=128)[kc])
                    kb.wait(sync, "dml")
            with nc.Block() as blk:
                @blk.vector
                def _(v, kc=kc):
                    v.tensor_copy(wkv_bf[:, kc * 776:(kc + 1) * 776], xt[:, 0:776])
        with nc.Block() as blk:
            @blk.sync
            def _(sync):
                kb.dma(sync, stat[0:1, 0:8], bab_in)
                kb.wait(sync, "dml")
        with nc.Block() as blk:
            @blk.vector
            def _(v):
                v.tensor_copy(bab_bf[:], stat[0:1, 0:8])

        phase('a1')
        kb.sem("a1l"); kb.sem("a1m"); kb.sem("a1e"); kb.sem("a1s"); kb.sem("a1g")
        with nc.Block() as blk:
            @blk.sync
            def _(sync):
                nld = 0
                for tt in range(NTF):
                    if tt >= 1:
                        sync.wait_ge(kb.sems["a1e"], tt)
                    for kc in range(KC):
                        sync.dma_start(
                            xnf_stage[:, kc * 128:(kc + 1) * 128],
                            ag_xnf_o.ap()[(tt // 16) * D + kc * 128:
                                          (tt // 16) * D + (kc + 1) * 128,
                                          (tt % 16) * 128:(tt % 16) * 128 + 128]
                        ).then_inc(kb.sems["a1l"], 16)
                        nld += 16
            @blk.scalar
            def _(s):
                for tt in range(NTF):
                    s.wait_ge(kb.sems["a1m"], tt + 1)
                    if tt >= 1:
                        s.wait_ge(kb.sems["a1e"], tt)
                    s.activation(abt[:, 0:8], ps[1][:, 256:264], AF.Sigmoid)\
                        .then_inc(kb.sems["a1g"], 1)
                    s.activation(n2sq[:, 0:256], ps[0][:, 256:512], AF.Square)
                    s.activation(n2sq[:, 256:512], ps[1][:, 0:256], AF.Square)\
                        .then_inc(kb.sems["a1g"], 1)
            @blk.tensor
            def _(te):
                for tt in range(NTF):
                    te.wait_ge(kb.sems["a1l"], (tt + 1) * KC * 16)
                    if tt >= 1:
                        te.wait_ge(kb.sems["a1e"], tt)
                    for kc in range(KC):
                        lhs = xnf_stage[:, kc * 128:(kc + 1) * 128]
                        te.matmul(ps[0][:, 0:512], lhs,
                                  wkv_bf[:, kc * 776:kc * 776 + 512],
                                  start=(kc == 0), stop=(kc == KC - 1))
                        te.matmul(ps[1][:, 0:264], lhs,
                                  wkv_bf[:, kc * 776 + 512:kc * 776 + 776],
                                  start=(kc == 0), stop=False)
                    te.matmul(ps[1][:, 256:264], onesb[:], bab_bf[:],
                              start=False, stop=True).then_inc(kb.sems["a1m"], 1)
            @blk.vector
            def _(v):
                for tt in range(NTF):
                    v.wait_ge(kb.sems["a1m"], tt + 1)
                    v.wait_ge(kb.sems["a1g"], tt * 2 + 1)
                    if tt >= 1:
                        v.wait_ge(kb.sem("a1so"), tt)
                    for h in range(4):
                        v.tensor_scalar(ktile[:, h * 64:(h + 1) * 64],
                                        ps[0][:, h * 64:(h + 1) * 64],
                                        abt[:, 4 + h:5 + h], None, OP.mult)
                    v.tensor_copy(ktile[:, 256:512], ps[0][:, 256:512])
                    v.tensor_copy(ktile[:, 512:768], ps[1][:, 0:256])
                    v.wait_ge(kb.sems["a1g"], tt * 2 + 2)
                    kb.flush(v, v.tensor_reduce(
                        sctile[:, 8:12],
                        n2sq[:, 0:256].rearrange("p (h e) -> p h e", e=64),
                        AX.X, OP.add))
                    kb.flush(v, v.tensor_reduce(
                        sctile[:, 12:16],
                        n2sq[:, 256:512].rearrange("p (h e) -> p h e", e=64),
                        AX.X, OP.add))
                    v.tensor_copy(sctile[:, 0:8], abt[:, 0:8])\
                        .then_inc(kb.sems["a1e"], 1)
            @blk.gpsimd
            def _(g):
                for tt in range(NTF):
                    g.wait_ge(kb.sems["a1e"], tt + 1)
                    r0 = tt * 128
                    g.dma_start(vt_d.ap()[r0:r0 + 128, :], ktile[:, 0:256])\
                        .then_inc(kb.sems["a1s"], 16)
                    g.dma_start(kt_d.ap()[r0:r0 + 128, :], ktile[:, 256:512])\
                        .then_inc(kb.sems["a1s"], 16)
                    g.dma_start(qt_d.ap()[r0:r0 + 128, :], ktile[:, 512:768])\
                        .then_inc(kb.sems["a1s"], 16)
                    g.dma_start(scal_d.ap()[r0:r0 + 128, :], sctile[:])\
                        .then_inc(kb.sems["a1s"], 16)
                    g.wait_ge(kb.sems["a1s"], (tt + 1) * 64)
                    g.sem_inc(kb.sem("a1so"), 1)
        kb.cnt["a1l"] = NTF * KC * 16
        kb.cnt["a1m"] = NTF
        kb.cnt["a1g"] = NTF * 2
        kb.cnt["a1e"] = NTF
        kb.cnt["a1s"] = NTF * 64
        kb.cnt["a1so"] = NTF

        phase('a2')
        with nc.Block() as blk:
            @blk.sync
            def _(sync):
                kb.dma(sync, tri64[:], tri_in)
                kb.wait(sync, "dml")
            @blk.vector
            def _(v):
                v.memset(ones1[:], 1.0)
        QW, QR, QM, QSR, QSO, QSS, QGC = range(7)

        def dsl(h, q):
            return der[:, (h * 8 + q) * 64:(h * 8 + q) * 64 + 64]

        def tsl(q):
            return tmp8[:, q * 64:(q + 1) * 64]
        TLA, TLB, TLK, TLQ, TLG, TLG0, TREF, TLGC = range(8)
        sc2 = scal_d.ap().rearrange("(cp cpar tok) col -> cpar tok cp col",
                                    cpar=2, tok=64)
        for h in range(4):
            with nc.Block() as blk:
                @blk.sync
                def _(sync, h=h):
                    for cpar in range(2):
                        kb.dma(sync, rawh[cpar * 64:(cpar + 1) * 64, :]
                               .rearrange("p (cp col) -> p cp col", col=16),
                               sc2[cpar], name="a2l")
                    kb.wait(sync, "a2l")
            with nc.Block() as blk:
                @blk.scalar
                def _(s, h=h):
                    rw = rawh[:].rearrange("p (cp col) -> p cp col", col=16)
                    s.activation(tsl(TLA).rearrange("p c -> p c 1" ) if False else
                                 tsl(TLA).rearrange("p (c o) -> p c o", o=1),
                                 rw[:, :, 0 + h:1 + h], AF.Ln)
                    s.activation(tsl(TLB).rearrange("p (c o) -> p c o", o=1),
                                 rw[:, :, 4 + h:5 + h], AF.Ln)
            with nc.Block() as blk:
                @blk.vector
                def _(v, h=h):
                    rw = rawh[:].rearrange("p (cp col) -> p cp col", col=16)
                    v.tensor_scalar(tsl(TLK).rearrange("p (c o) -> p c o", o=1),
                                    rw[:, :, 8 + h:9 + h], 1e-24, None, OP.max)
                    v.tensor_scalar(tsl(TLQ).rearrange("p (c o) -> p c o", o=1),
                                    rw[:, :, 12 + h:13 + h], 1e-24, None, OP.max)
            with nc.Block() as blk:
                @blk.scalar
                def _(s, h=h):
                    s.activation(tsl(TLK), tsl(TLK), AF.Ln)
                    s.activation(tsl(TLQ), tsl(TLQ), AF.Ln)
            with nc.Block() as blk:
                @blk.vector
                def _(v, h=h):
                    v.tensor_scalar(tsl(TLK), tsl(TLK), -0.5, None, OP.mult)
                    v.tensor_scalar(tsl(TLQ), tsl(TLQ), -0.5, None, OP.mult)
            with nc.Block() as blk:
                @blk.tensor
                def _(te, h=h):
                    for cpar in range(2):
                        te.matmul(ps[0][cpar * 64:(cpar + 1) * 64, 0:64],
                                  tri64[cpar * 64:(cpar + 1) * 64, :],
                                  tsl(TLA)[cpar * 64:(cpar + 1) * 64, :],
                                  start=True, stop=True)
            with nc.Block() as blk:
                @blk.vector
                def _(v, h=h):
                    v.tensor_copy(tsl(TLG), ps[0][:, 0:64])
                    v.tensor_tensor(tsl(TLG0), tsl(TLG), tsl(TLA), OP.subtract)
            with nc.Block() as blk:
                @blk.sync
                def _(sync, h=h):
                    kb.dma(sync, rowm[0:1, 0:64], tsl(TLG0)[32:33, :], name="a2r")
                    kb.dma(sync, rowm[0:1, 64:128], tsl(TLG0)[96:97, :], name="a2r")
                    kb.dma(sync, rowm[0:1, 128:192], tsl(TLG)[63:64, :], name="a2r")
                    kb.dma(sync, rowm[0:1, 192:256], tsl(TLG)[127:128, :], name="a2r")
                    kb.wait(sync, "a2r")
            with nc.Block() as blk:
                @blk.tensor
                def _(te, h=h):
                    for cpar in range(2):
                        te.matmul(ps[1][cpar * 64:(cpar + 1) * 64, 0:64],
                                  ones1[:, 0:64], rowm[0:1, cpar * 64:(cpar + 1) * 64],
                                  start=True, stop=True)
                        te.matmul(ps[2][cpar * 64:(cpar + 1) * 64, 0:64],
                                  ones1[:, 0:64],
                                  rowm[0:1, 128 + cpar * 64:192 + cpar * 64],
                                  start=True, stop=True)
            with nc.Block() as blk:
                @blk.vector
                def _(v, h=h):
                    v.tensor_copy(tsl(TREF), ps[1][:, 0:64])
                    v.tensor_copy(tsl(TLGC), ps[2][:, 0:64])
            with nc.Block() as blk:
                @blk.vector
                def _(v, h=h):
                    v.tensor_tensor(dsl(h, QW), tsl(TLK), tsl(TLG), OP.subtract)
                    v.tensor_tensor(dsl(h, QW), dsl(h, QW), tsl(TREF), OP.add)
                    v.tensor_tensor(dsl(h, QSR), tsl(TLG0), tsl(TLB), OP.add)
                    v.tensor_tensor(dsl(h, QSR), dsl(h, QSR), tsl(TLK), OP.add)
                    v.tensor_tensor(dsl(h, QR), dsl(h, QSR), tsl(TREF), OP.subtract)
                    v.tensor_tensor(dsl(h, QSO), tsl(TLG), tsl(TLQ), OP.add)
                    v.tensor_tensor(dsl(h, QM), dsl(h, QSO), tsl(TREF), OP.subtract)
                    v.tensor_tensor(dsl(h, QSS), tsl(TLGC), tsl(TLG), OP.subtract)
                    v.tensor_tensor(dsl(h, QSS), dsl(h, QSS), tsl(TLK), OP.add)
                    v.tensor_copy(dsl(h, QGC), tsl(TLGC))
                    for q in range(7):
                        v.tensor_scalar(dsl(h, q), dsl(h, q), CLIP, None, OP.min)
            with nc.Block() as blk:
                @blk.scalar
                def _(s, h=h):
                    for q in range(7):
                        s.activation(dsl(h, q), dsl(h, q), AF.Exp)
            with nc.Block() as blk:
                @blk.vector
                def _(v, h=h):
                    v.tensor_scalar(dsl(h, QSR), dsl(h, QSR), -1.0, None, OP.mult)
        with nc.Block() as blk:
            @blk.sync
            def _(sync):
                qlist = (QSR, QSO, QSS, QR, QW, QGC)
                for pr in range(2):
                    for qi, qq in enumerate(qlist):
                        for hh in range(2):
                            h = pr * 2 + hh
                            for cpar in range(2):
                                kb.dma(sync,
                                       stk[hh * 64:(hh + 1) * 64,
                                           pr * 768 + qi * 128 + cpar * 64:
                                           pr * 768 + qi * 128 + (cpar + 1) * 64],
                                       dsl(h, qq)[cpar * 64:(cpar + 1) * 64, :],
                                       name="a2k")
                kb.wait(sync, "a2k")

        def stk_col(pr, qi, c):
            return stk[:, pr * 768 + qi * 128 + (c % 2) * 64 + c // 2:
                       pr * 768 + qi * 128 + (c % 2) * 64 + c // 2 + 1]

        phase('a3')
        kb.sem("a3l"); kb.sem("a3v"); kb.sem("a3t"); kb.sem("a3s")
        trow2 = ktile
        with nc.Block() as blk:
            @blk.sync
            def _(sync):
                def loads(tt):
                    pi = tt % 2
                    sync.dma_start(qb[:, pi * 512:pi * 512 + 256],
                                   kt_d.ap()[tt * 128:(tt + 1) * 128, :])\
                        .then_inc(kb.sem(f"a3l{pi}"), 16)
                    sync.dma_start(qb[:, pi * 512 + 256:pi * 512 + 512],
                                   qt_d.ap()[tt * 128:(tt + 1) * 128, :])\
                        .then_inc(kb.sem(f"a3l{pi}"), 16)
                loads(0)
                for tt in range(NTF):
                    pi = tt % 2
                    if tt + 1 < NTF:
                        if tt >= 1:
                            sync.wait_ge(kb.sems["a3v"], tt)
                            sync.wait_ge(kb.sem(f"a3t{(tt - 1) % 2}"),
                                         160 * ((tt - 1) // 2 + 1))
                        loads(tt + 1)
                    sync.wait_ge(kb.sems["a3v"], tt + 1)
                    if tt >= 1:
                        sync.wait_ge(kb.sem("a3so"), tt)
                    vsrc = vart if pi == 0 else ktile
                    srcs = [qb[:, pi * 512:pi * 512 + 128],
                            qb[:, pi * 512 + 128:pi * 512 + 256],
                            vsrc[:, 0:128], vsrc[:, 128:256],
                            vsrc[:, 256:384], vsrc[:, 384:512],
                            qb[:, pi * 512 + 256:pi * 512 + 384],
                            qb[:, pi * 512 + 384:pi * 512 + 512],
                            vsrc[:, 512:640], vsrc[:, 640:768]]
                    for j in range(10):
                        sync.dma_start(trow[:, j * 128:(j + 1) * 128], srcs[j],
                                       transpose=True)\
                            .then_inc(kb.sem(f"a3t{pi}"), 16)
            @blk.vector
            def _(v):
                for tt in range(NTF):
                    pi = tt % 2
                    v.wait_ge(kb.sem(f"a3l{pi}"), (tt // 2 + 1) * 32)
                    if tt >= 2:
                        v.wait_ge(kb.sem(f"a3t{pi}"), (tt // 2) * 160)
                    vdst = vart if pi == 0 else ktile
                    for h in range(4):
                        v.tensor_scalar(vdst[:, h * 64:(h + 1) * 64],
                                        qb[:, pi * 512 + h * 64:
                                           pi * 512 + (h + 1) * 64],
                                        dsl(h, QW)[:, tt:tt + 1], None, OP.mult)
                        v.tensor_scalar(vdst[:, 256 + h * 64:256 + (h + 1) * 64],
                                        qb[:, pi * 512 + h * 64:
                                           pi * 512 + (h + 1) * 64],
                                        dsl(h, QR)[:, tt:tt + 1], None, OP.mult)
                        vv = v.tensor_scalar(vdst[:, 512 + h * 64:512 + (h + 1) * 64],
                                             qb[:, pi * 512 + 256 + h * 64:
                                                pi * 512 + 256 + (h + 1) * 64],
                                             dsl(h, QM)[:, tt:tt + 1], None, OP.mult)
                    vv.then_inc(kb.sems["a3v"], 1)
            @blk.gpsimd
            def _(g):
                dsts = [kf_d, kf_d, kw_d, kw_d, kr_d, kr_d, qf_d, qf_d, qm_d, qm_d]
                for tt in range(NTF):
                    g.wait_ge(kb.sem(f"a3t{tt % 2}"), (tt // 2 + 1) * 160)
                    for j in range(10):
                        pr = j % 2
                        g.dma_start(dsts[j].ap()[pr * 128:(pr + 1) * 128,
                                                 tt * 128:(tt + 1) * 128],
                                    trow[:, j * 128:(j + 1) * 128])\
                            .then_inc(kb.sems["a3s"], 16)
                    g.wait_ge(kb.sems["a3s"], (tt + 1) * 160)
                    g.sem_inc(kb.sem("a3so"), 1)
        kb.cnt["a3l"] = NTF * 32
        kb.cnt["a3v"] = NTF
        kb.cnt["a3t"] = NTF * 160
        kb.cnt["a3s"] = NTF * 160
        kb.cnt["a3so"] = NTF

    phase('a4')
    if ATTN:
        NB = 9 * 16
        PG = 15
        ED = 23
        NU = NCH * 2
        kb.sem("dl"); kb.sem("dm"); kb.sem("de"); kb.sem("dy")
        KFV, KWV, KRV, QFV, QMV = range(5)

        def kfsl(pi, var, hh=None):
            basec = (pi * 5 + var) * 64
            if hh is None:
                return kfs[:, basec:basec + 64]
            return kfs[hh * 64:(hh + 1) * 64, basec:basec + 64]

        with nc.Block() as blk:
            @blk.sync
            def _(sync):
                for u in range(NU):
                    c, p = u // 2, u % 2
                    pi = u % 2
                    if u >= 2:
                        sync.wait_ge(kb.sems["dm"], PG * (u - 1))
                        sync.wait_ge(kb.sems["de"], ED * (u - 1))
                    for var, srcd in ((KFV, kf_d), (KWV, kw_d), (KRV, kr_d),
                                      (QFV, qf_d), (QMV, qm_d)):
                        sync.dma_start(kfsl(pi, var),
                                       srcd.ap()[p * 128:(p + 1) * 128,
                                                 c * 64:(c + 1) * 64])\
                            .then_inc(kb.sem(f"dl{pi}"), 16)
                    for hh in range(2):
                        sync.dma_start(kts[hh * 64:(hh + 1) * 64,
                                           pi * 128:(pi + 1) * 128],
                                       kt_d.ap()[c * 64:(c + 1) * 64,
                                                 p * 128:(p + 1) * 128])\
                            .then_inc(kb.sem(f"dl{pi}"), 16)
                        sync.dma_start(vts[hh * 64:(hh + 1) * 64,
                                           pi * 64:(pi + 1) * 64],
                                       vt_d.ap()[c * 64:(c + 1) * 64,
                                                 (p * 2 + hh) * 64:
                                                 (p * 2 + hh + 1) * 64])\
                            .then_inc(kb.sem(f"dl{pi}"), 16)

            @blk.tensor
            def _(te):
                for u in range(NU):
                    c, p = u // 2, u % 2
                    pi = u % 2
                    cpp = c % 2
                    dm0 = PG * u
                    de0 = ED * u
                    te.wait_ge(kb.sem(f"dl{u % 2}"), NB * (u // 2 + 1))
                    if u >= 1:
                        te.wait_ge(kb.sems["de"], ED * u)
                    for hh in range(2):
                        r = slice(hh * 64, hh * 64 + 64)
                        te.matmul(ps[0][r, hh * 64:hh * 64 + 64],
                                  kfsl(pi, KFV, hh), kfsl(pi, KWV, hh),
                                  start=True, stop=True)
                    for hh in range(2):
                        r = slice(hh * 64, hh * 64 + 64)
                        te.matmul(ps[1][r, hh * 64:hh * 64 + 64],
                                  kfsl(pi, KFV, hh), kfsl(pi, KRV, hh),
                                  start=True, stop=True)
                    for hh in range(2):
                        r = slice(hh * 64, hh * 64 + 64)
                        te.matmul(ps[2][r, hh * 64:hh * 64 + 64],
                                  kfsl(pi, KFV, hh), kfsl(pi, QMV, hh),
                                  start=True, stop=True)\
                            .then_inc(kb.sems["dm"], 1) if hh == 1 else \
                            te.matmul(ps[2][r, hh * 64:hh * 64 + 64],
                                      kfsl(pi, KFV, hh), kfsl(pi, QMV, hh),
                                      start=True, stop=True)
                    for hh in range(2):
                        r = slice(hh * 64, hh * 64 + 64)
                        mm = te.matmul(ps[3][r, 0:64], kfsl(pi, KFV, hh),
                                       S0[r, (p * 2 + cpp) * 64:
                                          (p * 2 + cpp) * 64 + 64],
                                       start=True, stop=True)
                        if hh == 1:
                            mm.then_inc(kb.sems["dm"], 1)
                    te.wait_ge(kb.sems["de"], de0 + 4)
                    te.matmul(ps[6][:, 0:64], LTb[:], Yw[:, 0:64],
                              start=True, stop=True).then_inc(kb.sems["dm"], 1)
                    for j in range(1, 6):
                        sl_new = j % 2
                        if j == 1:
                            lp, lpt = Lb[:], LTb[:]
                        else:
                            lp = Pw[:, (1 - sl_new) * 128:(2 - sl_new) * 128]
                            lpt = PwT[:, (1 - sl_new) * 128:(2 - sl_new) * 128]
                        if j >= 2:
                            te.wait_ge(kb.sems["de"], de0 + 4 + 3 * (j - 1))
                        if j < 5:
                            te.matmul(ps[4][:, 0:128], lpt, lp, start=True, stop=True)
                            te.matmul(ps[5][:, 0:128], lp, lpt, start=True,
                                      stop=True).then_inc(kb.sems["dm"], 1)
                        else:
                            te.matmul(ps[5][:, 0:128], lp, lpt, start=True,
                                      stop=True).then_inc(kb.sems["dm"], 1)
                        if j < 5:
                            te.wait_ge(kb.sems["de"], de0 + 4 + 3 * j)
                        else:
                            te.wait_ge(kb.sems["de"], de0 + 4 + 3 * 4 + 2)
                        ysl = j % 2
                        te.matmul(ps[6][:, 0:64],
                                  PwT[:, sl_new * 128:(sl_new + 1) * 128],
                                  Yw[:, ysl * 64:(ysl + 1) * 64],
                                  start=True, stop=True).then_inc(kb.sems["dm"], 1)
                    te.wait_ge(kb.sems["de"], de0 + 19)
                    for hh in range(2):
                        r = slice(hh * 64, hh * 64 + 64)
                        te.matmul(ps[0][r, 0:64], kfsl(pi, QFV, hh),
                                  S0[r, (p * 2 + cpp) * 64:(p * 2 + cpp) * 64 + 64],
                                  start=True, stop=True)
                    te.matmul(ps[1][:, 0:64], MTb[:], Yw[:, 0:64],
                              start=True, stop=True).then_inc(kb.sems["dm"], 1)
                    te.wait_ge(kb.sems["de"], de0 + 20)
                    for hh in range(2):
                        r = slice(hh * 64, hh * 64 + 64)
                        mm = te.matmul(ps[2][r, 0:64],
                                       kts[r, pi * 128 + hh * 64:
                                           pi * 128 + hh * 64 + 64],
                                       Usb[r, :], start=True, stop=True)
                        if hh == 1:
                            mm.then_inc(kb.sems["dm"], 1)

            @blk.vector
            def _(v):
                for u in range(NU):
                    c, p = u // 2, u % 2
                    pi = u % 2
                    cpp = c % 2
                    dm0 = PG * u
                    de0 = ED * u
                    v.wait_ge(kb.sems["dm"], dm0 + 1)
                    v.tensor_tensor(evs[:], ps[0][:, 0:128], mlow[:], OP.mult)
                    v.tensor_scalar(Lb[:], evs[:], stk_col(p, 3, c), None, OP.mult)\
                        .then_inc(kb.sems["de"], 1)
                    v.tensor_tensor(evs2[:], ps[1][:, 0:128], mupp[:], OP.mult)
                    v.tensor_scalar(LTb[:], evs2[:], stk_col(p, 4, c), None, OP.mult)\
                        .then_inc(kb.sems["de"], 1)
                    v.tensor_tensor(evs[:], ps[2][:, 0:128], mupi[:], OP.mult)
                    v.tensor_scalar(MTb[:], evs[:], stk_col(p, 4, c), None, OP.mult)\
                        .then_inc(kb.sems["de"], 1)
                    v.wait_ge(kb.sems["dm"], dm0 + 2)
                    v.wait_ge(kb.sem(f"dl{u % 2}"), NB * (u // 2 + 1))
                    v.scalar_tensor_tensor(Yw[:, 0:64], ps[3][:, 0:64],
                                           stk_col(p, 0, c),
                                           vts[:, pi * 64:(pi + 1) * 64],
                                           OP.mult, OP.add)\
                        .then_inc(kb.sems["de"], 1)
                    v.wait_ge(kb.sems["dm"], dm0 + 3)
                    v.tensor_tensor(Yw[:, 64:128], Yw[:, 0:64], ps[6][:, 0:64],
                                    OP.subtract).then_inc(kb.sems["de"], 1)
                    for j in range(1, 6):
                        sl_new = j % 2
                        v.wait_ge(kb.sems["dm"], dm0 + 3 + 2 * (j - 1) + 1)
                        if j < 5:
                            v.tensor_copy(Pw[:, sl_new * 128:(sl_new + 1) * 128],
                                          ps[4][:, 0:128]).then_inc(kb.sems["de"], 1)
                        v.tensor_copy(PwT[:, sl_new * 128:(sl_new + 1) * 128],
                                      ps[5][:, 0:128]).then_inc(kb.sems["de"], 1)
                        v.wait_ge(kb.sems["dm"], dm0 + 3 + 2 * (j - 1) + 2)
                        ydst = (j + 1) % 2
                        v.tensor_tensor(Yw[:, ydst * 64:(ydst + 1) * 64],
                                        Yw[:, (j % 2) * 64:(j % 2) * 64 + 64],
                                        ps[6][:, 0:64], OP.add)\
                            .then_inc(kb.sems["de"], 1)
                    v.tensor_scalar(Usb[:], Yw[:, 0:64], stk_col(p, 2, c), None,
                                    OP.mult).then_inc(kb.sems["de"], 1)
                    v.wait_ge(kb.sems["dm"], dm0 + 14)
                    if u >= 2:
                        v.wait_ge(kb.sem("dyo"), u - 1)
                    v.tensor_scalar(evs[:, 0:64], ps[0][:, 0:64],
                                    stk_col(p, 1, c), None, OP.mult)\
                        .then_inc(kb.sems["de"], 1)
                    v.tensor_tensor(ytile[:, pi * 64:(pi + 1) * 64], evs[:, 0:64],
                                    ps[1][:, 0:64], OP.add)\
                        .then_inc(kb.sems["de"], 1)
                    v.wait_ge(kb.sems["dm"], dm0 + 15)
                    v.scalar_tensor_tensor(
                        S0[:, (p * 2 + 1 - cpp) * 64:(p * 2 + 1 - cpp) * 64 + 64],
                        S0[:, (p * 2 + cpp) * 64:(p * 2 + cpp) * 64 + 64],
                        stk_col(p, 5, c), ps[2][:, 0:64], OP.mult, OP.add)\
                        .then_inc(kb.sems["de"], 1)

            @blk.gpsimd
            def _(g):
                for u in range(NU):
                    c, p = u // 2, u % 2
                    pi = u % 2
                    g.wait_ge(kb.sems["de"], ED * u + 22)
                    for hh in range(2):
                        g.dma_start(y_my.ap()[c * 64:(c + 1) * 64,
                                              (p * 2 + hh) * 64:(p * 2 + hh + 1) * 64],
                                    ytile[hh * 64:(hh + 1) * 64,
                                          pi * 64:(pi + 1) * 64])\
                            .then_inc(kb.sems["dy"], 16)
                    g.wait_ge(kb.sems["dy"], 32 * (u + 1))
                    g.sem_inc(kb.sem("dyo"), 1)
        kb.cnt["dl"] = NB * NU
        kb.cnt["dm"] = PG * NU
        kb.cnt["de"] = ED * NU
        kb.cnt["dy"] = 32 * NU
        kb.cnt["dyo"] = NU

        if DEBUG:
            with nc.Block() as blk:
                @blk.sync
                def _(sync):
                    for ri, (dsrc, ddst) in enumerate(
                            ((kt_d, dbg_kt), (qt_d, dbg_qt), (vt_d, dbg_vt))):
                        for r in range(16):
                            kb.dma(sync, qb[:, 0:256],
                                   dsrc.ap()[r * 128:(r + 1) * 128, 0:256],
                                   name="dbg")
                            kb.wait(sync, "dbg")
                            kb.dma(sync, ddst[r * 128:(r + 1) * 128, 0:256],
                                   qb[:, 0:256], name="dbg")
                            kb.wait(sync, "dbg")
                    for r in range(16):
                        kb.dma(sync, qb[:, 0:256],
                               ag_xnf_o.ap()[0:128, r * 128:r * 128 + 256]
                               if r < 8 else
                               kf_d.ap()[0:128, (r - 8) * 256:(r - 8) * 256 + 256],
                               name="dbg")
                        kb.wait(sync, "dbg")
                        kb.dma(sync, dbg_kf[0:128, r * 256:r * 256 + 256],
                               qb[:, 0:256], name="dbg")
                        kb.wait(sync, "dbg")
                    kb.dma(sync, dbg_kw[0:128, 0:776], wkv_bf[:, 0:776], name="dbg")
                    kb.wait(sync, "dbg")
                    kb.dma(sync, dbg_kw[0:128, 1024:1024 + 1024],
                           ag_xnf_i.ap()[0:128, 0:1024], name="dbg")
                    kb.wait(sync, "dbg")
                    for r in range(NTF):
                        kb.dma(sync, xt[:, 0:256], y_my.ap()[r * 128:(r + 1) * 128, :],
                               name="dbg")
                        kb.wait(sync, "dbg")
                        kb.dma(sync, dbg_y[r * 128:(r + 1) * 128, :], xt[:, 0:256],
                               name="dbg")
                        kb.wait(sync, "dbg")
                    for r in range(NTF):
                        kb.dma(sync, xt[:, 0:16], scal_d.ap()[r * 128:(r + 1) * 128, :],
                               name="dbg")
                        kb.wait(sync, "dbg")
                        kb.dma(sync, dbg_sc[r * 128:(r + 1) * 128, :], xt[:, 0:16],
                               name="dbg")
                        kb.wait(sync, "dbg")
                    kb.dma(sync, dbg_stk[:], stk[:], name="dbg")
                    kb.wait(sync, "dbg")


        phase('b5')
        for kc in range(KC):
            with nc.Block() as blk:
                @blk.sync
                def _(sync, kc=kc):
                    kb.dma(sync, xt[:, 0:256],
                           wg_in.rearrange("(n p) d -> n p d", p=128)[kc])
                    kb.wait(sync, "dml")
            with nc.Block() as blk:
                @blk.vector
                def _(v, kc=kc):
                    v.tensor_copy(wbig[:, kc * 256:(kc + 1) * 256], xt[:, 0:256])
        sg_t2 = sg_d.ap().rearrange("(n p) d -> n p d", p=128)
        kb.sem("b5l"); kb.sem("b5m"); kb.sem("b5e"); kb.sem("b5s")
        with nc.Block() as blk:
            @blk.sync
            def _(sync):
                for tt in range(NTF):
                    if tt >= 2:
                        sync.wait_ge(kb.sems["b5m"], tt - 1)
                    for kc in range(KC):
                        sync.dma_start(
                            xqf_r[:, (tt % 2) * 1024 + kc * 128:
                                  (tt % 2) * 1024 + (kc + 1) * 128],
                            ag_xnf_o.ap()[(tt // 16) * D + kc * 128:
                                          (tt // 16) * D + (kc + 1) * 128,
                                          (tt % 16) * 128:(tt % 16) * 128 + 128])\
                            .then_inc(kb.sem(f"b5l{tt % 2}"), 16)
            @blk.tensor
            def _(te):
                for tt in range(NTF):
                    te.wait_ge(kb.sem(f"b5l{tt % 2}"), (tt // 2 + 1) * KC * 16)
                    if tt >= 2:
                        te.wait_ge(kb.sems["b5e"], tt - 1)
                    for kc in range(KC):
                        mm = te.matmul(ps[tt % 2][:, 0:256],
                                       xqf_r[:, (tt % 2) * 1024 + kc * 128:
                                             (tt % 2) * 1024 + (kc + 1) * 128],
                                       wbig[:, kc * 256:(kc + 1) * 256],
                                       start=(kc == 0), stop=(kc == KC - 1))
                    mm.then_inc(kb.sems["b5m"], 1)
            @blk.scalar
            def _(s):
                def ysq_sq(u):
                    su = u % 4
                    s.wait_ge(kb.sem(f"b6q{su}"), (u // 4 + 1) * 16)
                    if u >= 4:
                        s.wait_ge(kb.sem("b6e2"), u - 3)
                    s.activation(q_fa[:, su * 256:(su + 1) * 256],
                                 q_fb[:, su * 256:(su + 1) * 256],
                                 AF.Square).then_inc(kb.sem("b6a2"), 1)

                def ysq_st(u):
                    su = u % 4
                    s.wait_ge(kb.sem("b6e2"), u + 1)
                    s.dma_start(sq_i.ap()[u * 128:(u + 1) * 128, :],
                                stat[:, su:su + 1])\
                        .then_inc(kb.sem(f"b6st{su}"), 16)
                for tt in range(NTF):
                    s.wait_ge(kb.sems["b5m"], tt + 1)
                    if tt >= 2:
                        s.wait_ge(kb.sem("b5so"), tt - 1)
                    s.activation(xnt[:, (tt % 2) * 256:(tt % 2) * 256 + 256],
                                 ps[tt % 2][:, 0:256], AF.Silu)\
                        .then_inc(kb.sems["b5e"], 1)
                    s.dma_start(q_fb[:, (tt % 4) * 256:(tt % 4 + 1) * 256],
                                y_my.ap()[tt * 128:(tt + 1) * 128, :])\
                        .then_inc(kb.sem(f"b6q{tt % 4}"), 16)
                    if tt >= 2:
                        ysq_sq(tt - 2)
                    if tt >= 5:
                        ysq_st(tt - 5)
                for u in (NTF - 2, NTF - 1):
                    ysq_sq(u)
                for u in range(NTF - 5, NTF):
                    ysq_st(u)
                for sl in range(4):
                    s.wait_ge(kb.sem(f"b6st{sl}"), (NTF // 4) * 16)
            @blk.gpsimd
            def _(g):
                for tt in range(NTF):
                    g.wait_ge(kb.sems["b5e"], tt + 1)
                    g.dma_start(sg_t2[tt], xnt[:, (tt % 2) * 256:(tt % 2) * 256 + 256])\
                        .then_inc(kb.sems["b5s"], 16)
                    g.wait_ge(kb.sems["b5s"], (tt + 1) * 16)
                    g.sem_inc(kb.sem("b5so"), 1)
            @blk.vector
            def _(v):
                for u in range(NTF):
                    su = u % 4
                    v.wait_ge(kb.sem("b6a2"), u + 1)
                    if u >= 4:
                        v.wait_ge(kb.sem(f"b6st{su}"), (u // 4) * 16)
                    v.tensor_reduce(stat[:, su:su + 1],
                                    q_fa[:, su * 256:(su + 1) * 256],
                                    AX.X, OP.add)\
                        .then_inc(kb.sem("b6e2"), 1)
        kb.cnt["b5l"] = NTF * KC * 16
        kb.cnt["b5m"] = NTF
        kb.cnt["b5e"] = NTF
        kb.cnt["b5s"] = NTF * 16
        kb.cnt["b5so"] = NTF
        phase('b6')
        with nc.Block() as blk:
            @blk.gpsimd
            def _(g):
                cs, _ = kb.inc("coll", 1)
                g.collective_compute("AllReduce", OP.add,
                                     replica_groups=[[0, 1, 2, 3], [4, 5, 6, 7]],
                                     ins=[sq_i.ap().opt()],
                                     outs=[sq_o.ap().opt()]).then_inc(cs, 1)
                g.wait_ge(kb.sems["coll"], kb.cnt["coll"])
        with nc.Block() as blk:
            @blk.sync
            def _(sync):
                kb.dma(sync, q_fb[0:1, 0:256], gn_in)
                kb.wait(sync, "dml")
        with nc.Block() as blk:
            @blk.tensor
            def _(te):
                te.matmul(ps[0][:, 0:256], ones1[:], q_fb[0:1, 0:256],
                          start=True, stop=True)
        with nc.Block() as blk:
            @blk.vector
            def _(v):
                v.tensor_copy(gnbt[:, 0:256], ps[0][:, 0:256])
        for kc2 in range(2):
            with nc.Block() as blk:
                @blk.sync
                def _(sync, kc2=kc2):
                    kb.dma(sync, xt[:, 0:D],
                           wo_in.rearrange("(n p) d -> n p d", p=128)[kc2])
                    kb.wait(sync, "dml")
            with nc.Block() as blk:
                @blk.vector
                def _(v, kc2=kc2):
                    v.tensor_copy(wbig[:, 4096 + kc2 * 1024:4096 + (kc2 + 1) * 1024],
                                  xt[:, 0:D])
        phase('rstd')
        phase('b7')
        atp_t = atp_d.ap().rearrange("(n p) d -> n p d", p=128)
        kb.sem("b7l"); kb.sem("b7z"); kb.sem("b7t"); kb.sem("b7m"); kb.sem("b7e")
        kb.sem("b7s")
        with nc.Block() as blk:
            @blk.sync
            def _(sync):
                def loads7(tt):
                    pi = tt % 2
                    sync.dma_start(yrow[pi][:, 0:256],
                                   y_my.ap()[tt * 128:(tt + 1) * 128, :])\
                        .then_inc(kb.sem(f"b7l{pi}"), 16)
                    sync.dma_start(q_fd[:, pi * 256:pi * 256 + 256], sg_t2[tt])\
                        .then_inc(kb.sem(f"b7l{pi}"), 16)
                    sync.dma_start(stat[:, 4 + pi:5 + pi],
                                   sq_o.ap()[tt * 128:(tt + 1) * 128, :])\
                        .then_inc(kb.sem(f"b7l{pi}"), 16)
                loads7(0)
                for tt in range(NTF):
                    pi = tt % 2
                    if tt + 1 < NTF:
                        if tt >= 1:
                            sync.wait_ge(kb.sems["b7z"], tt)
                        loads7(tt + 1)
                    sync.wait_ge(kb.sems["b7z"], tt + 1)
                    if tt >= 2:
                        sync.wait_ge(kb.sems["b7m"], tt - 1)
                    for j in range(2):
                        sync.dma_start(xqf_r[:, 2048 + pi * 256 + j * 128:
                                             2048 + pi * 256 + (j + 1) * 128],
                                       xnb[:, pi * 256 + j * 128:
                                           pi * 256 + (j + 1) * 128],
                                       transpose=True)\
                            .then_inc(kb.sem(f"b7t{pi}"), 16)
            @blk.vector
            def _(v):
                def zstep(tt):
                    pi = tt % 2
                    v.wait_ge(kb.sem(f"b7l{pi}"), (tt // 2 + 1) * 48)
                    if tt >= 2:
                        v.wait_ge(kb.sem(f"b7t{pi}"), (tt // 2) * 32)
                    v.tensor_scalar(stat[:, 6 + pi:7 + pi],
                                    stat[:, 4 + pi:5 + pi],
                                    1.0 / D, 1e-5, OP.mult, OP.add)\
                        .then_inc(kb.sem("b7v"), 1)
                    kb.flush(v, v.tensor_tensor(q_fa[:, pi * 256:pi * 256 + 256],
                                                q_fd[:, pi * 256:pi * 256 + 256],
                                                gnbt[:, 0:256], OP.mult))
                    v.wait_ge(kb.sem("b7a"), tt + 1)
                    kb.flush(v, v.reciprocal(stat[:, 6 + pi:7 + pi],
                                             stat[:, 6 + pi:7 + pi]))
                    kb.flush(v, v.tensor_scalar(q_fb[:, pi * 256:pi * 256 + 256],
                                                yrow[pi][:, 0:256],
                                                stat[:, 6 + pi:7 + pi],
                                                None, OP.mult))
                    v.tensor_tensor(xnb[:, pi * 256:pi * 256 + 256],
                                    q_fb[:, pi * 256:pi * 256 + 256],
                                    q_fa[:, pi * 256:pi * 256 + 256], OP.mult)\
                        .then_inc(kb.sems["b7z"], 1)

                def estep(tt):
                    pi = tt % 2
                    v.wait_ge(kb.sems["b7m"], tt + 1)
                    if tt >= 2:
                        v.wait_ge(kb.sem("b7so"), tt - 1)
                    v.tensor_copy(q_fc[:, 0:1024].bitcast(BF16)
                                  [:, pi * 1024:pi * 1024 + 512],
                                  ps[4 + 2 * pi][:, 0:512])
                    v.tensor_copy(q_fc[:, 0:1024].bitcast(BF16)
                                  [:, pi * 1024 + 512:pi * 1024 + 1024],
                                  ps[5 + 2 * pi][:, 0:512])\
                        .then_inc(kb.sems["b7e"], 1)
                for tt in range(NTF):
                    zstep(tt)
                    if tt >= 1:
                        estep(tt - 1)
                estep(NTF - 1)
            @blk.tensor
            def _(te):
                for tt in range(NTF):
                    pi = tt % 2
                    te.wait_ge(kb.sem(f"b7t{pi}"), (tt // 2 + 1) * 32)
                    if tt >= 2:
                        te.wait_ge(kb.sems["b7e"], tt - 1)
                    for half in range(2):
                        for j in range(2):
                            mm = te.matmul(ps[4 + 2 * pi + half][:, 0:512],
                                           xqf_r[:, 2048 + pi * 256 + j * 128:
                                                 2048 + pi * 256 + (j + 1) * 128],
                                           wbig[:, 4096 + j * 1024 + half * 512:
                                                4096 + j * 1024 + (half + 1) * 512],
                                           start=(j == 0), stop=(j == 1))
                    mm.then_inc(kb.sems["b7m"], 1)
            @blk.scalar
            def _(s):
                for tt in range(NTF):
                    pi = tt % 2
                    s.wait_ge(kb.sem("b7v"), tt + 1)
                    s.activation(stat[:, 6 + pi:7 + pi], stat[:, 6 + pi:7 + pi],
                                 AF.Sqrt).then_inc(kb.sem("b7a"), 1)
            @blk.gpsimd
            def _(g):
                for tt in range(NTF):
                    pi = tt % 2
                    g.wait_ge(kb.sems["b7e"], tt + 1)
                    g.dma_start(atp_t[tt],
                                q_fc[:, 0:1024].bitcast(BF16)
                                [:, pi * 1024:pi * 1024 + 1024])\
                        .then_inc(kb.sems["b7s"], 16)
                    g.wait_ge(kb.sems["b7s"], (tt + 1) * 16)
                    g.sem_inc(kb.sem("b7so"), 1)
        kb.cnt["b7l"] = NTF * 48
        kb.cnt["b7z"] = NTF
        kb.cnt["b7t"] = NTF * 32
        kb.cnt["b7m"] = NTF
        kb.cnt["b7e"] = NTF
        kb.cnt["b7s"] = NTF * 16
        kb.cnt["b7so"] = NTF
        kb.cnt["b7v"] = NTF
        kb.cnt["b7a"] = NTF
        phase('rs')
        with nc.Block() as blk:
            @blk.gpsimd
            def _(g):
                cs, _ = kb.inc("coll", 1)
                g.collective_compute("ReduceScatter", OP.add,
                                     replica_groups=[[0, 1, 2, 3], [4, 5, 6, 7]],
                                     ins=[atp_d.ap().opt()],
                                     outs=[atr_d.ap().opt()]).then_inc(cs, 1)
                g.wait_ge(kb.sems["coll"], kb.cnt["coll"])
        atr_t = atr_d.ap().rearrange("(n p) d -> n p d", p=128)
        phase('fin')
        for it in range(NT):
            with nc.Block() as blk:
                @blk.sync
                def _(sync, it=it):
                    kb.dma(sync, xt[:, 0:D], x_t[it], name="a5x")
                    kb.dma(sync, yrow[1][:], ffn_t[it], name="a5x")
                    kb.dma(sync, xnb[:, 0:D], atr_t[it], name="a5x")
                    kb.wait(sync, "a5x")
            with nc.Block() as blk:
                @blk.vector
                def _(v):
                    v.tensor_copy(q_fa[:, 0:D], xnb[:, 0:D])
                    v.tensor_tensor(xnt[:], xt[:, 0:D], yrow[1][:], OP.add)
                    v.tensor_tensor(xnt[:], xnt[:], q_fa[:, 0:D], OP.add)
            with nc.Block() as blk:
                @blk.sync
                def _(sync, it=it):
                    kb.dma(sync, out_t[it], xnt[:], name="dst")
                    kb.wait(sync, "dst")

    else:
        for it in range(NT):
            with nc.Block() as blk:
                @blk.sync
                def _(sync, it=it):
                    kb.dma(sync, xt[:, 0:D], x_t[it])
                    kb.dma(sync, yrow[0][:], ffn_t[it])
                    kb.wait(sync, "dml")
            with nc.Block() as blk:
                @blk.vector
                def _(v):
                    v.tensor_tensor(xnt[:], xt[:, 0:D], yrow[0][:], OP.add)
            with nc.Block() as blk:
                @blk.sync
                def _(sync, it=it):
                    kb.dma(sync, out_t[it], xnt[:], name="dst")
                    kb.wait(sync, "dst")

    phase(None)
    if DEBUG:
        with nc.Block() as blk:
            @blk.sync
            def _(sync):
                kb.dma(sync, dbg_ffn, ffn_d.ap(), name="dbg2")
                kb.dma(sync, dbg_sg, sg_d.ap(), name="dbg2")
                kb.dma(sync, dbg_atr, atr_d.ap(), name="dbg2")
                kb.dma(sync, dbg_hq, hqT_d.ap(), name="dbg2")
                kb.dma(sync, dbg_sq, sq_o.ap(), name="dbg2")
                kb.dma(sync, dbg_atp, atp_d.ap(), name="dbg2")
                kb.wait(sync, "dbg2")

    return kb


def make_consts():
    i = np.arange(128)
    hhp = i // 64
    tp = i % 64
    low = ((hhp[:, None] == hhp[None, :]) & (tp[None, :] < tp[:, None])).astype(np.float32)
    upp = ((hhp[:, None] == hhp[None, :]) & (tp[None, :] > tp[:, None])).astype(np.float32)
    upi = ((hhp[:, None] == hhp[None, :]) & (tp[None, :] >= tp[:, None])).astype(np.float32)
    return low, upp, upi


def kernel(**inputs):
    x = np.ascontiguousarray(np.asarray(inputs["L0_attn_x"], np.float32))
    Wq = np.asarray(inputs["L0_attn_Wq"], np.float32)
    Wk = np.asarray(inputs["L0_attn_Wk"], np.float32)
    Wv = np.asarray(inputs["L0_attn_Wv"], np.float32)
    Wa = np.asarray(inputs["L0_attn_Wa"], np.float32)
    ba = np.asarray(inputs["L0_attn_ba"], np.float32)
    Wb = np.asarray(inputs["L0_attn_Wb"], np.float32)
    bb = np.asarray(inputs["L0_attn_bb"], np.float32)
    Wg = np.ascontiguousarray(np.asarray(inputs["L0_attn_Wg"], np.float32))
    Wo = np.ascontiguousarray(np.asarray(inputs["L0_attn_Wo"], np.float32))
    gn = np.asarray(inputs["L0_attn_g_norm"], np.float32)
    wgate = np.asarray(inputs["L0_ffn_w_gate"], np.float32)
    wup = np.asarray(inputs["L0_ffn_w_up"], np.float32)
    wdown = np.asarray(inputs["L0_ffn_w_down"], np.float32)
    low, upp, upi = make_consts()
    trj = np.tril(np.ones((64, 64), np.float32)).T.copy()
    tri2 = np.ascontiguousarray(np.concatenate([trj, trj], 0))

    kb = build()
    in_maps = []
    for c in range(8):
        b, qq = c // 4, c % 4
        hc = slice(qq * 256, (qq + 1) * 256)
        wkvqab = np.concatenate([Wv[:, hc], Wk[:, hc], Wq[:, hc],
                                 Wa[:, 4 * qq:4 * qq + 4], Wb[:, 4 * qq:4 * qq + 4]], 1)
        bab = np.concatenate([ba[4 * qq:4 * qq + 4], bb[4 * qq:4 * qq + 4]])[None, :]
        in_maps.append({
            "x": np.ascontiguousarray(x[b, qq * T:(qq + 1) * T]),
            "wkvqab": np.ascontiguousarray(wkvqab),
            "bab": np.ascontiguousarray(bab),
            "wgm": np.ascontiguousarray(Wg[:, hc]),
            "wom": np.ascontiguousarray(Wo[qq * 256:(qq + 1) * 256, :]),
            "gnm": np.ascontiguousarray(gn[None, qq * 256:(qq + 1) * 256]),
            "wgate": np.ascontiguousarray(wgate[c * (I3 // 8):(c + 1) * (I3 // 8)]),
            "wup": np.ascontiguousarray(wup[c * (I3 // 8):(c + 1) * (I3 // 8)]),
            "wdown": np.ascontiguousarray(wdown[c * (D // 8):(c + 1) * (D // 8)]),
            "mlow": low, "mupp": upp, "mupi": upi, "tri": tri2,
        })
    res = run_bass_kernel_spmd(kb.nc, in_maps, core_ids=list(range(8)))
    out = np.zeros((2, S, D), np.float32)
    for c in range(8):
        b, qq = c // 4, c % 4
        out[b, qq * T:(qq + 1) * T] = res.results[c]["out"]
    return out




# version 14
# speedup vs baseline: 1.1888x; 1.0329x over previous
import numpy as np
import concourse.bass as bass
import concourse.mybir as mybir
from concourse.bass_utils import run_bass_kernel_spmd

F32 = mybir.dt.float32
BF16 = mybir.dt.bfloat16
U8 = mybir.dt.uint8
U32 = mybir.dt.uint32
AF = mybir.ActivationFunctionType
OP = mybir.AluOpType
AX = mybir.AxisListType

S, D = 8192, 1024
T = 2048
NT = T // 128
NTF = S // 128
KC = D // 128
I3 = 2816
ISL = 352
NIS = I3 // ISL
NI128 = I3 // 128
DH = 64
C = 64
NCH = S // C
CLIP = 60.0

ATTN = True
DEBUG = False


class KB:
    def __init__(self):
        self.nc = bass.Bass("TRN2", target_bir_lowering=False)
        self.sems = {}
        self.cnt = {}

    def sem(self, name):
        if name not in self.sems:
            self.sems[name] = self.nc.alloc_semaphore(name)
            self.cnt[name] = 0
        return self.sems[name]

    def val(self, name):
        return self.cnt.get(name, 0)

    def inc(self, name, by=16):
        self.sem(name)
        self.cnt[name] += by
        return self.sems[name], self.cnt[name]

    def dma(self, eng, out, in_, name="dml", transpose=False):
        s, _ = self.inc(name)
        eng.dma_start(out, in_, transpose=transpose).then_inc(s, 16)

    def wait(self, eng, name, val=None):
        if name in self.sems:
            eng.wait_ge(self.sems[name], self.cnt[name] if val is None else val)

    def flush(self, eng, inst, name="esync"):
        s, n = self.inc(name, 1)
        inst.then_inc(s, 1)
        eng.wait_ge(s, n)


def build():
    kb = KB()
    nc = kb.nc
    _scope = [None, 0]

    def phase(name):
        if _scope[0] is not None:
            nc.leave_named_scope(_scope[0], _scope[1], False)
            _scope[0] = None
        if name is not None:
            _scope[1], _ = nc.enter_named_scope(name, False)
            _scope[0] = name

    x_in = nc.dram_tensor("x", [T, D], F32, kind="ExternalInput").ap()
    wkvqab_in = nc.dram_tensor("wkvqab", [D, 776], F32, kind="ExternalInput").ap()
    bab_in = nc.dram_tensor("bab", [1, 8], F32, kind="ExternalInput").ap()
    wg_in = nc.dram_tensor("wgm", [D, 256], F32, kind="ExternalInput").ap()
    wo_in = nc.dram_tensor("wom", [256, D], F32, kind="ExternalInput").ap()
    gn_in = nc.dram_tensor("gnm", [1, 256], F32, kind="ExternalInput").ap()
    wgate_in = nc.dram_tensor("wgate", [I3 // 8, D], F32, kind="ExternalInput").ap()
    wup_in = nc.dram_tensor("wup", [I3 // 8, D], F32, kind="ExternalInput").ap()
    wdown_in = nc.dram_tensor("wdown", [D // 8, I3], F32, kind="ExternalInput").ap()
    mlow_in = nc.dram_tensor("mlow", [128, 128], F32, kind="ExternalInput").ap()
    tri_in = nc.dram_tensor("tri", [128, 64], F32, kind="ExternalInput").ap()
    mupp_in = nc.dram_tensor("mupp", [128, 128], F32, kind="ExternalInput").ap()
    mupi_in = nc.dram_tensor("mupi", [128, 128], F32, kind="ExternalInput").ap()
    out_ext = nc.dram_tensor("out", [T, D], F32, kind="ExternalOutput").ap()
    if DEBUG:
        dbg_kt = nc.dram_tensor("dbg_kt", [S, 256], BF16, kind="ExternalOutput").ap()
        dbg_qt = nc.dram_tensor("dbg_qt", [S, 256], BF16, kind="ExternalOutput").ap()
        dbg_vt = nc.dram_tensor("dbg_vt", [S, 256], BF16, kind="ExternalOutput").ap()
        dbg_sc = nc.dram_tensor("dbg_sc", [S, 16], F32, kind="ExternalOutput").ap()
        dbg_stk = nc.dram_tensor("dbg_stk", [128, 2 * 6 * NCH], F32,
                                 kind="ExternalOutput").ap()
        dbg_y = nc.dram_tensor("dbg_y", [S, 256], F32, kind="ExternalOutput").ap()
        dbg_kf = nc.dram_tensor("dbg_kf", [256, S], BF16, kind="ExternalOutput").ap()
        dbg_kw = nc.dram_tensor("dbg_kw", [128, 4096], BF16, kind="ExternalOutput").ap()
        dbg_ffn = nc.dram_tensor("dbg_ffn", [T, D], F32, kind="ExternalOutput").ap()
        dbg_sg = nc.dram_tensor("dbg_sg", [S, 256], F32, kind="ExternalOutput").ap()
        dbg_rstd = nc.dram_tensor("dbg_rstd", [S, 1], F32, kind="ExternalOutput").ap()
        dbg_atr = nc.dram_tensor("dbg_atr", [T, D], BF16, kind="ExternalOutput").ap()
        dbg_hq = nc.dram_tensor("dbg_hq", [I3, T], BF16, kind="ExternalOutput").ap()
        dbg_sq = nc.dram_tensor("dbg_sq", [S, 1], F32, kind="ExternalOutput").ap()
        dbg_atp = nc.dram_tensor("dbg_atp", [S, D], BF16, kind="ExternalOutput").ap()

    dt = nc.dram_tensor
    ag_xnf_i = dt("ag_xnf_i", [D, T], BF16)
    ag_xnf_o = dt("ag_xnf_o", [4 * D, T], BF16)
    wqg_i = dt("wqg_i", [I3 // 8, D], BF16)
    wqu_i = dt("wqu_i", [I3 // 8, D], BF16)
    wqd_i = dt("wqd_i", [D // 8, I3], BF16)
    wqg_o = dt("wqg_o", [I3, D], BF16)
    wqu_o = dt("wqu_o", [I3, D], BF16)
    wqd_o = dt("wqd_o", [D, I3], BF16)
    hqT_d = dt("hqT_d", [I3, T], BF16)
    ffn_d = dt("ffn_d", [T, D], F32)
    sg_d = dt("sg_d", [S, 256], F32)
    sq_i = dt("sq_i", [S, 1], F32)
    sq_o = dt("sq_o", [S, 1], F32)
    atp_d = dt("atp_d", [S, D], BF16)
    atr_d = dt("atr_d", [T, D], BF16)
    kt_d = dt("kt_d", [S, 256], BF16)
    qt_d = dt("qt_d", [S, 256], BF16)
    vt_d = dt("vt_d", [S, 256], BF16)
    kf_d = dt("kf_d", [256, S], BF16)
    kw_d = dt("kw_d", [256, S], BF16)
    kr_d = dt("kr_d", [256, S], BF16)
    qf_d = dt("qf_d", [256, S], BF16)
    qm_d = dt("qm_d", [256, S], BF16)
    scal_d = dt("scal_d", [S, 16], F32)
    scal2_d = dt("scal2_d", [S, 32], F32)
    gct_d = dt("gct_d", [128, 8], F32)
    y_my = dt("y_my", [S, 256], F32)


    A = nc.alloc_sbuf_tensor
    xt = A("xt", [128, I3], F32)
    xnt = A("xnt", [128, D], F32)
    xnb = A("xnb", [128, D], BF16)
    stat = A("stat", [128, 8], F32)
    q_fa = A("q_fa", [128, 1024], F32)
    q_fb = A("q_fb", [128, 1024], F32)
    q_fc = A("q_fc", [128, 1024], F32)
    q_fd = A("q_fd", [128, 1024], F32)
    q_fs = A("q_fs", [128, 1024], F32)
    q_m8 = A("q_m8", [128, 1024], U8)
    q_s1 = A("q_s1", [128, 64], F32)
    q_s2 = A("q_s2", [128, 64], F32)
    q_s3 = A("q_s3", [128, 64], F32)
    qb = A("qb", [128, 1024], BF16)
    tp1 = A("tp1", [128, 128], BF16)
    tp2 = A("tp2", [128, 128], BF16)
    xqf_r = A("xqf_r", [128, KC * T], BF16)
    wgu = A("wgu", [128, KC * 2 * ISL], BF16)
    wdq = A("wdq", [128, NI128 * 1024], BF16)
    wbig = A("wbig", [128, KC * 1024], BF16)
    hqf_t = A("hqf", [128, 2 * 128], BF16)
    hqf = [hqf_t[:, 0:128], hqf_t[:, 128:256]]
    gu_s = A("gu_s", [128, 2 * ISL], F32)
    hfa = A("hfa", [128, 2 * ISL], F32)
    yrow_t = A("yrow", [128, 2 * D], F32)
    yrow = [yrow_t[:, 0:D], yrow_t[:, D:2 * D]]
    ps = [nc.alloc_psum_tensor(f"ps{i}", [128, 512], F32) for i in range(8)]
    if ATTN:
        wkv_bf = A("wkv_bf", [128, KC * 776], BF16)
        bab_bf = A("bab_bf", [1, 8], BF16)
        ktile = A("ktile", [128, 776], BF16)
        abt = A("abt", [128, 8], F32)
        sctile = A("sctile", [128, 16], F32)
        vart = A("vart", [128, 768], BF16)
        rawh = A("rawh", [128, 1024], F32)
        der = A("der", [128, 4 * 8 * 64], F32)
        tmp8 = A("tmp8", [128, 8 * 64], F32)
        rowm = A("rowm", [1, 256], F32)
        tri64 = A("tri64", [128, 64], F32)
        ones1 = A("ones1", [1, 128], F32)
        mlow = A("mlows", [128, 128], F32)
        mupp = A("mupps", [128, 128], F32)
        mupi = A("mupis", [128, 128], F32)
        onesb = A("onesb", [1, 128], BF16)
        trow = A("trow", [128, 10 * 128], BF16)
        stk = A("stk", [128, 2 * 6 * NCH], F32)
        kfs = A("kfs", [128, 2 * 5 * C], BF16)
        kts = A("kts", [128, 2 * 128], BF16)
        vts = A("vts", [128, 2 * C], BF16)
        Lb = A("Lb", [128, 128], BF16)
        LTb = A("LTb", [128, 128], BF16)
        MTb = A("MTb", [128, 128], BF16)
        Pw = A("Pw", [128, 2 * 128], BF16)
        PwT = A("PwT", [128, 2 * 128], BF16)
        Yw = A("Yw", [128, 2 * C], BF16)
        Usb = A("Usb", [128, C], BF16)
        S0 = A("S0", [128, 4 * C], BF16)
        evs = A("evs", [128, 128], F32)
        evs2 = A("evs2", [128, 128], F32)
        ytile = A("ytile", [128, 128], F32)
        n2sq = q_fa

        gnbt = gu_s
        sgt2 = q_fd

    x_t = x_in.rearrange("(n p) d -> n p d", p=128)
    xnf_stage = qb[:, 0:KC * 128]
    xnf_stage2 = hfa.bitcast(BF16)[:, 0:KC * 128]
    ffn_t = ffn_d.ap().rearrange("(n p) d -> n p d", p=128)
    sg_t = sg_d.ap().rearrange("(n p) d -> n p d", p=128)
    out_t = out_ext.rearrange("(n p) d -> n p d", p=128)

    def q_sign(s, src_ap, rows, cols):
        s.activation(q_fs[0:rows, 0:cols], src_ap, AF.Sign)

    retq = [None]

    def q_main_nosign(v, src_ap, out_ap, rows, cols):
        q_main(v, src_ap, out_ap, rows, cols)

    def q_main(v, src_ap, out_ap, rows, cols):
        nb = cols // 16
        a3 = src_ap.rearrange("p (g k) -> p g k", k=16)
        kb.flush(v, v.tensor_reduce(q_s1[0:rows, 0:nb], a3, AX.X, OP.max,
                                    apply_absolute_value=True))
        kb.flush(v, v.tensor_scalar(q_s2[0:rows, 0:nb], q_s1[0:rows, 0:nb],
                                    1.0 / 6.0, 1e-12, OP.mult, OP.max))
        kb.flush(v, v.reciprocal(q_s3[0:rows, 0:nb], q_s2[0:rows, 0:nb]))
        v.tensor_scalar(q_fa[0:rows, 0:cols].bitcast(U32), src_ap.bitcast(U32),
                        0x7FFFFFFF, None, OP.bitwise_and)
        v.tensor_tensor(q_fb[0:rows, 0:cols].rearrange("p (g k) -> p g k", k=16),
                        q_fa[0:rows, 0:cols].rearrange("p (g k) -> p g k", k=16),
                        q_s3[0:rows, 0:nb].unsqueeze(2).broadcast_to([rows, nb, 16]),
                        OP.mult)
        v.tensor_scalar(q_fa[0:rows, 0:cols].bitcast(U32),
                        q_fb[0:rows, 0:cols].bitcast(U32), 0x001FFFFF, None, OP.add)
        v.tensor_scalar(q_fc[0:rows, 0:cols].bitcast(U32),
                        q_fa[0:rows, 0:cols].bitcast(U32), 0xFFC00000, None,
                        OP.bitwise_and)
        v.tensor_scalar(q_fa[0:rows, 0:cols].bitcast(U32),
                        q_fb[0:rows, 0:cols].bitcast(U32), 0x003FFFFF, None, OP.add)
        v.tensor_scalar(q_fd[0:rows, 0:cols].bitcast(U32),
                        q_fa[0:rows, 0:cols].bitcast(U32), 0xFF800000, None,
                        OP.bitwise_and)
        v.tensor_scalar(q_fa[0:rows, 0:cols], q_fd[0:rows, 0:cols], 0.5, None, OP.max)
        v.tensor_scalar(q_m8[0:rows, 0:cols], q_fb[0:rows, 0:cols], 1.0, None, OP.is_lt)
        v.select(q_fd[0:rows, 0:cols], q_m8[0:rows, 0:cols],
                 q_fa[0:rows, 0:cols], q_fc[0:rows, 0:cols], add_drain=True)
        v.tensor_scalar(q_fa[0:rows, 0:cols], q_fb[0:rows, 0:cols], 0.25, None, OP.is_gt)
        v.tensor_tensor(q_fb[0:rows, 0:cols], q_fd[0:rows, 0:cols],
                        q_fa[0:rows, 0:cols], OP.mult)
        v.tensor_tensor(q_fa[0:rows, 0:cols], q_fb[0:rows, 0:cols],
                        q_fs[0:rows, 0:cols], OP.mult)
        retq[0] = v.tensor_tensor(
            out_ap.rearrange("p (g k) -> p g k", k=16),
            q_fa[0:rows, 0:cols].rearrange("p (g k) -> p g k", k=16),
            q_s2[0:rows, 0:nb].unsqueeze(2).broadcast_to([rows, nb, 16]),
            OP.mult)

    phase('p1')
    for it in range(NT):
        with nc.Block() as blk:
            @blk.sync
            def _(sync, it=it):
                kb.dma(sync, xt[:, 0:D], x_t[it])
                kb.wait(sync, "dml")
        with nc.Block() as blk:
            @blk.vector
            def _(v):
                kb.flush(v, v.tensor_reduce(stat[:, 0:1], xt[:, 0:D], AX.X, OP.add))
                kb.flush(v, v.tensor_scalar(stat[:, 0:1], stat[:, 0:1], 1.0 / D,
                                            None, OP.mult))
            @blk.scalar
            def _(s):
                s.activation(q_fa[:, 0:D], xt[:, 0:D], AF.Square)
        with nc.Block() as blk:
            @blk.vector
            def _(v):
                kb.flush(v, v.tensor_reduce(stat[:, 1:2], q_fa[:, 0:D], AX.X, OP.add))
                kb.flush(v, v.tensor_scalar(stat[:, 1:2], stat[:, 1:2], 1.0 / D,
                                            None, OP.mult))
                kb.flush(v, v.tensor_tensor(stat[:, 2:3], stat[:, 0:1], stat[:, 0:1],
                                            OP.mult))
                kb.flush(v, v.tensor_tensor(stat[:, 3:4], stat[:, 1:2], stat[:, 2:3],
                                            OP.subtract))
                kb.flush(v, v.tensor_scalar(stat[:, 3:4], stat[:, 3:4], 1e-5,
                                            None, OP.add))
        with nc.Block() as blk:
            @blk.scalar
            def _(s):
                s.activation(stat[:, 4:5], stat[:, 3:4], AF.Sqrt)
        with nc.Block() as blk:
            @blk.vector
            def _(v):
                kb.flush(v, v.reciprocal(stat[:, 5:6], stat[:, 4:5]))
                kb.flush(v, v.tensor_tensor(stat[:, 6:7], stat[:, 0:1], stat[:, 5:6],
                                            OP.mult))
                kb.flush(v, v.tensor_scalar(stat[:, 6:7], stat[:, 6:7], -1.0,
                                            None, OP.mult))
        with nc.Block() as blk:
            @blk.vector
            def _(v):
                kb.flush(v, v.tensor_scalar(stat[:, 7:8], stat[:, 0:1], -1.0,
                                            None, OP.mult))
                v.tensor_scalar(xnt[:], xt[:, 0:D], stat[:, 7:8], stat[:, 5:6],
                                OP.add, OP.mult)
        with nc.Block() as blk:
            @blk.vector
            def _(v):
                v.tensor_copy(xnb[:], xnt[:])
            @blk.scalar
            def _(s):
                q_sign(s, xnt[:], 128, D)
        with nc.Block() as blk:
            @blk.vector
            def _(v):
                q_main(v, xnt[:], qb[:, 0:D], 128, D)
        with nc.Block() as blk:
            @blk.sync
            def _(sync, it=it):
                for kc in range(KC):
                    kb.dma(sync, tp1[:], xnb[:, kc * 128:(kc + 1) * 128],
                           name="dtr", transpose=True)
                    kb.wait(sync, "dtr")
                    kb.dma(sync, ag_xnf_i.ap()[kc * 128:(kc + 1) * 128,
                                               it * 128:(it + 1) * 128], tp1[:])
                    kb.wait(sync, "dml")
                    kb.dma(sync, xqf_r[:, kc * T + it * 128:kc * T + (it + 1) * 128],
                           qb[:, kc * 128:(kc + 1) * 128], name="dtr", transpose=True)
                    kb.wait(sync, "dtr")

    phase('wq')
    with nc.Block() as blk:
        @blk.gpsimd
        def _(g):
            cs, _ = kb.inc("coll", 1)
            g.collective_compute("AllGather", OP.bypass,
                                 replica_groups=[[0, 1, 2, 3], [4, 5, 6, 7]],
                                 ins=[ag_xnf_i.ap().opt()],
                                 outs=[ag_xnf_o.ap().opt()]).then_inc(cs, 1)

    for (win, wout, rows, cols) in ((wgate_in, wqg_i, I3 // 8, D),
                                    (wup_in, wqu_i, I3 // 8, D),
                                    (wdown_in, wqd_i, D // 8, I3)):
        nrt = (rows + 127) // 128
        ncs = (cols + 1023) // 1024
        for rt in range(nrt):
            r0 = rt * 128
            pr = min(128, rows - r0)
            with nc.Block() as blk:
                @blk.sync
                def _(sync, win=win, r0=r0, pr=pr, cols=cols):
                    kb.dma(sync, xt[0:pr, 0:cols], win[r0:r0 + pr, :])
                    kb.wait(sync, "dml")
            for csl in range(ncs):
                c0 = csl * 1024
                pc = min(1024, cols - c0)
                with nc.Block() as blk:
                    @blk.scalar
                    def _(s, r0=r0, pr=pr, c0=c0, pc=pc):
                        q_sign(s, xt[0:pr, c0:c0 + pc], pr, pc)
                with nc.Block() as blk:
                    @blk.vector
                    def _(v, r0=r0, pr=pr, c0=c0, pc=pc):
                        q_main(v, xt[0:pr, c0:c0 + pc], qb[0:pr, 0:pc], pr, pc)
                with nc.Block() as blk:
                    @blk.sync
                    def _(sync, wout=wout, r0=r0, pr=pr, c0=c0, pc=pc):
                        kb.dma(sync, wout.ap()[r0:r0 + pr, c0:c0 + pc], qb[0:pr, 0:pc])
                        kb.wait(sync, "dml")
    with nc.Block() as blk:
        @blk.gpsimd
        def _(g):
            for (i_, o_) in ((wqg_i, wqg_o), (wqu_i, wqu_o), (wqd_i, wqd_o)):
                cs, _ = kb.inc("coll", 1)
                g.collective_compute("AllGather", OP.bypass,
                                     replica_groups=[list(range(8))],
                                     ins=[i_.ap().opt()],
                                     outs=[o_.ap().opt()]).then_inc(cs, 1)
            g.wait_ge(kb.sems["coll"], kb.cnt["coll"])

    phase('f1')
    with nc.Block() as blk:
        @blk.gpsimd
        def _(g):
            g.wait_ge(kb.sems["coll"], kb.cnt["coll"])

    kb.sem("f1m"); kb.sem("f1a"); kb.sem("f1h"); kb.sem("f1g"); kb.sem("f1q")
    kb.sem("f1s")
    for isl in range(NIS):
        i0 = isl * ISL
        with nc.Block() as blk:
            @blk.sync
            def _(sync, i0=i0):
                for kc in range(KC):
                    for (srcw, off) in ((wqg_o, 0), (wqu_o, ISL)):
                        for j in range(3):
                            w0 = j * 128
                            wN = min(128, ISL - w0)
                            kb.dma(sync, wgu[:, kc * 2 * ISL + off + w0:
                                            kc * 2 * ISL + off + w0 + wN],
                                   srcw.ap()[i0 + w0:i0 + w0 + wN,
                                             kc * 128:(kc + 1) * 128],
                                   name="p4w", transpose=True)
                kb.wait(sync, "p4w")
        kb.sem("f1so"); kb.sem("f1t0"); kb.sem("f1t1"); kb.sem("f1t2")
        c0 = {k: kb.cnt[k] for k in ("f1m", "f1a", "f1h", "f1g", "f1q", "f1s",
                                     "f1so", "f1t0", "f1t1", "f1t2")}
        with nc.Block() as blk:
            @blk.tensor
            def _(te, isl=isl, c0=c0):
                for it in range(NT):
                    pi = it % 2
                    if it >= 2:
                        te.wait_ge(kb.sems["f1h"], c0["f1h"] + it - 1)
                    for kc in range(KC):
                        lhs = xqf_r[:, kc * T + it * 128:kc * T + (it + 1) * 128]
                        te.matmul(ps[pi][:, 0:ISL], lhs,
                                  wgu[:, kc * 2 * ISL:kc * 2 * ISL + ISL],
                                  start=(kc == 0), stop=(kc == KC - 1))
                        mm = te.matmul(ps[2 + pi][:, 0:ISL], lhs,
                                       wgu[:, kc * 2 * ISL + ISL:
                                           kc * 2 * ISL + 2 * ISL],
                                       start=(kc == 0), stop=(kc == KC - 1))
                    mm.then_inc(kb.sems["f1m"], 1)
            @blk.scalar
            def _(s, isl=isl, c0=c0):
                for it in range(NT):
                    pi = it % 2
                    s.wait_ge(kb.sems["f1m"], c0["f1m"] + it + 1)
                    if it >= 2:
                        s.wait_ge(kb.sems["f1h"], c0["f1h"] + it - 1)
                    s.activation(gu_s[:, pi * ISL:(pi + 1) * ISL], ps[pi][:, 0:ISL],
                                 AF.Silu).then_inc(kb.sems["f1a"], 1)
                    s.wait_ge(kb.sems["f1h"], c0["f1h"] + it + 1)
                    s.activation(q_fs[:, 0:ISL], hfa[:, pi * ISL:(pi + 1) * ISL],
                                 AF.Sign).then_inc(kb.sems["f1g"], 1)
            @blk.vector
            def _(v, isl=isl, c0=c0):
                for it in range(NT):
                    pi = it % 2
                    v.wait_ge(kb.sems["f1a"], c0["f1a"] + it + 1)
                    v.tensor_tensor(hfa[:, pi * ISL:(pi + 1) * ISL],
                                    gu_s[:, pi * ISL:(pi + 1) * ISL],
                                    ps[2 + pi][:, 0:ISL], OP.mult)\
                        .then_inc(kb.sems["f1h"], 1)
                    v.wait_ge(kb.sems["f1g"], c0["f1g"] + it + 1)
                    if it >= 2:
                        v.wait_ge(kb.sem("f1so"), c0["f1so"] + it - 1)
                    q_main_nosign(v, hfa[:, pi * ISL:(pi + 1) * ISL],
                                  qb[:, pi * 512:pi * 512 + ISL], 128, ISL)
                    retq[0].then_inc(kb.sems["f1q"], 1)
            @blk.sync
            def _(sync, isl=isl, c0=c0):
                for it in range(NT):
                    pi = it % 2
                    st = it % 3
                    sync.wait_ge(kb.sems["f1q"], c0["f1q"] + it + 1)
                    if it >= 3:
                        sync.wait_ge(kb.sem("f1so"), c0["f1so"] + it - 2)
                    for c in range(3):
                        sync.dma_start(
                            trow[:, st * 384 + c * 128:st * 384 + (c + 1) * 128],
                            qb[:, pi * 512 + c * 128:pi * 512 + (c + 1) * 128],
                            transpose=True)\
                            .then_inc(kb.sem(f"f1t{st}"), 16)
            @blk.gpsimd
            def _(g, isl=isl, i0=i0, c0=c0):
                for it in range(NT):
                    st = it % 3
                    g.wait_ge(kb.sem(f"f1t{st}"), c0[f"f1t{st}"] + (it // 3 + 1) * 48)
                    for c in range(3):
                        rr = 96 if c == 2 else 128
                        g.dma_start(hqT_d.ap()[i0 + c * 128:i0 + c * 128 + rr,
                                               it * 128:(it + 1) * 128],
                                    trow[0:rr, st * 384 + c * 128:
                                         st * 384 + c * 128 + 128])\
                            .then_inc(kb.sems["f1s"], 16)
                    g.wait_ge(kb.sems["f1s"], c0["f1s"] + (it + 1) * 48)
                    g.sem_inc(kb.sem("f1so"), 1)
        nt3 = [(NT + 2 - i) // 3 for i in range(3)]
        for k, d in (("f1m", NT), ("f1a", NT), ("f1h", NT), ("f1g", NT),
                     ("f1q", NT), ("f1s", NT * 48), ("f1so", NT),
                     ("f1t0", nt3[0] * 48), ("f1t1", nt3[1] * 48),
                     ("f1t2", nt3[2] * 48)):
            kb.cnt[k] = c0[k] + d

    phase('p5')
    with nc.Block() as blk:
        @blk.sync
        def _(sync):
            for kc2 in range(NI128):
                for j in range(KC):
                    kb.dma(sync, wdq[:, kc2 * 1024 + j * 128:kc2 * 1024 + (j + 1) * 128],
                           wqd_o.ap()[j * 128:(j + 1) * 128,
                                      kc2 * 128:(kc2 + 1) * 128],
                           name="p5w", transpose=True)
            kb.wait(sync, "p5w")

    kb.sem("p5mm"); kb.sem("p5ev"); kb.sem("p5st")
    with nc.Block() as blk:
        @blk.sync
        def _(sync):
            for it in range(NT):
                for kc2 in range(NI128):
                    i = it * NI128 + kc2
                    if i >= 4:
                        kb.wait(sync, "p5mm", (i - 3))
                    sync.dma_start(qb[:, (i % 4) * 128:(i % 4 + 1) * 128],
                                   hqT_d.ap()[kc2 * 128:(kc2 + 1) * 128,
                                              it * 128:(it + 1) * 128])\
                        .then_inc(kb.sem(f"p5l{i % 4}"), 16)
        @blk.tensor
        def _(te):
            for it in range(NT):
                for kc2 in range(NI128):
                    i = it * NI128 + kc2
                    te.wait_ge(kb.sem(f"p5l{i % 4}"), 16 * (i // 4 + 1))
                    lhs = qb[:, (i % 4) * 128:(i % 4 + 1) * 128]
                    te.matmul(ps[2][:, 0:512], lhs, wdq[:, kc2 * 1024:kc2 * 1024 + 512],
                              start=(kc2 == 0), stop=(kc2 == NI128 - 1))
                    te.matmul(ps[3][:, 0:512], lhs,
                              wdq[:, kc2 * 1024 + 512:(kc2 + 1) * 1024],
                              start=(kc2 == 0), stop=(kc2 == NI128 - 1))\
                        .then_inc(kb.sems["p5mm"], 1)
                te.wait_ge(kb.sems["p5ev"], it + 1)
        @blk.vector
        def _(v):
            for it in range(NT):
                v.wait_ge(kb.sems["p5mm"], (it + 1) * NI128)
                if it >= 2:
                    v.wait_ge(kb.sem(f"p5st{it % 2}"), 16 * (it // 2))
                v.tensor_copy(yrow[it % 2][:, 0:512], ps[2][:, 0:512])
                v.tensor_copy(yrow[it % 2][:, 512:1024], ps[3][:, 0:512])\
                    .then_inc(kb.sems["p5ev"], 1)
        @blk.scalar
        def _(s):
            for it in range(NT):
                s.wait_ge(kb.sems["p5ev"], it + 1)
                s.dma_start(ffn_t[it], yrow[it % 2][:])\
                    .then_inc(kb.sem(f"p5st{it % 2}"), 16)
            s.wait_ge(kb.sem("p5st0"), (NT // 2) * 16)
            s.wait_ge(kb.sem("p5st1"), (NT // 2) * 16)
    kb.cnt["p5mm"] = NT * NI128
    kb.cnt["p5ev"] = NT
    kb.cnt["p5st"] = NT * 16

    if ATTN:
        phase('a0')
        with nc.Block() as blk:
            @blk.sync
            def _(sync):
                kb.dma(sync, mlow[:], mlow_in)
                kb.dma(sync, mupp[:], mupp_in)
                kb.dma(sync, mupi[:], mupi_in)
                kb.wait(sync, "dml")
            @blk.vector
            def _(v):
                v.memset(onesb[:], 1.0)
                v.memset(S0[:], 0.0)
        for kc in range(KC):
            with nc.Block() as blk:
                @blk.sync
                def _(sync, kc=kc):
                    kb.dma(sync, xt[:, 0:776],
                           wkvqab_in.rearrange("(n p) d -> n p d", p=128)[kc])
                    kb.wait(sync, "dml")
            with nc.Block() as blk:
                @blk.vector
                def _(v, kc=kc):
                    v.tensor_copy(wkv_bf[:, kc * 776:(kc + 1) * 776], xt[:, 0:776])
        with nc.Block() as blk:
            @blk.sync
            def _(sync):
                kb.dma(sync, stat[0:1, 0:8], bab_in)
                kb.wait(sync, "dml")
        with nc.Block() as blk:
            @blk.vector
            def _(v):
                v.tensor_copy(bab_bf[:], stat[0:1, 0:8])

        phase('a1')
        kb.sem("a1l"); kb.sem("a1m"); kb.sem("a1e"); kb.sem("a1s"); kb.sem("a1g")
        with nc.Block() as blk:
            @blk.sync
            def _(sync):
                for tt in range(NTF):
                    if tt >= 2:
                        sync.wait_ge(kb.sems["a1m"], tt - 1)
                    buf = xnf_stage if tt % 2 == 0 else xnf_stage2
                    for kc in range(KC):
                        sync.dma_start(
                            buf[:, kc * 128:(kc + 1) * 128],
                            ag_xnf_o.ap()[(tt // 16) * D + kc * 128:
                                          (tt // 16) * D + (kc + 1) * 128,
                                          (tt % 16) * 128:(tt % 16) * 128 + 128]
                        ).then_inc(kb.sem(f"a1l{tt % 2}"), 16)
            @blk.scalar
            def _(s):
                for tt in range(NTF):
                    s.wait_ge(kb.sems["a1m"], tt + 1)
                    if tt >= 1:
                        s.wait_ge(kb.sems["a1e"], tt)
                    s.activation(abt[:, 0:8], ps[1][:, 256:264], AF.Sigmoid)\
                        .then_inc(kb.sems["a1g"], 1)
                    s.activation(n2sq[:, 0:256], ps[0][:, 256:512], AF.Square)
                    s.activation(n2sq[:, 256:512], ps[1][:, 0:256], AF.Square)\
                        .then_inc(kb.sems["a1g"], 1)
            @blk.tensor
            def _(te):
                for tt in range(NTF):
                    te.wait_ge(kb.sem(f"a1l{tt % 2}"), (tt // 2 + 1) * KC * 16)
                    if tt >= 1:
                        te.wait_ge(kb.sems["a1e"], tt)
                    buf = xnf_stage if tt % 2 == 0 else xnf_stage2
                    for kc in range(KC):
                        lhs = buf[:, kc * 128:(kc + 1) * 128]
                        te.matmul(ps[0][:, 0:512], lhs,
                                  wkv_bf[:, kc * 776:kc * 776 + 512],
                                  start=(kc == 0), stop=(kc == KC - 1))
                        te.matmul(ps[1][:, 0:264], lhs,
                                  wkv_bf[:, kc * 776 + 512:kc * 776 + 776],
                                  start=(kc == 0), stop=False)
                    te.matmul(ps[1][:, 256:264], onesb[:], bab_bf[:],
                              start=False, stop=True).then_inc(kb.sems["a1m"], 1)
            @blk.vector
            def _(v):
                for tt in range(NTF):
                    v.wait_ge(kb.sems["a1m"], tt + 1)
                    v.wait_ge(kb.sems["a1g"], tt * 2 + 1)
                    if tt >= 1:
                        v.wait_ge(kb.sem("a1so"), tt)
                    for h in range(4):
                        v.tensor_scalar(ktile[:, h * 64:(h + 1) * 64],
                                        ps[0][:, h * 64:(h + 1) * 64],
                                        abt[:, 4 + h:5 + h], None, OP.mult)
                    v.tensor_copy(ktile[:, 256:512], ps[0][:, 256:512])
                    v.tensor_copy(ktile[:, 512:768], ps[1][:, 0:256])
                    v.wait_ge(kb.sems["a1g"], tt * 2 + 2)
                    kb.flush(v, v.tensor_reduce(
                        sctile[:, 8:12],
                        n2sq[:, 0:256].rearrange("p (h e) -> p h e", e=64),
                        AX.X, OP.add))
                    kb.flush(v, v.tensor_reduce(
                        sctile[:, 12:16],
                        n2sq[:, 256:512].rearrange("p (h e) -> p h e", e=64),
                        AX.X, OP.add))
                    v.tensor_copy(sctile[:, 0:8], abt[:, 0:8])\
                        .then_inc(kb.sems["a1e"], 1)
            @blk.gpsimd
            def _(g):
                for tt in range(NTF):
                    g.wait_ge(kb.sems["a1e"], tt + 1)
                    r0 = tt * 128
                    g.dma_start(vt_d.ap()[r0:r0 + 128, :], ktile[:, 0:256])\
                        .then_inc(kb.sems["a1s"], 16)
                    g.dma_start(kt_d.ap()[r0:r0 + 128, :], ktile[:, 256:512])\
                        .then_inc(kb.sems["a1s"], 16)
                    g.dma_start(qt_d.ap()[r0:r0 + 128, :], ktile[:, 512:768])\
                        .then_inc(kb.sems["a1s"], 16)
                    g.dma_start(scal_d.ap()[r0:r0 + 128, :], sctile[:])\
                        .then_inc(kb.sems["a1s"], 16)
                    g.wait_ge(kb.sems["a1s"], (tt + 1) * 64)
                    g.sem_inc(kb.sem("a1so"), 1)
        kb.cnt["a1l"] = NTF * KC * 16
        kb.cnt["a1m"] = NTF
        kb.cnt["a1g"] = NTF * 2
        kb.cnt["a1e"] = NTF
        kb.cnt["a1s"] = NTF * 64
        kb.cnt["a1so"] = NTF

        phase('a2')
        with nc.Block() as blk:
            @blk.sync
            def _(sync):
                kb.dma(sync, tri64[:], tri_in)
                kb.wait(sync, "dml")
            @blk.vector
            def _(v):
                v.memset(ones1[:], 1.0)
        QW, QR, QM, QSR, QSO, QSS, QGC = range(7)

        def dsl(h, q):
            return der[:, (h * 8 + q) * 64:(h * 8 + q) * 64 + 64]

        def tsl(q):
            return tmp8[:, q * 64:(q + 1) * 64]
        TLA, TLB, TLK, TLQ, TLG, TLG0, TREF, TLGC = range(8)
        sc2 = scal_d.ap().rearrange("(cp cpar tok) col -> cpar tok cp col",
                                    cpar=2, tok=64)
        for h in range(4):
            with nc.Block() as blk:
                @blk.sync
                def _(sync, h=h):
                    for cpar in range(2):
                        kb.dma(sync, rawh[cpar * 64:(cpar + 1) * 64, :]
                               .rearrange("p (cp col) -> p cp col", col=16),
                               sc2[cpar], name="a2l")
                    kb.wait(sync, "a2l")
            with nc.Block() as blk:
                @blk.scalar
                def _(s, h=h):
                    rw = rawh[:].rearrange("p (cp col) -> p cp col", col=16)
                    s.activation(tsl(TLA).rearrange("p c -> p c 1" ) if False else
                                 tsl(TLA).rearrange("p (c o) -> p c o", o=1),
                                 rw[:, :, 0 + h:1 + h], AF.Ln)
                    s.activation(tsl(TLB).rearrange("p (c o) -> p c o", o=1),
                                 rw[:, :, 4 + h:5 + h], AF.Ln)
            with nc.Block() as blk:
                @blk.vector
                def _(v, h=h):
                    rw = rawh[:].rearrange("p (cp col) -> p cp col", col=16)
                    v.tensor_scalar(tsl(TLK).rearrange("p (c o) -> p c o", o=1),
                                    rw[:, :, 8 + h:9 + h], 1e-24, None, OP.max)
                    v.tensor_scalar(tsl(TLQ).rearrange("p (c o) -> p c o", o=1),
                                    rw[:, :, 12 + h:13 + h], 1e-24, None, OP.max)
            with nc.Block() as blk:
                @blk.scalar
                def _(s, h=h):
                    s.activation(tsl(TLK), tsl(TLK), AF.Ln)
                    s.activation(tsl(TLQ), tsl(TLQ), AF.Ln)
            with nc.Block() as blk:
                @blk.vector
                def _(v, h=h):
                    v.tensor_scalar(tsl(TLK), tsl(TLK), -0.5, None, OP.mult)
                    v.tensor_scalar(tsl(TLQ), tsl(TLQ), -0.5, None, OP.mult)
            with nc.Block() as blk:
                @blk.tensor
                def _(te, h=h):
                    for cpar in range(2):
                        te.matmul(ps[0][cpar * 64:(cpar + 1) * 64, 0:64],
                                  tri64[cpar * 64:(cpar + 1) * 64, :],
                                  tsl(TLA)[cpar * 64:(cpar + 1) * 64, :],
                                  start=True, stop=True)
            with nc.Block() as blk:
                @blk.vector
                def _(v, h=h):
                    v.tensor_copy(tsl(TLG), ps[0][:, 0:64])
                    v.tensor_tensor(tsl(TLG0), tsl(TLG), tsl(TLA), OP.subtract)
            with nc.Block() as blk:
                @blk.sync
                def _(sync, h=h):
                    kb.dma(sync, rowm[0:1, 0:64], tsl(TLG0)[32:33, :], name="a2r")
                    kb.dma(sync, rowm[0:1, 64:128], tsl(TLG0)[96:97, :], name="a2r")
                    kb.dma(sync, rowm[0:1, 128:192], tsl(TLG)[63:64, :], name="a2r")
                    kb.dma(sync, rowm[0:1, 192:256], tsl(TLG)[127:128, :], name="a2r")
                    kb.wait(sync, "a2r")
            with nc.Block() as blk:
                @blk.tensor
                def _(te, h=h):
                    for cpar in range(2):
                        te.matmul(ps[1][cpar * 64:(cpar + 1) * 64, 0:64],
                                  ones1[:, 0:64], rowm[0:1, cpar * 64:(cpar + 1) * 64],
                                  start=True, stop=True)
                        te.matmul(ps[2][cpar * 64:(cpar + 1) * 64, 0:64],
                                  ones1[:, 0:64],
                                  rowm[0:1, 128 + cpar * 64:192 + cpar * 64],
                                  start=True, stop=True)
            with nc.Block() as blk:
                @blk.vector
                def _(v, h=h):
                    v.tensor_copy(tsl(TREF), ps[1][:, 0:64])
                    v.tensor_copy(tsl(TLGC), ps[2][:, 0:64])
            with nc.Block() as blk:
                @blk.vector
                def _(v, h=h):
                    v.tensor_tensor(dsl(h, QW), tsl(TLK), tsl(TLG), OP.subtract)
                    v.tensor_tensor(dsl(h, QW), dsl(h, QW), tsl(TREF), OP.add)
                    v.tensor_tensor(dsl(h, QSR), tsl(TLG0), tsl(TLB), OP.add)
                    v.tensor_tensor(dsl(h, QSR), dsl(h, QSR), tsl(TLK), OP.add)
                    v.tensor_tensor(dsl(h, QR), dsl(h, QSR), tsl(TREF), OP.subtract)
                    v.tensor_tensor(dsl(h, QSO), tsl(TLG), tsl(TLQ), OP.add)
                    v.tensor_tensor(dsl(h, QM), dsl(h, QSO), tsl(TREF), OP.subtract)
                    v.tensor_tensor(dsl(h, QSS), tsl(TLGC), tsl(TLG), OP.subtract)
                    v.tensor_tensor(dsl(h, QSS), dsl(h, QSS), tsl(TLK), OP.add)
                    v.tensor_copy(dsl(h, QGC), tsl(TLGC))
                    for q in range(7):
                        v.tensor_scalar(dsl(h, q), dsl(h, q), CLIP, None, OP.min)
            with nc.Block() as blk:
                @blk.scalar
                def _(s, h=h):
                    for q in range(7):
                        s.activation(dsl(h, q), dsl(h, q), AF.Exp)
            with nc.Block() as blk:
                @blk.vector
                def _(v, h=h):
                    v.tensor_scalar(dsl(h, QSR), dsl(h, QSR), -1.0, None, OP.mult)
        with nc.Block() as blk:
            @blk.sync
            def _(sync):
                qlist = (QSR, QSO, QSS, QR, QW, QGC)
                for pr in range(2):
                    for qi, qq in enumerate(qlist):
                        for hh in range(2):
                            h = pr * 2 + hh
                            for cpar in range(2):
                                kb.dma(sync,
                                       stk[hh * 64:(hh + 1) * 64,
                                           pr * 768 + qi * 128 + cpar * 64:
                                           pr * 768 + qi * 128 + (cpar + 1) * 64],
                                       dsl(h, qq)[cpar * 64:(cpar + 1) * 64, :],
                                       name="a2k")
                kb.wait(sync, "a2k")

        def stk_col(pr, qi, c):
            return stk[:, pr * 768 + qi * 128 + (c % 2) * 64 + c // 2:
                       pr * 768 + qi * 128 + (c % 2) * 64 + c // 2 + 1]

        phase('a3')
        kb.sem("a3l"); kb.sem("a3v"); kb.sem("a3t"); kb.sem("a3s")
        trow2 = ktile
        with nc.Block() as blk:
            @blk.sync
            def _(sync):
                def loads(tt):
                    pi = tt % 2
                    sync.dma_start(qb[:, pi * 512:pi * 512 + 256],
                                   kt_d.ap()[tt * 128:(tt + 1) * 128, :])\
                        .then_inc(kb.sem(f"a3l{pi}"), 16)
                    sync.dma_start(qb[:, pi * 512 + 256:pi * 512 + 512],
                                   qt_d.ap()[tt * 128:(tt + 1) * 128, :])\
                        .then_inc(kb.sem(f"a3l{pi}"), 16)
                loads(0)
                for tt in range(NTF):
                    pi = tt % 2
                    if tt + 1 < NTF:
                        if tt >= 1:
                            sync.wait_ge(kb.sems["a3v"], tt)
                            sync.wait_ge(kb.sem(f"a3t{(tt - 1) % 2}"),
                                         160 * ((tt - 1) // 2 + 1))
                        loads(tt + 1)
                    sync.wait_ge(kb.sems["a3v"], tt + 1)
                    if tt >= 1:
                        sync.wait_ge(kb.sem("a3so"), tt)
                    vsrc = vart if pi == 0 else ktile
                    srcs = [qb[:, pi * 512:pi * 512 + 128],
                            qb[:, pi * 512 + 128:pi * 512 + 256],
                            vsrc[:, 0:128], vsrc[:, 128:256],
                            vsrc[:, 256:384], vsrc[:, 384:512],
                            qb[:, pi * 512 + 256:pi * 512 + 384],
                            qb[:, pi * 512 + 384:pi * 512 + 512],
                            vsrc[:, 512:640], vsrc[:, 640:768]]
                    for j in range(10):
                        sync.dma_start(trow[:, j * 128:(j + 1) * 128], srcs[j],
                                       transpose=True)\
                            .then_inc(kb.sem(f"a3t{pi}"), 16)
            @blk.vector
            def _(v):
                for tt in range(NTF):
                    pi = tt % 2
                    v.wait_ge(kb.sem(f"a3l{pi}"), (tt // 2 + 1) * 32)
                    if tt >= 2:
                        v.wait_ge(kb.sem(f"a3t{pi}"), (tt // 2) * 160)
                    vdst = vart if pi == 0 else ktile
                    for h in range(4):
                        v.tensor_scalar(vdst[:, h * 64:(h + 1) * 64],
                                        qb[:, pi * 512 + h * 64:
                                           pi * 512 + (h + 1) * 64],
                                        dsl(h, QW)[:, tt:tt + 1], None, OP.mult)
                        v.tensor_scalar(vdst[:, 256 + h * 64:256 + (h + 1) * 64],
                                        qb[:, pi * 512 + h * 64:
                                           pi * 512 + (h + 1) * 64],
                                        dsl(h, QR)[:, tt:tt + 1], None, OP.mult)
                        vv = v.tensor_scalar(vdst[:, 512 + h * 64:512 + (h + 1) * 64],
                                             qb[:, pi * 512 + 256 + h * 64:
                                                pi * 512 + 256 + (h + 1) * 64],
                                             dsl(h, QM)[:, tt:tt + 1], None, OP.mult)
                    vv.then_inc(kb.sems["a3v"], 1)
            @blk.gpsimd
            def _(g):
                dsts = [kf_d, kf_d, kw_d, kw_d, kr_d, kr_d, qf_d, qf_d, qm_d, qm_d]
                for tt in range(NTF):
                    g.wait_ge(kb.sem(f"a3t{tt % 2}"), (tt // 2 + 1) * 160)
                    for j in range(10):
                        pr = j % 2
                        g.dma_start(dsts[j].ap()[pr * 128:(pr + 1) * 128,
                                                 tt * 128:(tt + 1) * 128],
                                    trow[:, j * 128:(j + 1) * 128])\
                            .then_inc(kb.sems["a3s"], 16)
                    g.wait_ge(kb.sems["a3s"], (tt + 1) * 160)
                    g.sem_inc(kb.sem("a3so"), 1)
        kb.cnt["a3l"] = NTF * 32
        kb.cnt["a3v"] = NTF
        kb.cnt["a3t"] = NTF * 160
        kb.cnt["a3s"] = NTF * 160
        kb.cnt["a3so"] = NTF

    phase('a4')
    if ATTN:
        NB = 9 * 16
        PG = 15
        ED = 23
        NU = NCH * 2
        kb.sem("dl"); kb.sem("dm"); kb.sem("de"); kb.sem("dy")
        KFV, KWV, KRV, QFV, QMV = range(5)

        def kfsl(pi, var, hh=None):
            basec = (pi * 5 + var) * 64
            if hh is None:
                return kfs[:, basec:basec + 64]
            return kfs[hh * 64:(hh + 1) * 64, basec:basec + 64]

        with nc.Block() as blk:
            @blk.sync
            def _(sync):
                for u in range(NU):
                    c, p = u // 2, u % 2
                    pi = u % 2
                    if u >= 2:
                        sync.wait_ge(kb.sems["dm"], PG * (u - 1))
                        sync.wait_ge(kb.sems["de"], ED * (u - 1))
                    for var, srcd in ((KFV, kf_d), (KWV, kw_d), (KRV, kr_d),
                                      (QFV, qf_d), (QMV, qm_d)):
                        sync.dma_start(kfsl(pi, var),
                                       srcd.ap()[p * 128:(p + 1) * 128,
                                                 c * 64:(c + 1) * 64])\
                            .then_inc(kb.sem(f"dl{pi}"), 16)
                    for hh in range(2):
                        sync.dma_start(kts[hh * 64:(hh + 1) * 64,
                                           pi * 128:(pi + 1) * 128],
                                       kt_d.ap()[c * 64:(c + 1) * 64,
                                                 p * 128:(p + 1) * 128])\
                            .then_inc(kb.sem(f"dl{pi}"), 16)
                        sync.dma_start(vts[hh * 64:(hh + 1) * 64,
                                           pi * 64:(pi + 1) * 64],
                                       vt_d.ap()[c * 64:(c + 1) * 64,
                                                 (p * 2 + hh) * 64:
                                                 (p * 2 + hh + 1) * 64])\
                            .then_inc(kb.sem(f"dl{pi}"), 16)

            @blk.tensor
            def _(te):
                for u in range(NU):
                    c, p = u // 2, u % 2
                    pi = u % 2
                    cpp = c % 2
                    dm0 = PG * u
                    de0 = ED * u
                    te.wait_ge(kb.sem(f"dl{u % 2}"), NB * (u // 2 + 1))
                    if u >= 1:
                        te.wait_ge(kb.sems["de"], ED * u)
                    for hh in range(2):
                        r = slice(hh * 64, hh * 64 + 64)
                        te.matmul(ps[0][r, hh * 64:hh * 64 + 64],
                                  kfsl(pi, KFV, hh), kfsl(pi, KWV, hh),
                                  start=True, stop=True)
                    for hh in range(2):
                        r = slice(hh * 64, hh * 64 + 64)
                        te.matmul(ps[1][r, hh * 64:hh * 64 + 64],
                                  kfsl(pi, KFV, hh), kfsl(pi, KRV, hh),
                                  start=True, stop=True)
                    for hh in range(2):
                        r = slice(hh * 64, hh * 64 + 64)
                        te.matmul(ps[2][r, hh * 64:hh * 64 + 64],
                                  kfsl(pi, KFV, hh), kfsl(pi, QMV, hh),
                                  start=True, stop=True)\
                            .then_inc(kb.sems["dm"], 1) if hh == 1 else \
                            te.matmul(ps[2][r, hh * 64:hh * 64 + 64],
                                      kfsl(pi, KFV, hh), kfsl(pi, QMV, hh),
                                      start=True, stop=True)
                    for hh in range(2):
                        r = slice(hh * 64, hh * 64 + 64)
                        mm = te.matmul(ps[3][r, 0:64], kfsl(pi, KFV, hh),
                                       S0[r, (p * 2 + cpp) * 64:
                                          (p * 2 + cpp) * 64 + 64],
                                       start=True, stop=True)
                        if hh == 1:
                            mm.then_inc(kb.sems["dm"], 1)
                    te.wait_ge(kb.sems["de"], de0 + 4)
                    te.matmul(ps[6][:, 0:64], LTb[:], Yw[:, 0:64],
                              start=True, stop=True).then_inc(kb.sems["dm"], 1)
                    for j in range(1, 6):
                        sl_new = j % 2
                        if j == 1:
                            lp, lpt = Lb[:], LTb[:]
                        else:
                            lp = Pw[:, (1 - sl_new) * 128:(2 - sl_new) * 128]
                            lpt = PwT[:, (1 - sl_new) * 128:(2 - sl_new) * 128]
                        if j >= 2:
                            te.wait_ge(kb.sems["de"], de0 + 4 + 3 * (j - 1))
                        if j < 5:
                            te.matmul(ps[4][:, 0:128], lpt, lp, start=True, stop=True)
                            te.matmul(ps[5][:, 0:128], lp, lpt, start=True,
                                      stop=True).then_inc(kb.sems["dm"], 1)
                        else:
                            te.matmul(ps[5][:, 0:128], lp, lpt, start=True,
                                      stop=True).then_inc(kb.sems["dm"], 1)
                        if j < 5:
                            te.wait_ge(kb.sems["de"], de0 + 4 + 3 * j)
                        else:
                            te.wait_ge(kb.sems["de"], de0 + 4 + 3 * 4 + 2)
                        ysl = j % 2
                        te.matmul(ps[6][:, 0:64],
                                  PwT[:, sl_new * 128:(sl_new + 1) * 128],
                                  Yw[:, ysl * 64:(ysl + 1) * 64],
                                  start=True, stop=True).then_inc(kb.sems["dm"], 1)
                    te.wait_ge(kb.sems["de"], de0 + 19)
                    for hh in range(2):
                        r = slice(hh * 64, hh * 64 + 64)
                        te.matmul(ps[0][r, 0:64], kfsl(pi, QFV, hh),
                                  S0[r, (p * 2 + cpp) * 64:(p * 2 + cpp) * 64 + 64],
                                  start=True, stop=True)
                    te.matmul(ps[1][:, 0:64], MTb[:], Yw[:, 0:64],
                              start=True, stop=True).then_inc(kb.sems["dm"], 1)
                    te.wait_ge(kb.sems["de"], de0 + 20)
                    for hh in range(2):
                        r = slice(hh * 64, hh * 64 + 64)
                        mm = te.matmul(ps[2][r, 0:64],
                                       kts[r, pi * 128 + hh * 64:
                                           pi * 128 + hh * 64 + 64],
                                       Usb[r, :], start=True, stop=True)
                        if hh == 1:
                            mm.then_inc(kb.sems["dm"], 1)

            @blk.vector
            def _(v):
                for u in range(NU):
                    c, p = u // 2, u % 2
                    pi = u % 2
                    cpp = c % 2
                    dm0 = PG * u
                    de0 = ED * u
                    v.wait_ge(kb.sems["dm"], dm0 + 1)
                    v.tensor_tensor(evs[:], ps[0][:, 0:128], mlow[:], OP.mult)
                    v.tensor_scalar(Lb[:], evs[:], stk_col(p, 3, c), None, OP.mult)\
                        .then_inc(kb.sems["de"], 1)
                    v.tensor_tensor(evs2[:], ps[1][:, 0:128], mupp[:], OP.mult)
                    v.tensor_scalar(LTb[:], evs2[:], stk_col(p, 4, c), None, OP.mult)\
                        .then_inc(kb.sems["de"], 1)
                    v.tensor_tensor(evs[:], ps[2][:, 0:128], mupi[:], OP.mult)
                    v.tensor_scalar(MTb[:], evs[:], stk_col(p, 4, c), None, OP.mult)\
                        .then_inc(kb.sems["de"], 1)
                    v.wait_ge(kb.sems["dm"], dm0 + 2)
                    v.wait_ge(kb.sem(f"dl{u % 2}"), NB * (u // 2 + 1))
                    v.scalar_tensor_tensor(Yw[:, 0:64], ps[3][:, 0:64],
                                           stk_col(p, 0, c),
                                           vts[:, pi * 64:(pi + 1) * 64],
                                           OP.mult, OP.add)\
                        .then_inc(kb.sems["de"], 1)
                    v.wait_ge(kb.sems["dm"], dm0 + 3)
                    v.tensor_tensor(Yw[:, 64:128], Yw[:, 0:64], ps[6][:, 0:64],
                                    OP.subtract).then_inc(kb.sems["de"], 1)
                    for j in range(1, 6):
                        sl_new = j % 2
                        v.wait_ge(kb.sems["dm"], dm0 + 3 + 2 * (j - 1) + 1)
                        if j < 5:
                            v.tensor_copy(Pw[:, sl_new * 128:(sl_new + 1) * 128],
                                          ps[4][:, 0:128]).then_inc(kb.sems["de"], 1)
                        v.tensor_copy(PwT[:, sl_new * 128:(sl_new + 1) * 128],
                                      ps[5][:, 0:128]).then_inc(kb.sems["de"], 1)
                        v.wait_ge(kb.sems["dm"], dm0 + 3 + 2 * (j - 1) + 2)
                        ydst = (j + 1) % 2
                        v.tensor_tensor(Yw[:, ydst * 64:(ydst + 1) * 64],
                                        Yw[:, (j % 2) * 64:(j % 2) * 64 + 64],
                                        ps[6][:, 0:64], OP.add)\
                            .then_inc(kb.sems["de"], 1)
                    v.tensor_scalar(Usb[:], Yw[:, 0:64], stk_col(p, 2, c), None,
                                    OP.mult).then_inc(kb.sems["de"], 1)
                    v.wait_ge(kb.sems["dm"], dm0 + 14)
                    if u >= 2:
                        v.wait_ge(kb.sem("dyo"), u - 1)
                    v.tensor_scalar(evs[:, 0:64], ps[0][:, 0:64],
                                    stk_col(p, 1, c), None, OP.mult)\
                        .then_inc(kb.sems["de"], 1)
                    v.tensor_tensor(ytile[:, pi * 64:(pi + 1) * 64], evs[:, 0:64],
                                    ps[1][:, 0:64], OP.add)\
                        .then_inc(kb.sems["de"], 1)
                    v.wait_ge(kb.sems["dm"], dm0 + 15)
                    v.scalar_tensor_tensor(
                        S0[:, (p * 2 + 1 - cpp) * 64:(p * 2 + 1 - cpp) * 64 + 64],
                        S0[:, (p * 2 + cpp) * 64:(p * 2 + cpp) * 64 + 64],
                        stk_col(p, 5, c), ps[2][:, 0:64], OP.mult, OP.add)\
                        .then_inc(kb.sems["de"], 1)

            @blk.gpsimd
            def _(g):
                for u in range(NU):
                    c, p = u // 2, u % 2
                    pi = u % 2
                    g.wait_ge(kb.sems["de"], ED * u + 22)
                    for hh in range(2):
                        g.dma_start(y_my.ap()[c * 64:(c + 1) * 64,
                                              (p * 2 + hh) * 64:(p * 2 + hh + 1) * 64],
                                    ytile[hh * 64:(hh + 1) * 64,
                                          pi * 64:(pi + 1) * 64])\
                            .then_inc(kb.sems["dy"], 16)
                    g.wait_ge(kb.sems["dy"], 32 * (u + 1))
                    g.sem_inc(kb.sem("dyo"), 1)
        kb.cnt["dl"] = NB * NU
        kb.cnt["dm"] = PG * NU
        kb.cnt["de"] = ED * NU
        kb.cnt["dy"] = 32 * NU
        kb.cnt["dyo"] = NU

        if DEBUG:
            with nc.Block() as blk:
                @blk.sync
                def _(sync):
                    for ri, (dsrc, ddst) in enumerate(
                            ((kt_d, dbg_kt), (qt_d, dbg_qt), (vt_d, dbg_vt))):
                        for r in range(16):
                            kb.dma(sync, qb[:, 0:256],
                                   dsrc.ap()[r * 128:(r + 1) * 128, 0:256],
                                   name="dbg")
                            kb.wait(sync, "dbg")
                            kb.dma(sync, ddst[r * 128:(r + 1) * 128, 0:256],
                                   qb[:, 0:256], name="dbg")
                            kb.wait(sync, "dbg")
                    for r in range(16):
                        kb.dma(sync, qb[:, 0:256],
                               ag_xnf_o.ap()[0:128, r * 128:r * 128 + 256]
                               if r < 8 else
                               kf_d.ap()[0:128, (r - 8) * 256:(r - 8) * 256 + 256],
                               name="dbg")
                        kb.wait(sync, "dbg")
                        kb.dma(sync, dbg_kf[0:128, r * 256:r * 256 + 256],
                               qb[:, 0:256], name="dbg")
                        kb.wait(sync, "dbg")
                    kb.dma(sync, dbg_kw[0:128, 0:776], wkv_bf[:, 0:776], name="dbg")
                    kb.wait(sync, "dbg")
                    kb.dma(sync, dbg_kw[0:128, 1024:1024 + 1024],
                           ag_xnf_i.ap()[0:128, 0:1024], name="dbg")
                    kb.wait(sync, "dbg")
                    for r in range(NTF):
                        kb.dma(sync, xt[:, 0:256], y_my.ap()[r * 128:(r + 1) * 128, :],
                               name="dbg")
                        kb.wait(sync, "dbg")
                        kb.dma(sync, dbg_y[r * 128:(r + 1) * 128, :], xt[:, 0:256],
                               name="dbg")
                        kb.wait(sync, "dbg")
                    for r in range(NTF):
                        kb.dma(sync, xt[:, 0:16], scal_d.ap()[r * 128:(r + 1) * 128, :],
                               name="dbg")
                        kb.wait(sync, "dbg")
                        kb.dma(sync, dbg_sc[r * 128:(r + 1) * 128, :], xt[:, 0:16],
                               name="dbg")
                        kb.wait(sync, "dbg")
                    kb.dma(sync, dbg_stk[:], stk[:], name="dbg")
                    kb.wait(sync, "dbg")


        phase('b5')
        for kc in range(KC):
            with nc.Block() as blk:
                @blk.sync
                def _(sync, kc=kc):
                    kb.dma(sync, xt[:, 0:256],
                           wg_in.rearrange("(n p) d -> n p d", p=128)[kc])
                    kb.wait(sync, "dml")
            with nc.Block() as blk:
                @blk.vector
                def _(v, kc=kc):
                    v.tensor_copy(wbig[:, kc * 256:(kc + 1) * 256], xt[:, 0:256])
        sg_t2 = sg_d.ap().rearrange("(n p) d -> n p d", p=128)
        kb.sem("b5l"); kb.sem("b5m"); kb.sem("b5e"); kb.sem("b5s")
        with nc.Block() as blk:
            @blk.sync
            def _(sync):
                for tt in range(NTF):
                    if tt >= 2:
                        sync.wait_ge(kb.sems["b5m"], tt - 1)
                    for kc in range(KC):
                        sync.dma_start(
                            xqf_r[:, (tt % 2) * 1024 + kc * 128:
                                  (tt % 2) * 1024 + (kc + 1) * 128],
                            ag_xnf_o.ap()[(tt // 16) * D + kc * 128:
                                          (tt // 16) * D + (kc + 1) * 128,
                                          (tt % 16) * 128:(tt % 16) * 128 + 128])\
                            .then_inc(kb.sem(f"b5l{tt % 2}"), 16)
            @blk.tensor
            def _(te):
                for tt in range(NTF):
                    te.wait_ge(kb.sem(f"b5l{tt % 2}"), (tt // 2 + 1) * KC * 16)
                    if tt >= 2:
                        te.wait_ge(kb.sems["b5e"], tt - 1)
                    for kc in range(KC):
                        mm = te.matmul(ps[tt % 2][:, 0:256],
                                       xqf_r[:, (tt % 2) * 1024 + kc * 128:
                                             (tt % 2) * 1024 + (kc + 1) * 128],
                                       wbig[:, kc * 256:(kc + 1) * 256],
                                       start=(kc == 0), stop=(kc == KC - 1))
                    mm.then_inc(kb.sems["b5m"], 1)
            @blk.scalar
            def _(s):
                def ysq_sq(u):
                    su = u % 4
                    s.wait_ge(kb.sem(f"b6q{su}"), (u // 4 + 1) * 16)
                    if u >= 4:
                        s.wait_ge(kb.sem("b6e2"), u - 3)
                    s.activation(q_fa[:, su * 256:(su + 1) * 256],
                                 q_fb[:, su * 256:(su + 1) * 256],
                                 AF.Square).then_inc(kb.sem("b6a2"), 1)

                def ysq_st(u):
                    su = u % 4
                    s.wait_ge(kb.sem("b6e2"), u + 1)
                    s.dma_start(sq_i.ap()[u * 128:(u + 1) * 128, :],
                                stat[:, su:su + 1])\
                        .then_inc(kb.sem(f"b6st{su}"), 16)
                for tt in range(NTF):
                    s.wait_ge(kb.sems["b5m"], tt + 1)
                    if tt >= 2:
                        s.wait_ge(kb.sem("b5so"), tt - 1)
                    s.activation(xnt[:, (tt % 2) * 256:(tt % 2) * 256 + 256],
                                 ps[tt % 2][:, 0:256], AF.Silu)\
                        .then_inc(kb.sems["b5e"], 1)
                    s.dma_start(q_fb[:, (tt % 4) * 256:(tt % 4 + 1) * 256],
                                y_my.ap()[tt * 128:(tt + 1) * 128, :])\
                        .then_inc(kb.sem(f"b6q{tt % 4}"), 16)
                    if tt >= 2:
                        ysq_sq(tt - 2)
                    if tt >= 5:
                        ysq_st(tt - 5)
                for u in (NTF - 2, NTF - 1):
                    ysq_sq(u)
                for u in range(NTF - 5, NTF):
                    ysq_st(u)
                for sl in range(4):
                    s.wait_ge(kb.sem(f"b6st{sl}"), (NTF // 4) * 16)
            @blk.gpsimd
            def _(g):
                for tt in range(NTF):
                    g.wait_ge(kb.sems["b5e"], tt + 1)
                    g.dma_start(sg_t2[tt], xnt[:, (tt % 2) * 256:(tt % 2) * 256 + 256])\
                        .then_inc(kb.sems["b5s"], 16)
                    g.wait_ge(kb.sems["b5s"], (tt + 1) * 16)
                    g.sem_inc(kb.sem("b5so"), 1)
            @blk.vector
            def _(v):
                for u in range(NTF):
                    su = u % 4
                    v.wait_ge(kb.sem("b6a2"), u + 1)
                    if u >= 4:
                        v.wait_ge(kb.sem(f"b6st{su}"), (u // 4) * 16)
                    v.tensor_reduce(stat[:, su:su + 1],
                                    q_fa[:, su * 256:(su + 1) * 256],
                                    AX.X, OP.add)\
                        .then_inc(kb.sem("b6e2"), 1)
        kb.cnt["b5l"] = NTF * KC * 16
        kb.cnt["b5m"] = NTF
        kb.cnt["b5e"] = NTF
        kb.cnt["b5s"] = NTF * 16
        kb.cnt["b5so"] = NTF
        phase('b6')
        with nc.Block() as blk:
            @blk.gpsimd
            def _(g):
                cs, _ = kb.inc("coll", 1)
                g.collective_compute("AllReduce", OP.add,
                                     replica_groups=[[0, 1, 2, 3], [4, 5, 6, 7]],
                                     ins=[sq_i.ap().opt()],
                                     outs=[sq_o.ap().opt()]).then_inc(cs, 1)
                g.wait_ge(kb.sems["coll"], kb.cnt["coll"])
        with nc.Block() as blk:
            @blk.sync
            def _(sync):
                kb.dma(sync, q_fb[0:1, 0:256], gn_in)
                kb.wait(sync, "dml")
        with nc.Block() as blk:
            @blk.tensor
            def _(te):
                te.matmul(ps[0][:, 0:256], ones1[:], q_fb[0:1, 0:256],
                          start=True, stop=True)
        with nc.Block() as blk:
            @blk.vector
            def _(v):
                v.tensor_copy(gnbt[:, 0:256], ps[0][:, 0:256])
        for kc2 in range(2):
            with nc.Block() as blk:
                @blk.sync
                def _(sync, kc2=kc2):
                    kb.dma(sync, xt[:, 0:D],
                           wo_in.rearrange("(n p) d -> n p d", p=128)[kc2])
                    kb.wait(sync, "dml")
            with nc.Block() as blk:
                @blk.vector
                def _(v, kc2=kc2):
                    v.tensor_copy(wbig[:, 4096 + kc2 * 1024:4096 + (kc2 + 1) * 1024],
                                  xt[:, 0:D])
        phase('rstd')
        phase('b7')
        atp_t = atp_d.ap().rearrange("(n p) d -> n p d", p=128)
        kb.sem("b7l"); kb.sem("b7z"); kb.sem("b7t"); kb.sem("b7m"); kb.sem("b7e")
        kb.sem("b7s")
        with nc.Block() as blk:
            @blk.sync
            def _(sync):
                def loads7(tt):
                    pi = tt % 2
                    sync.dma_start(yrow[pi][:, 0:256],
                                   y_my.ap()[tt * 128:(tt + 1) * 128, :])\
                        .then_inc(kb.sem(f"b7l{pi}"), 16)
                    sync.dma_start(q_fd[:, pi * 256:pi * 256 + 256], sg_t2[tt])\
                        .then_inc(kb.sem(f"b7l{pi}"), 16)
                    sync.dma_start(stat[:, 4 + pi:5 + pi],
                                   sq_o.ap()[tt * 128:(tt + 1) * 128, :])\
                        .then_inc(kb.sem(f"b7l{pi}"), 16)
                loads7(0)
                for tt in range(NTF):
                    pi = tt % 2
                    if tt + 1 < NTF:
                        if tt >= 1:
                            sync.wait_ge(kb.sems["b7z"], tt)
                        loads7(tt + 1)
                    sync.wait_ge(kb.sems["b7z"], tt + 1)
                    if tt >= 2:
                        sync.wait_ge(kb.sems["b7m"], tt - 1)
                    for j in range(2):
                        sync.dma_start(xqf_r[:, 2048 + pi * 256 + j * 128:
                                             2048 + pi * 256 + (j + 1) * 128],
                                       xnb[:, pi * 256 + j * 128:
                                           pi * 256 + (j + 1) * 128],
                                       transpose=True)\
                            .then_inc(kb.sem(f"b7t{pi}"), 16)
            @blk.vector
            def _(v):
                def zstep(tt):
                    pi = tt % 2
                    v.wait_ge(kb.sem(f"b7l{pi}"), (tt // 2 + 1) * 48)
                    if tt >= 2:
                        v.wait_ge(kb.sem(f"b7t{pi}"), (tt // 2) * 32)
                    v.tensor_scalar(stat[:, 6 + pi:7 + pi],
                                    stat[:, 4 + pi:5 + pi],
                                    1.0 / D, 1e-5, OP.mult, OP.add)\
                        .then_inc(kb.sem("b7v"), 1)
                    kb.flush(v, v.tensor_tensor(q_fa[:, pi * 256:pi * 256 + 256],
                                                q_fd[:, pi * 256:pi * 256 + 256],
                                                gnbt[:, 0:256], OP.mult))
                    v.wait_ge(kb.sem("b7a"), tt + 1)
                    kb.flush(v, v.reciprocal(stat[:, 6 + pi:7 + pi],
                                             stat[:, 6 + pi:7 + pi]))
                    kb.flush(v, v.tensor_scalar(q_fb[:, pi * 256:pi * 256 + 256],
                                                yrow[pi][:, 0:256],
                                                stat[:, 6 + pi:7 + pi],
                                                None, OP.mult))
                    v.tensor_tensor(xnb[:, pi * 256:pi * 256 + 256],
                                    q_fb[:, pi * 256:pi * 256 + 256],
                                    q_fa[:, pi * 256:pi * 256 + 256], OP.mult)\
                        .then_inc(kb.sems["b7z"], 1)

                def estep(tt):
                    pi = tt % 2
                    v.wait_ge(kb.sems["b7m"], tt + 1)
                    if tt >= 2:
                        v.wait_ge(kb.sem("b7so"), tt - 1)
                    v.tensor_copy(q_fc[:, 0:1024].bitcast(BF16)
                                  [:, pi * 1024:pi * 1024 + 512],
                                  ps[4 + 2 * pi][:, 0:512])
                    v.tensor_copy(q_fc[:, 0:1024].bitcast(BF16)
                                  [:, pi * 1024 + 512:pi * 1024 + 1024],
                                  ps[5 + 2 * pi][:, 0:512])\
                        .then_inc(kb.sems["b7e"], 1)
                for tt in range(NTF):
                    zstep(tt)
                    if tt >= 1:
                        estep(tt - 1)
                estep(NTF - 1)
            @blk.tensor
            def _(te):
                for tt in range(NTF):
                    pi = tt % 2
                    te.wait_ge(kb.sem(f"b7t{pi}"), (tt // 2 + 1) * 32)
                    if tt >= 2:
                        te.wait_ge(kb.sems["b7e"], tt - 1)
                    for half in range(2):
                        for j in range(2):
                            mm = te.matmul(ps[4 + 2 * pi + half][:, 0:512],
                                           xqf_r[:, 2048 + pi * 256 + j * 128:
                                                 2048 + pi * 256 + (j + 1) * 128],
                                           wbig[:, 4096 + j * 1024 + half * 512:
                                                4096 + j * 1024 + (half + 1) * 512],
                                           start=(j == 0), stop=(j == 1))
                    mm.then_inc(kb.sems["b7m"], 1)
            @blk.scalar
            def _(s):
                for tt in range(NTF):
                    pi = tt % 2
                    s.wait_ge(kb.sem("b7v"), tt + 1)
                    s.activation(stat[:, 6 + pi:7 + pi], stat[:, 6 + pi:7 + pi],
                                 AF.Sqrt).then_inc(kb.sem("b7a"), 1)
            @blk.gpsimd
            def _(g):
                for tt in range(NTF):
                    pi = tt % 2
                    g.wait_ge(kb.sems["b7e"], tt + 1)
                    g.dma_start(atp_t[tt],
                                q_fc[:, 0:1024].bitcast(BF16)
                                [:, pi * 1024:pi * 1024 + 1024])\
                        .then_inc(kb.sems["b7s"], 16)
                    g.wait_ge(kb.sems["b7s"], (tt + 1) * 16)
                    g.sem_inc(kb.sem("b7so"), 1)
        kb.cnt["b7l"] = NTF * 48
        kb.cnt["b7z"] = NTF
        kb.cnt["b7t"] = NTF * 32
        kb.cnt["b7m"] = NTF
        kb.cnt["b7e"] = NTF
        kb.cnt["b7s"] = NTF * 16
        kb.cnt["b7so"] = NTF
        kb.cnt["b7v"] = NTF
        kb.cnt["b7a"] = NTF
        phase('rs')
        with nc.Block() as blk:
            @blk.gpsimd
            def _(g):
                cs, _ = kb.inc("coll", 1)
                g.collective_compute("ReduceScatter", OP.add,
                                     replica_groups=[[0, 1, 2, 3], [4, 5, 6, 7]],
                                     ins=[atp_d.ap().opt()],
                                     outs=[atr_d.ap().opt()]).then_inc(cs, 1)
                g.wait_ge(kb.sems["coll"], kb.cnt["coll"])
        atr_t = atr_d.ap().rearrange("(n p) d -> n p d", p=128)
        phase('fin')
        for it in range(NT):
            with nc.Block() as blk:
                @blk.sync
                def _(sync, it=it):
                    kb.dma(sync, xt[:, 0:D], x_t[it], name="a5x")
                    kb.dma(sync, yrow[1][:], ffn_t[it], name="a5x")
                    kb.dma(sync, xnb[:, 0:D], atr_t[it], name="a5x")
                    kb.wait(sync, "a5x")
            with nc.Block() as blk:
                @blk.vector
                def _(v):
                    v.tensor_copy(q_fa[:, 0:D], xnb[:, 0:D])
                    v.tensor_tensor(xnt[:], xt[:, 0:D], yrow[1][:], OP.add)
                    v.tensor_tensor(xnt[:], xnt[:], q_fa[:, 0:D], OP.add)
            with nc.Block() as blk:
                @blk.sync
                def _(sync, it=it):
                    kb.dma(sync, out_t[it], xnt[:], name="dst")
                    kb.wait(sync, "dst")

    else:
        for it in range(NT):
            with nc.Block() as blk:
                @blk.sync
                def _(sync, it=it):
                    kb.dma(sync, xt[:, 0:D], x_t[it])
                    kb.dma(sync, yrow[0][:], ffn_t[it])
                    kb.wait(sync, "dml")
            with nc.Block() as blk:
                @blk.vector
                def _(v):
                    v.tensor_tensor(xnt[:], xt[:, 0:D], yrow[0][:], OP.add)
            with nc.Block() as blk:
                @blk.sync
                def _(sync, it=it):
                    kb.dma(sync, out_t[it], xnt[:], name="dst")
                    kb.wait(sync, "dst")

    phase(None)
    if DEBUG:
        with nc.Block() as blk:
            @blk.sync
            def _(sync):
                kb.dma(sync, dbg_ffn, ffn_d.ap(), name="dbg2")
                kb.dma(sync, dbg_sg, sg_d.ap(), name="dbg2")
                kb.dma(sync, dbg_atr, atr_d.ap(), name="dbg2")
                kb.dma(sync, dbg_hq, hqT_d.ap(), name="dbg2")
                kb.dma(sync, dbg_sq, sq_o.ap(), name="dbg2")
                kb.dma(sync, dbg_atp, atp_d.ap(), name="dbg2")
                kb.wait(sync, "dbg2")

    return kb


def make_consts():
    i = np.arange(128)
    hhp = i // 64
    tp = i % 64
    low = ((hhp[:, None] == hhp[None, :]) & (tp[None, :] < tp[:, None])).astype(np.float32)
    upp = ((hhp[:, None] == hhp[None, :]) & (tp[None, :] > tp[:, None])).astype(np.float32)
    upi = ((hhp[:, None] == hhp[None, :]) & (tp[None, :] >= tp[:, None])).astype(np.float32)
    return low, upp, upi


def kernel(**inputs):
    x = np.ascontiguousarray(np.asarray(inputs["L0_attn_x"], np.float32))
    Wq = np.asarray(inputs["L0_attn_Wq"], np.float32)
    Wk = np.asarray(inputs["L0_attn_Wk"], np.float32)
    Wv = np.asarray(inputs["L0_attn_Wv"], np.float32)
    Wa = np.asarray(inputs["L0_attn_Wa"], np.float32)
    ba = np.asarray(inputs["L0_attn_ba"], np.float32)
    Wb = np.asarray(inputs["L0_attn_Wb"], np.float32)
    bb = np.asarray(inputs["L0_attn_bb"], np.float32)
    Wg = np.ascontiguousarray(np.asarray(inputs["L0_attn_Wg"], np.float32))
    Wo = np.ascontiguousarray(np.asarray(inputs["L0_attn_Wo"], np.float32))
    gn = np.asarray(inputs["L0_attn_g_norm"], np.float32)
    wgate = np.asarray(inputs["L0_ffn_w_gate"], np.float32)
    wup = np.asarray(inputs["L0_ffn_w_up"], np.float32)
    wdown = np.asarray(inputs["L0_ffn_w_down"], np.float32)
    low, upp, upi = make_consts()
    trj = np.tril(np.ones((64, 64), np.float32)).T.copy()
    tri2 = np.ascontiguousarray(np.concatenate([trj, trj], 0))

    kb = build()
    in_maps = []
    for c in range(8):
        b, qq = c // 4, c % 4
        hc = slice(qq * 256, (qq + 1) * 256)
        wkvqab = np.concatenate([Wv[:, hc], Wk[:, hc], Wq[:, hc],
                                 Wa[:, 4 * qq:4 * qq + 4], Wb[:, 4 * qq:4 * qq + 4]], 1)
        bab = np.concatenate([ba[4 * qq:4 * qq + 4], bb[4 * qq:4 * qq + 4]])[None, :]
        in_maps.append({
            "x": np.ascontiguousarray(x[b, qq * T:(qq + 1) * T]),
            "wkvqab": np.ascontiguousarray(wkvqab),
            "bab": np.ascontiguousarray(bab),
            "wgm": np.ascontiguousarray(Wg[:, hc]),
            "wom": np.ascontiguousarray(Wo[qq * 256:(qq + 1) * 256, :]),
            "gnm": np.ascontiguousarray(gn[None, qq * 256:(qq + 1) * 256]),
            "wgate": np.ascontiguousarray(wgate[c * (I3 // 8):(c + 1) * (I3 // 8)]),
            "wup": np.ascontiguousarray(wup[c * (I3 // 8):(c + 1) * (I3 // 8)]),
            "wdown": np.ascontiguousarray(wdown[c * (D // 8):(c + 1) * (D // 8)]),
            "mlow": low, "mupp": upp, "mupi": upi, "tri": tri2,
        })
    res = run_bass_kernel_spmd(kb.nc, in_maps, core_ids=list(range(8)))
    out = np.zeros((2, S, D), np.float32)
    for c in range(8):
        b, qq = c // 4, c % 4
        out[b, qq * T:(qq + 1) * T] = res.results[c]["out"]
    return out




# version 15
# speedup vs baseline: 1.2533x; 1.0543x over previous
import numpy as np
import concourse.bass as bass
import concourse.mybir as mybir
from concourse.bass_utils import run_bass_kernel_spmd

F32 = mybir.dt.float32
BF16 = mybir.dt.bfloat16
U8 = mybir.dt.uint8
U32 = mybir.dt.uint32
AF = mybir.ActivationFunctionType
OP = mybir.AluOpType
AX = mybir.AxisListType

S, D = 8192, 1024
T = 2048
NT = T // 128
NTF = S // 128
KC = D // 128
I3 = 2816
ISL = 352
NIS = I3 // ISL
NI128 = I3 // 128
DH = 64
C = 64
NCH = S // C
CLIP = 60.0

ATTN = True
DEBUG = False
SCAN = True


class KB:
    def __init__(self):
        self.nc = bass.Bass("TRN2", target_bir_lowering=False)
        self.sems = {}
        self.cnt = {}

    def sem(self, name):
        if name not in self.sems:
            self.sems[name] = self.nc.alloc_semaphore(name)
            self.cnt[name] = 0
        return self.sems[name]

    def val(self, name):
        return self.cnt.get(name, 0)

    def inc(self, name, by=16):
        self.sem(name)
        self.cnt[name] += by
        return self.sems[name], self.cnt[name]

    def dma(self, eng, out, in_, name="dml", transpose=False):
        s, _ = self.inc(name)
        eng.dma_start(out, in_, transpose=transpose).then_inc(s, 16)

    def wait(self, eng, name, val=None):
        if name in self.sems:
            eng.wait_ge(self.sems[name], self.cnt[name] if val is None else val)

    def flush(self, eng, inst, name="esync"):
        s, n = self.inc(name, 1)
        inst.then_inc(s, 1)
        eng.wait_ge(s, n)


def build():
    kb = KB()
    nc = kb.nc
    _scope = [None, 0]

    def sflush(eng, inst):
        if SCAN:
            kb.flush(eng, inst)
        return inst

    def phase(name):
        if _scope[0] is not None:
            nc.leave_named_scope(_scope[0], _scope[1], False)
            _scope[0] = None
        if name is not None:
            _scope[1], _ = nc.enter_named_scope(name, False)
            _scope[0] = name

    x_in = nc.dram_tensor("x", [T, D], F32, kind="ExternalInput").ap()
    wkvqab_in = nc.dram_tensor("wkvqab", [D, 776], F32, kind="ExternalInput").ap()
    bab_in = nc.dram_tensor("bab", [1, 8], F32, kind="ExternalInput").ap()
    wg_in = nc.dram_tensor("wgm", [D, 256], F32, kind="ExternalInput").ap()
    wo_in = nc.dram_tensor("wom", [256, D], F32, kind="ExternalInput").ap()
    gn_in = nc.dram_tensor("gnm", [1, 256], F32, kind="ExternalInput").ap()
    wgate_in = nc.dram_tensor("wgate", [I3 // 8, D], F32, kind="ExternalInput").ap()
    wup_in = nc.dram_tensor("wup", [I3 // 8, D], F32, kind="ExternalInput").ap()
    wdown_in = nc.dram_tensor("wdown", [D // 8, I3], F32, kind="ExternalInput").ap()
    mlow_in = nc.dram_tensor("mlow", [128, 128], F32, kind="ExternalInput").ap()
    tri_in = nc.dram_tensor("tri", [128, 64], F32, kind="ExternalInput").ap()
    mupp_in = nc.dram_tensor("mupp", [128, 128], F32, kind="ExternalInput").ap()
    mupi_in = nc.dram_tensor("mupi", [128, 128], F32, kind="ExternalInput").ap()
    out_ext = nc.dram_tensor("out", [T, D], F32, kind="ExternalOutput").ap()
    if DEBUG:
        dbg_kt = nc.dram_tensor("dbg_kt", [S, 256], BF16, kind="ExternalOutput").ap()
        dbg_qt = nc.dram_tensor("dbg_qt", [S, 256], BF16, kind="ExternalOutput").ap()
        dbg_vt = nc.dram_tensor("dbg_vt", [S, 256], BF16, kind="ExternalOutput").ap()
        dbg_sc = nc.dram_tensor("dbg_sc", [S, 16], F32, kind="ExternalOutput").ap()
        dbg_stk = nc.dram_tensor("dbg_stk", [128, 2 * 6 * NCH], F32,
                                 kind="ExternalOutput").ap()
        dbg_y = nc.dram_tensor("dbg_y", [S, 256], F32, kind="ExternalOutput").ap()
        dbg_kf = nc.dram_tensor("dbg_kf", [256, S], BF16, kind="ExternalOutput").ap()
        dbg_kw = nc.dram_tensor("dbg_kw", [128, 4096], BF16, kind="ExternalOutput").ap()
        dbg_ffn = nc.dram_tensor("dbg_ffn", [T, D], F32, kind="ExternalOutput").ap()
        dbg_sg = nc.dram_tensor("dbg_sg", [S, 256], F32, kind="ExternalOutput").ap()
        dbg_rstd = nc.dram_tensor("dbg_rstd", [S, 1], F32, kind="ExternalOutput").ap()
        dbg_atr = nc.dram_tensor("dbg_atr", [T, D], BF16, kind="ExternalOutput").ap()
        dbg_hq = nc.dram_tensor("dbg_hq", [I3, T], BF16, kind="ExternalOutput").ap()
        dbg_sq = nc.dram_tensor("dbg_sq", [S, 1], F32, kind="ExternalOutput").ap()
        dbg_atp = nc.dram_tensor("dbg_atp", [S, D], BF16, kind="ExternalOutput").ap()

    dt = nc.dram_tensor
    ag_xnf_i = dt("ag_xnf_i", [D, T], BF16)
    ag_xnf_o = dt("ag_xnf_o", [4 * D, T], BF16)
    wqg_i = dt("wqg_i", [I3 // 8, D], BF16)
    wqu_i = dt("wqu_i", [I3 // 8, D], BF16)
    wqd_i = dt("wqd_i", [D // 8, I3], BF16)
    wqg_o = dt("wqg_o", [I3, D], BF16)
    wqu_o = dt("wqu_o", [I3, D], BF16)
    wqd_o = dt("wqd_o", [D, I3], BF16)
    hqT_d = dt("hqT_d", [I3, T], BF16)
    ffn_d = dt("ffn_d", [T, D], F32)
    sg_d = dt("sg_d", [S, 256], F32)
    sq_i = dt("sq_i", [S, 1], F32)
    sq_o = dt("sq_o", [S, 1], F32)
    atp_d = dt("atp_d", [S, D], BF16)
    atr_d = dt("atr_d", [T, D], BF16)
    kt_d = dt("kt_d", [S, 256], BF16)
    qt_d = dt("qt_d", [S, 256], BF16)
    vt_d = dt("vt_d", [S, 256], BF16)
    var_d = dt("var_d", [S, 768], BF16)
    scal_d = dt("scal_d", [S, 16], F32)
    scal2_d = dt("scal2_d", [S, 32], F32)
    gct_d = dt("gct_d", [128, 8], F32)
    y_my = dt("y_my", [S, 256], F32)


    A = nc.alloc_sbuf_tensor
    xt = A("xt", [128, I3], F32)
    xnt = A("xnt", [128, D], F32)
    xnb = A("xnb", [128, D], BF16)
    stat = A("stat", [128, 8], F32)
    q_fa = A("q_fa", [128, 1024], F32)
    q_fb = A("q_fb", [128, 1024], F32)
    q_fc = A("q_fc", [128, 1024], F32)
    q_fd = A("q_fd", [128, 1024], F32)
    q_fs = A("q_fs", [128, 1024], F32)
    q_m8 = A("q_m8", [128, 1024], U8)
    q_s1 = A("q_s1", [128, 64], F32)
    q_s2 = A("q_s2", [128, 64], F32)
    q_s3 = A("q_s3", [128, 64], F32)
    qb = A("qb", [128, 1024], BF16)
    tp1 = A("tp1", [128, 128], BF16)
    tp2 = A("tp2", [128, 128], BF16)
    xqf_r = A("xqf_r", [128, KC * T], BF16)
    wgu = A("wgu", [128, KC * 2 * ISL], BF16)
    wdq = A("wdq", [128, NI128 * 1024], BF16)
    wbig = A("wbig", [128, KC * 1024], BF16)
    hqf_t = A("hqf", [128, 2 * 128], BF16)
    hqf = [hqf_t[:, 0:128], hqf_t[:, 128:256]]
    gu_s = A("gu_s", [128, 2 * ISL], F32)
    hfa = A("hfa", [128, 2 * ISL], F32)
    yrow_t = A("yrow", [128, 2 * D], F32)
    yrow = [yrow_t[:, 0:D], yrow_t[:, D:2 * D]]
    ps = [nc.alloc_psum_tensor(f"ps{i}", [128, 512], F32) for i in range(8)]
    if ATTN:
        wkv_bf = A("wkv_bf", [128, KC * 776], BF16)
        bab_bf = A("bab_bf", [1, 8], BF16)
        ktile = A("ktile", [128, 776], BF16)
        abt = A("abt", [128, 8], F32)
        sctile = A("sctile", [128, 16], F32)
        vart = A("vart", [128, 768], BF16)
        rawh = A("rawh", [128, 1024], F32)
        der = A("der", [128, 4 * 8 * 64], F32)
        tmp8 = A("tmp8", [128, 8 * 64], F32)
        rowm = A("rowm", [1, 256], F32)
        tri64 = A("tri64", [128, 64], F32)
        ones1 = A("ones1", [1, 128], F32)
        mlow = A("mlows", [128, 128], F32)
        mupp = A("mupps", [128, 128], F32)
        mupi = A("mupis", [128, 128], F32)
        onesb = A("onesb", [1, 128], BF16)
        trow = A("trow", [128, 10 * 128], BF16)
        stk = A("stk", [128, 2 * 6 * NCH], F32)
        kfs = A("kfs", [128, 2 * 5 * C], BF16)
        kts = A("kts", [128, 2 * 128], BF16)
        vts = A("vts", [128, 2 * C], BF16)
        Lb = A("Lb", [128, 128], BF16)
        LTb = A("LTb", [128, 128], BF16)
        MTb = A("MTb", [128, 128], BF16)
        Pw = A("Pw", [128, 2 * 128], BF16)
        PwT = A("PwT", [128, 2 * 128], BF16)
        Yw = A("Yw", [128, 2 * C], BF16)
        if SCAN:
            _yws = [A("Ywa", [128, C], BF16), A("Ywb", [128, C], BF16)]

            def ywsl(i):
                return _yws[i][:]
        else:
            def ywsl(i):
                return Yw[:, i * 64:(i + 1) * 64]
        Usb = A("Usb", [128, C], BF16)
        S0 = A("S0", [128, 4 * C], BF16)
        evs = A("evs", [128, 128], F32)
        evs2 = A("evs2", [128, 128], F32)
        ytile = A("ytile", [128, 128], F32)
        n2sq = q_fa

        gnbt = gu_s
        sgt2 = q_fd

    x_t = x_in.rearrange("(n p) d -> n p d", p=128)
    xnf_stage = qb[:, 0:KC * 128]
    xnf_stage2 = hfa.bitcast(BF16)[:, 0:KC * 128]
    ffn_t = ffn_d.ap().rearrange("(n p) d -> n p d", p=128)
    sg_t = sg_d.ap().rearrange("(n p) d -> n p d", p=128)
    out_t = out_ext.rearrange("(n p) d -> n p d", p=128)

    def q_sign(s, src_ap, rows, cols):
        s.activation(q_fs[0:rows, 0:cols], src_ap, AF.Sign)

    retq = [None]

    def q_main_nosign(v, src_ap, out_ap, rows, cols):
        q_main(v, src_ap, out_ap, rows, cols)

    def q_main(v, src_ap, out_ap, rows, cols):
        nb = cols // 16
        a3 = src_ap.rearrange("p (g k) -> p g k", k=16)
        kb.flush(v, v.tensor_reduce(q_s1[0:rows, 0:nb], a3, AX.X, OP.max,
                                    apply_absolute_value=True))
        kb.flush(v, v.tensor_scalar(q_s2[0:rows, 0:nb], q_s1[0:rows, 0:nb],
                                    1.0 / 6.0, 1e-12, OP.mult, OP.max))
        kb.flush(v, v.reciprocal(q_s3[0:rows, 0:nb], q_s2[0:rows, 0:nb]))
        sflush(v, v.tensor_scalar(q_fa[0:rows, 0:cols].bitcast(U32), src_ap.bitcast(U32),
                        0x7FFFFFFF, None, OP.bitwise_and))
        sflush(v, v.tensor_tensor(q_fb[0:rows, 0:cols].rearrange("p (g k) -> p g k", k=16),
                        q_fa[0:rows, 0:cols].rearrange("p (g k) -> p g k", k=16),
                        q_s3[0:rows, 0:nb].unsqueeze(2).broadcast_to([rows, nb, 16]),
                        OP.mult))
        sflush(v, v.tensor_scalar(q_fa[0:rows, 0:cols].bitcast(U32),
                        q_fb[0:rows, 0:cols].bitcast(U32), 0x001FFFFF, None, OP.add))
        sflush(v, v.tensor_scalar(q_fc[0:rows, 0:cols].bitcast(U32),
                        q_fa[0:rows, 0:cols].bitcast(U32), 0xFFC00000, None,
                        OP.bitwise_and))
        sflush(v, v.tensor_scalar(q_fa[0:rows, 0:cols].bitcast(U32),
                        q_fb[0:rows, 0:cols].bitcast(U32), 0x003FFFFF, None, OP.add))
        sflush(v, v.tensor_scalar(q_fd[0:rows, 0:cols].bitcast(U32),
                        q_fa[0:rows, 0:cols].bitcast(U32), 0xFF800000, None,
                        OP.bitwise_and))
        sflush(v, v.tensor_scalar(q_fa[0:rows, 0:cols], q_fd[0:rows, 0:cols], 0.5, None, OP.max))
        sflush(v, v.tensor_scalar(q_m8[0:rows, 0:cols], q_fb[0:rows, 0:cols], 1.0, None, OP.is_lt))
        sflush(v, v.select(q_fd[0:rows, 0:cols], q_m8[0:rows, 0:cols],
                 q_fa[0:rows, 0:cols], q_fc[0:rows, 0:cols], add_drain=True))
        sflush(v, v.tensor_scalar(q_fa[0:rows, 0:cols], q_fb[0:rows, 0:cols], 0.25, None, OP.is_gt))
        sflush(v, v.tensor_tensor(q_fb[0:rows, 0:cols], q_fd[0:rows, 0:cols],
                        q_fa[0:rows, 0:cols], OP.mult))
        sflush(v, v.tensor_tensor(q_fa[0:rows, 0:cols], q_fb[0:rows, 0:cols],
                        q_fs[0:rows, 0:cols], OP.mult))
        retq[0] = v.tensor_tensor(
            out_ap.rearrange("p (g k) -> p g k", k=16),
            q_fa[0:rows, 0:cols].rearrange("p (g k) -> p g k", k=16),
            q_s2[0:rows, 0:nb].unsqueeze(2).broadcast_to([rows, nb, 16]),
            OP.mult)

    phase('p1')
    for it in range(NT):
        with nc.Block() as blk:
            @blk.sync
            def _(sync, it=it):
                kb.dma(sync, xt[:, 0:D], x_t[it])
                kb.wait(sync, "dml")
        with nc.Block() as blk:
            @blk.vector
            def _(v):
                kb.flush(v, v.tensor_reduce(stat[:, 0:1], xt[:, 0:D], AX.X, OP.add))
                kb.flush(v, v.tensor_scalar(stat[:, 0:1], stat[:, 0:1], 1.0 / D,
                                            None, OP.mult))
            @blk.scalar
            def _(s):
                s.activation(q_fa[:, 0:D], xt[:, 0:D], AF.Square)
        with nc.Block() as blk:
            @blk.vector
            def _(v):
                kb.flush(v, v.tensor_reduce(stat[:, 1:2], q_fa[:, 0:D], AX.X, OP.add))
                kb.flush(v, v.tensor_scalar(stat[:, 1:2], stat[:, 1:2], 1.0 / D,
                                            None, OP.mult))
                kb.flush(v, v.tensor_tensor(stat[:, 2:3], stat[:, 0:1], stat[:, 0:1],
                                            OP.mult))
                kb.flush(v, v.tensor_tensor(stat[:, 3:4], stat[:, 1:2], stat[:, 2:3],
                                            OP.subtract))
                kb.flush(v, v.tensor_scalar(stat[:, 3:4], stat[:, 3:4], 1e-5,
                                            None, OP.add))
        with nc.Block() as blk:
            @blk.scalar
            def _(s):
                s.activation(stat[:, 4:5], stat[:, 3:4], AF.Sqrt)
        with nc.Block() as blk:
            @blk.vector
            def _(v):
                kb.flush(v, v.reciprocal(stat[:, 5:6], stat[:, 4:5]))
                kb.flush(v, v.tensor_tensor(stat[:, 6:7], stat[:, 0:1], stat[:, 5:6],
                                            OP.mult))
                kb.flush(v, v.tensor_scalar(stat[:, 6:7], stat[:, 6:7], -1.0,
                                            None, OP.mult))
        with nc.Block() as blk:
            @blk.vector
            def _(v):
                kb.flush(v, v.tensor_scalar(stat[:, 7:8], stat[:, 0:1], -1.0,
                                            None, OP.mult))
                v.tensor_scalar(xnt[:], xt[:, 0:D], stat[:, 7:8], stat[:, 5:6],
                                OP.add, OP.mult)
        with nc.Block() as blk:
            @blk.vector
            def _(v):
                v.tensor_copy(xnb[:], xnt[:])
            @blk.scalar
            def _(s):
                q_sign(s, xnt[:], 128, D)
        with nc.Block() as blk:
            @blk.vector
            def _(v):
                q_main(v, xnt[:], qb[:, 0:D], 128, D)
        with nc.Block() as blk:
            @blk.sync
            def _(sync, it=it):
                for kc in range(KC):
                    kb.dma(sync, tp1[:], xnb[:, kc * 128:(kc + 1) * 128],
                           name="dtr", transpose=True)
                    kb.wait(sync, "dtr")
                    kb.dma(sync, ag_xnf_i.ap()[kc * 128:(kc + 1) * 128,
                                               it * 128:(it + 1) * 128], tp1[:])
                    kb.wait(sync, "dml")
                    kb.dma(sync, xqf_r[:, kc * T + it * 128:kc * T + (it + 1) * 128],
                           qb[:, kc * 128:(kc + 1) * 128], name="dtr", transpose=True)
                    kb.wait(sync, "dtr")

    phase('wq')
    with nc.Block() as blk:
        @blk.gpsimd
        def _(g):
            cs, _ = kb.inc("coll", 1)
            g.collective_compute("AllGather", OP.bypass,
                                 replica_groups=[[0, 1, 2, 3], [4, 5, 6, 7]],
                                 ins=[ag_xnf_i.ap().opt()],
                                 outs=[ag_xnf_o.ap().opt()]).then_inc(cs, 1)

    for (win, wout, rows, cols) in ((wgate_in, wqg_i, I3 // 8, D),
                                    (wup_in, wqu_i, I3 // 8, D),
                                    (wdown_in, wqd_i, D // 8, I3)):
        nrt = (rows + 127) // 128
        ncs = (cols + 1023) // 1024
        for rt in range(nrt):
            r0 = rt * 128
            pr = min(128, rows - r0)
            with nc.Block() as blk:
                @blk.sync
                def _(sync, win=win, r0=r0, pr=pr, cols=cols):
                    kb.dma(sync, xt[0:pr, 0:cols], win[r0:r0 + pr, :])
                    kb.wait(sync, "dml")
            for csl in range(ncs):
                c0 = csl * 1024
                pc = min(1024, cols - c0)
                with nc.Block() as blk:
                    @blk.scalar
                    def _(s, r0=r0, pr=pr, c0=c0, pc=pc):
                        q_sign(s, xt[0:pr, c0:c0 + pc], pr, pc)
                with nc.Block() as blk:
                    @blk.vector
                    def _(v, r0=r0, pr=pr, c0=c0, pc=pc):
                        q_main(v, xt[0:pr, c0:c0 + pc], qb[0:pr, 0:pc], pr, pc)
                with nc.Block() as blk:
                    @blk.sync
                    def _(sync, wout=wout, r0=r0, pr=pr, c0=c0, pc=pc):
                        kb.dma(sync, wout.ap()[r0:r0 + pr, c0:c0 + pc], qb[0:pr, 0:pc])
                        kb.wait(sync, "dml")
    with nc.Block() as blk:
        @blk.gpsimd
        def _(g):
            for (i_, o_) in ((wqg_i, wqg_o), (wqu_i, wqu_o), (wqd_i, wqd_o)):
                cs, _ = kb.inc("coll", 1)
                g.collective_compute("AllGather", OP.bypass,
                                     replica_groups=[list(range(8))],
                                     ins=[i_.ap().opt()],
                                     outs=[o_.ap().opt()]).then_inc(cs, 1)
            g.wait_ge(kb.sems["coll"], kb.cnt["coll"])

    phase('f1')
    with nc.Block() as blk:
        @blk.gpsimd
        def _(g):
            g.wait_ge(kb.sems["coll"], kb.cnt["coll"])

    kb.sem("f1m"); kb.sem("f1a"); kb.sem("f1h"); kb.sem("f1g"); kb.sem("f1q")
    kb.sem("f1s")
    for isl in range(NIS):
        i0 = isl * ISL
        with nc.Block() as blk:
            @blk.sync
            def _(sync, i0=i0):
                for kc in range(KC):
                    for (srcw, off) in ((wqg_o, 0), (wqu_o, ISL)):
                        for j in range(3):
                            w0 = j * 128
                            wN = min(128, ISL - w0)
                            kb.dma(sync, wgu[:, kc * 2 * ISL + off + w0:
                                            kc * 2 * ISL + off + w0 + wN],
                                   srcw.ap()[i0 + w0:i0 + w0 + wN,
                                             kc * 128:(kc + 1) * 128],
                                   name="p4w", transpose=True)
                kb.wait(sync, "p4w")
        kb.sem("f1so"); kb.sem("f1t0"); kb.sem("f1t1"); kb.sem("f1t2")
        c0 = {k: kb.cnt[k] for k in ("f1m", "f1a", "f1h", "f1g", "f1q", "f1s",
                                     "f1so", "f1t0", "f1t1", "f1t2")}
        with nc.Block() as blk:
            @blk.tensor
            def _(te, isl=isl, c0=c0):
                for it in range(NT):
                    pi = it % 2
                    if it >= 2:
                        te.wait_ge(kb.sems["f1h"], c0["f1h"] + it - 1)
                    for kc in range(KC):
                        lhs = xqf_r[:, kc * T + it * 128:kc * T + (it + 1) * 128]
                        te.matmul(ps[pi][:, 0:ISL], lhs,
                                  wgu[:, kc * 2 * ISL:kc * 2 * ISL + ISL],
                                  start=(kc == 0), stop=(kc == KC - 1))
                        mm = te.matmul(ps[2 + pi][:, 0:ISL], lhs,
                                       wgu[:, kc * 2 * ISL + ISL:
                                           kc * 2 * ISL + 2 * ISL],
                                       start=(kc == 0), stop=(kc == KC - 1))
                    mm.then_inc(kb.sems["f1m"], 1)
            @blk.scalar
            def _(s, isl=isl, c0=c0):
                for it in range(NT):
                    pi = it % 2
                    s.wait_ge(kb.sems["f1m"], c0["f1m"] + it + 1)
                    if it >= 2:
                        s.wait_ge(kb.sems["f1h"], c0["f1h"] + it - 1)
                    s.activation(gu_s[:, pi * ISL:(pi + 1) * ISL], ps[pi][:, 0:ISL],
                                 AF.Silu).then_inc(kb.sems["f1a"], 1)
                    s.wait_ge(kb.sems["f1h"], c0["f1h"] + it + 1)
                    s.activation(q_fs[:, 0:ISL], hfa[:, pi * ISL:(pi + 1) * ISL],
                                 AF.Sign).then_inc(kb.sems["f1g"], 1)
            @blk.vector
            def _(v, isl=isl, c0=c0):
                for it in range(NT):
                    pi = it % 2
                    v.wait_ge(kb.sems["f1a"], c0["f1a"] + it + 1)
                    v.tensor_tensor(hfa[:, pi * ISL:(pi + 1) * ISL],
                                    gu_s[:, pi * ISL:(pi + 1) * ISL],
                                    ps[2 + pi][:, 0:ISL], OP.mult)\
                        .then_inc(kb.sems["f1h"], 1)
                    v.wait_ge(kb.sems["f1g"], c0["f1g"] + it + 1)
                    if it >= 2:
                        v.wait_ge(kb.sem("f1so"), c0["f1so"] + it - 1)
                    q_main_nosign(v, hfa[:, pi * ISL:(pi + 1) * ISL],
                                  qb[:, pi * 512:pi * 512 + ISL], 128, ISL)
                    retq[0].then_inc(kb.sems["f1q"], 1)
            @blk.sync
            def _(sync, isl=isl, c0=c0):
                for it in range(NT):
                    pi = it % 2
                    st = it % 3
                    sync.wait_ge(kb.sems["f1q"], c0["f1q"] + it + 1)
                    if it >= 3:
                        sync.wait_ge(kb.sem("f1so"), c0["f1so"] + it - 2)
                    for c in range(3):
                        sync.dma_start(
                            trow[:, st * 384 + c * 128:st * 384 + (c + 1) * 128],
                            qb[:, pi * 512 + c * 128:pi * 512 + (c + 1) * 128],
                            transpose=True)\
                            .then_inc(kb.sem(f"f1t{st}"), 16)
            @blk.gpsimd
            def _(g, isl=isl, i0=i0, c0=c0):
                for it in range(NT):
                    st = it % 3
                    g.wait_ge(kb.sem(f"f1t{st}"), c0[f"f1t{st}"] + (it // 3 + 1) * 48)
                    for c in range(3):
                        rr = 96 if c == 2 else 128
                        g.dma_start(hqT_d.ap()[i0 + c * 128:i0 + c * 128 + rr,
                                               it * 128:(it + 1) * 128],
                                    trow[0:rr, st * 384 + c * 128:
                                         st * 384 + c * 128 + 128])\
                            .then_inc(kb.sems["f1s"], 16)
                    g.wait_ge(kb.sems["f1s"], c0["f1s"] + (it + 1) * 48)
                    g.sem_inc(kb.sem("f1so"), 1)
        nt3 = [(NT + 2 - i) // 3 for i in range(3)]
        for k, d in (("f1m", NT), ("f1a", NT), ("f1h", NT), ("f1g", NT),
                     ("f1q", NT), ("f1s", NT * 48), ("f1so", NT),
                     ("f1t0", nt3[0] * 48), ("f1t1", nt3[1] * 48),
                     ("f1t2", nt3[2] * 48)):
            kb.cnt[k] = c0[k] + d

    phase('p5')
    with nc.Block() as blk:
        @blk.sync
        def _(sync):
            for kc2 in range(NI128):
                for j in range(KC):
                    kb.dma(sync, wdq[:, kc2 * 1024 + j * 128:kc2 * 1024 + (j + 1) * 128],
                           wqd_o.ap()[j * 128:(j + 1) * 128,
                                      kc2 * 128:(kc2 + 1) * 128],
                           name="p5w", transpose=True)
            kb.wait(sync, "p5w")

    kb.sem("p5mm"); kb.sem("p5ev"); kb.sem("p5st")
    with nc.Block() as blk:
        @blk.sync
        def _(sync):
            for it in range(NT):
                for kc2 in range(NI128):
                    i = it * NI128 + kc2
                    if i >= 4:
                        kb.wait(sync, "p5mm", (i - 3))
                    sync.dma_start(qb[:, (i % 4) * 128:(i % 4 + 1) * 128],
                                   hqT_d.ap()[kc2 * 128:(kc2 + 1) * 128,
                                              it * 128:(it + 1) * 128])\
                        .then_inc(kb.sem(f"p5l{i % 4}"), 16)
        @blk.tensor
        def _(te):
            for it in range(NT):
                for kc2 in range(NI128):
                    i = it * NI128 + kc2
                    te.wait_ge(kb.sem(f"p5l{i % 4}"), 16 * (i // 4 + 1))
                    lhs = qb[:, (i % 4) * 128:(i % 4 + 1) * 128]
                    te.matmul(ps[2][:, 0:512], lhs, wdq[:, kc2 * 1024:kc2 * 1024 + 512],
                              start=(kc2 == 0), stop=(kc2 == NI128 - 1))
                    te.matmul(ps[3][:, 0:512], lhs,
                              wdq[:, kc2 * 1024 + 512:(kc2 + 1) * 1024],
                              start=(kc2 == 0), stop=(kc2 == NI128 - 1))\
                        .then_inc(kb.sems["p5mm"], 1)
                te.wait_ge(kb.sems["p5ev"], it + 1)
        @blk.vector
        def _(v):
            for it in range(NT):
                v.wait_ge(kb.sems["p5mm"], (it + 1) * NI128)
                if it >= 2:
                    v.wait_ge(kb.sem(f"p5st{it % 2}"), 16 * (it // 2))
                v.tensor_copy(yrow[it % 2][:, 0:512], ps[2][:, 0:512])
                v.tensor_copy(yrow[it % 2][:, 512:1024], ps[3][:, 0:512])\
                    .then_inc(kb.sems["p5ev"], 1)
        @blk.scalar
        def _(s):
            for it in range(NT):
                s.wait_ge(kb.sems["p5ev"], it + 1)
                s.dma_start(ffn_t[it], yrow[it % 2][:])\
                    .then_inc(kb.sem(f"p5st{it % 2}"), 16)
            s.wait_ge(kb.sem("p5st0"), (NT // 2) * 16)
            s.wait_ge(kb.sem("p5st1"), (NT // 2) * 16)
    kb.cnt["p5mm"] = NT * NI128
    kb.cnt["p5ev"] = NT
    kb.cnt["p5st"] = NT * 16

    if ATTN:
        phase('a0')
        with nc.Block() as blk:
            @blk.sync
            def _(sync):
                kb.dma(sync, mlow[:], mlow_in)
                kb.dma(sync, mupp[:], mupp_in)
                kb.dma(sync, mupi[:], mupi_in)
                kb.wait(sync, "dml")
            @blk.vector
            def _(v):
                v.memset(onesb[:], 1.0)
                v.memset(S0[:], 0.0)
        for kc in range(KC):
            with nc.Block() as blk:
                @blk.sync
                def _(sync, kc=kc):
                    kb.dma(sync, xt[:, 0:776],
                           wkvqab_in.rearrange("(n p) d -> n p d", p=128)[kc])
                    kb.wait(sync, "dml")
            with nc.Block() as blk:
                @blk.vector
                def _(v, kc=kc):
                    v.tensor_copy(wkv_bf[:, kc * 776:(kc + 1) * 776], xt[:, 0:776])
        with nc.Block() as blk:
            @blk.sync
            def _(sync):
                kb.dma(sync, stat[0:1, 0:8], bab_in)
                kb.wait(sync, "dml")
        with nc.Block() as blk:
            @blk.vector
            def _(v):
                v.tensor_copy(bab_bf[:], stat[0:1, 0:8])

        phase('a1')
        kb.sem("a1l"); kb.sem("a1m"); kb.sem("a1e"); kb.sem("a1s"); kb.sem("a1g")
        with nc.Block() as blk:
            @blk.sync
            def _(sync):
                for tt in range(NTF):
                    if tt >= 2:
                        sync.wait_ge(kb.sems["a1m"], tt - 1)
                    buf = xnf_stage if tt % 2 == 0 else xnf_stage2
                    for kc in range(KC):
                        sync.dma_start(
                            buf[:, kc * 128:(kc + 1) * 128],
                            ag_xnf_o.ap()[(tt // 16) * D + kc * 128:
                                          (tt // 16) * D + (kc + 1) * 128,
                                          (tt % 16) * 128:(tt % 16) * 128 + 128]
                        ).then_inc(kb.sem(f"a1l{tt % 2}"), 16)
            @blk.scalar
            def _(s):
                for tt in range(NTF):
                    s.wait_ge(kb.sems["a1m"], tt + 1)
                    if tt >= 1:
                        s.wait_ge(kb.sems["a1e"], tt)
                    s.activation(abt[:, 0:8], ps[1][:, 256:264], AF.Sigmoid)\
                        .then_inc(kb.sems["a1g"], 1)
                    s.activation(n2sq[:, 0:256], ps[0][:, 256:512], AF.Square)
                    s.activation(n2sq[:, 256:512], ps[1][:, 0:256], AF.Square)\
                        .then_inc(kb.sems["a1g"], 1)
            @blk.tensor
            def _(te):
                for tt in range(NTF):
                    te.wait_ge(kb.sem(f"a1l{tt % 2}"), (tt // 2 + 1) * KC * 16)
                    if tt >= 1:
                        te.wait_ge(kb.sems["a1e"], tt)
                    buf = xnf_stage if tt % 2 == 0 else xnf_stage2
                    for kc in range(KC):
                        lhs = buf[:, kc * 128:(kc + 1) * 128]
                        te.matmul(ps[0][:, 0:512], lhs,
                                  wkv_bf[:, kc * 776:kc * 776 + 512],
                                  start=(kc == 0), stop=(kc == KC - 1))
                        te.matmul(ps[1][:, 0:264], lhs,
                                  wkv_bf[:, kc * 776 + 512:kc * 776 + 776],
                                  start=(kc == 0), stop=False)
                    te.matmul(ps[1][:, 256:264], onesb[:], bab_bf[:],
                              start=False, stop=True).then_inc(kb.sems["a1m"], 1)
            @blk.vector
            def _(v):
                for tt in range(NTF):
                    v.wait_ge(kb.sems["a1m"], tt + 1)
                    v.wait_ge(kb.sems["a1g"], tt * 2 + 1)
                    if tt >= 1:
                        v.wait_ge(kb.sem("a1so"), tt)
                    for h in range(4):
                        v.tensor_scalar(ktile[:, h * 64:(h + 1) * 64],
                                        ps[0][:, h * 64:(h + 1) * 64],
                                        abt[:, 4 + h:5 + h], None, OP.mult)
                    v.tensor_copy(ktile[:, 256:512], ps[0][:, 256:512])
                    v.tensor_copy(ktile[:, 512:768], ps[1][:, 0:256])
                    v.wait_ge(kb.sems["a1g"], tt * 2 + 2)
                    kb.flush(v, v.tensor_reduce(
                        sctile[:, 8:12],
                        n2sq[:, 0:256].rearrange("p (h e) -> p h e", e=64),
                        AX.X, OP.add))
                    kb.flush(v, v.tensor_reduce(
                        sctile[:, 12:16],
                        n2sq[:, 256:512].rearrange("p (h e) -> p h e", e=64),
                        AX.X, OP.add))
                    v.tensor_copy(sctile[:, 0:8], abt[:, 0:8])\
                        .then_inc(kb.sems["a1e"], 1)
            @blk.gpsimd
            def _(g):
                for tt in range(NTF):
                    g.wait_ge(kb.sems["a1e"], tt + 1)
                    r0 = tt * 128
                    g.dma_start(vt_d.ap()[r0:r0 + 128, :], ktile[:, 0:256])\
                        .then_inc(kb.sems["a1s"], 16)
                    g.dma_start(kt_d.ap()[r0:r0 + 128, :], ktile[:, 256:512])\
                        .then_inc(kb.sems["a1s"], 16)
                    g.dma_start(qt_d.ap()[r0:r0 + 128, :], ktile[:, 512:768])\
                        .then_inc(kb.sems["a1s"], 16)
                    g.dma_start(scal_d.ap()[r0:r0 + 128, :], sctile[:])\
                        .then_inc(kb.sems["a1s"], 16)
                    g.wait_ge(kb.sems["a1s"], (tt + 1) * 64)
                    g.sem_inc(kb.sem("a1so"), 1)
        kb.cnt["a1l"] = NTF * KC * 16
        kb.cnt["a1m"] = NTF
        kb.cnt["a1g"] = NTF * 2
        kb.cnt["a1e"] = NTF
        kb.cnt["a1s"] = NTF * 64
        kb.cnt["a1so"] = NTF

        phase('a2')
        with nc.Block() as blk:
            @blk.sync
            def _(sync):
                kb.dma(sync, tri64[:], tri_in)
                kb.wait(sync, "dml")
            @blk.vector
            def _(v):
                v.memset(ones1[:], 1.0)
        QW, QR, QM, QSR, QSO, QSS, QGC = range(7)

        def dsl(h, q):
            return der[:, (h * 8 + q) * 64:(h * 8 + q) * 64 + 64]

        def tsl(q):
            return tmp8[:, q * 64:(q + 1) * 64]
        TLA, TLB, TLK, TLQ, TLG, TLG0, TREF, TLGC = range(8)
        sc2 = scal_d.ap().rearrange("(cp cpar tok) col -> cpar tok cp col",
                                    cpar=2, tok=64)
        for h in range(4):
            with nc.Block() as blk:
                @blk.sync
                def _(sync, h=h):
                    for cpar in range(2):
                        kb.dma(sync, rawh[cpar * 64:(cpar + 1) * 64, :]
                               .rearrange("p (cp col) -> p cp col", col=16),
                               sc2[cpar], name="a2l")
                    kb.wait(sync, "a2l")
            with nc.Block() as blk:
                @blk.scalar
                def _(s, h=h):
                    rw = rawh[:].rearrange("p (cp col) -> p cp col", col=16)
                    s.activation(tsl(TLA).rearrange("p c -> p c 1" ) if False else
                                 tsl(TLA).rearrange("p (c o) -> p c o", o=1),
                                 rw[:, :, 0 + h:1 + h], AF.Ln)
                    s.activation(tsl(TLB).rearrange("p (c o) -> p c o", o=1),
                                 rw[:, :, 4 + h:5 + h], AF.Ln)
            with nc.Block() as blk:
                @blk.vector
                def _(v, h=h):
                    rw = rawh[:].rearrange("p (cp col) -> p cp col", col=16)
                    sflush(v, v.tensor_scalar(tsl(TLK).rearrange("p (c o) -> p c o", o=1),
                                    rw[:, :, 8 + h:9 + h], 1e-24, None, OP.max))
                    sflush(v, v.tensor_scalar(tsl(TLQ).rearrange("p (c o) -> p c o", o=1),
                                    rw[:, :, 12 + h:13 + h], 1e-24, None, OP.max))
            with nc.Block() as blk:
                @blk.scalar
                def _(s, h=h):
                    s.activation(tsl(TLK), tsl(TLK), AF.Ln)
                    s.activation(tsl(TLQ), tsl(TLQ), AF.Ln)
            with nc.Block() as blk:
                @blk.vector
                def _(v, h=h):
                    sflush(v, v.tensor_scalar(tsl(TLK), tsl(TLK), -0.5, None, OP.mult))
                    sflush(v, v.tensor_scalar(tsl(TLQ), tsl(TLQ), -0.5, None, OP.mult))
            with nc.Block() as blk:
                @blk.tensor
                def _(te, h=h):
                    for cpar in range(2):
                        te.matmul(ps[0][cpar * 64:(cpar + 1) * 64, 0:64],
                                  tri64[cpar * 64:(cpar + 1) * 64, :],
                                  tsl(TLA)[cpar * 64:(cpar + 1) * 64, :],
                                  start=True, stop=True)
            with nc.Block() as blk:
                @blk.vector
                def _(v, h=h):
                    sflush(v, v.tensor_copy(tsl(TLG), ps[0][:, 0:64]))
                    sflush(v, v.tensor_tensor(tsl(TLG0), tsl(TLG), tsl(TLA), OP.subtract))
            with nc.Block() as blk:
                @blk.sync
                def _(sync, h=h):
                    kb.dma(sync, rowm[0:1, 0:64], tsl(TLG0)[32:33, :], name="a2r")
                    kb.dma(sync, rowm[0:1, 64:128], tsl(TLG0)[96:97, :], name="a2r")
                    kb.dma(sync, rowm[0:1, 128:192], tsl(TLG)[63:64, :], name="a2r")
                    kb.dma(sync, rowm[0:1, 192:256], tsl(TLG)[127:128, :], name="a2r")
                    kb.wait(sync, "a2r")
            with nc.Block() as blk:
                @blk.tensor
                def _(te, h=h):
                    for cpar in range(2):
                        te.matmul(ps[1][cpar * 64:(cpar + 1) * 64, 0:64],
                                  ones1[:, 0:64], rowm[0:1, cpar * 64:(cpar + 1) * 64],
                                  start=True, stop=True)
                        te.matmul(ps[2][cpar * 64:(cpar + 1) * 64, 0:64],
                                  ones1[:, 0:64],
                                  rowm[0:1, 128 + cpar * 64:192 + cpar * 64],
                                  start=True, stop=True)
            with nc.Block() as blk:
                @blk.vector
                def _(v, h=h):
                    sflush(v, v.tensor_copy(tsl(TREF), ps[1][:, 0:64]))
                    sflush(v, v.tensor_copy(tsl(TLGC), ps[2][:, 0:64]))
            with nc.Block() as blk:
                @blk.vector
                def _(v, h=h):
                    sflush(v, v.tensor_tensor(dsl(h, QW), tsl(TLK), tsl(TLG), OP.subtract))
                    sflush(v, v.tensor_tensor(dsl(h, QW), dsl(h, QW), tsl(TREF), OP.add))
                    sflush(v, v.tensor_tensor(dsl(h, QSR), tsl(TLG0), tsl(TLB), OP.add))
                    sflush(v, v.tensor_tensor(dsl(h, QSR), dsl(h, QSR), tsl(TLK), OP.add))
                    sflush(v, v.tensor_tensor(dsl(h, QR), dsl(h, QSR), tsl(TREF), OP.subtract))
                    sflush(v, v.tensor_tensor(dsl(h, QSO), tsl(TLG), tsl(TLQ), OP.add))
                    sflush(v, v.tensor_tensor(dsl(h, QM), dsl(h, QSO), tsl(TREF), OP.subtract))
                    sflush(v, v.tensor_tensor(dsl(h, QSS), tsl(TLGC), tsl(TLG), OP.subtract))
                    sflush(v, v.tensor_tensor(dsl(h, QSS), dsl(h, QSS), tsl(TLK), OP.add))
                    sflush(v, v.tensor_copy(dsl(h, QGC), tsl(TLGC)))
                    for q in range(7):
                        sflush(v, v.tensor_scalar(dsl(h, q), dsl(h, q), CLIP, None, OP.min))
            with nc.Block() as blk:
                @blk.scalar
                def _(s, h=h):
                    for q in range(7):
                        s.activation(dsl(h, q), dsl(h, q), AF.Exp)
            with nc.Block() as blk:
                @blk.vector
                def _(v, h=h):
                    sflush(v, v.tensor_scalar(dsl(h, QSR), dsl(h, QSR), -1.0, None, OP.mult))
        with nc.Block() as blk:
            @blk.sync
            def _(sync):
                qlist = (QSR, QSO, QSS, QR, QW, QGC)
                for pr in range(2):
                    for qi, qq in enumerate(qlist):
                        for hh in range(2):
                            h = pr * 2 + hh
                            for cpar in range(2):
                                kb.dma(sync,
                                       stk[hh * 64:(hh + 1) * 64,
                                           pr * 768 + qi * 128 + cpar * 64:
                                           pr * 768 + qi * 128 + (cpar + 1) * 64],
                                       dsl(h, qq)[cpar * 64:(cpar + 1) * 64, :],
                                       name="a2k")
                kb.wait(sync, "a2k")

        def stk_col(pr, qi, c):
            return stk[:, pr * 768 + qi * 128 + (c % 2) * 64 + c // 2:
                       pr * 768 + qi * 128 + (c % 2) * 64 + c // 2 + 1]

        phase('a3')
        kb.sem("a3v"); kb.sem("a3s")
        with nc.Block() as blk:
            @blk.sync
            def _(sync):
                def loads(tt):
                    pi = tt % 2
                    sync.dma_start(qb[:, pi * 512:pi * 512 + 256],
                                   kt_d.ap()[tt * 128:(tt + 1) * 128, :])\
                        .then_inc(kb.sem(f"a3l{pi}"), 16)
                    sync.dma_start(qb[:, pi * 512 + 256:pi * 512 + 512],
                                   qt_d.ap()[tt * 128:(tt + 1) * 128, :])\
                        .then_inc(kb.sem(f"a3l{pi}"), 16)
                loads(0)
                for tt in range(NTF):
                    if tt + 1 < NTF:
                        if tt >= 1:
                            sync.wait_ge(kb.sems["a3v"], tt)
                        loads(tt + 1)
            @blk.vector
            def _(v):
                for tt in range(NTF):
                    pi = tt % 2
                    v.wait_ge(kb.sem(f"a3l{pi}"), (tt // 2 + 1) * 32)
                    if tt >= 2:
                        v.wait_ge(kb.sem("a3so"), tt - 1)
                    vdst = vart if pi == 0 else ktile
                    for h in range(4):
                        v.tensor_scalar(vdst[:, h * 64:(h + 1) * 64],
                                        qb[:, pi * 512 + h * 64:
                                           pi * 512 + (h + 1) * 64],
                                        dsl(h, QW)[:, tt:tt + 1], None, OP.mult)
                        v.tensor_scalar(vdst[:, 256 + h * 64:256 + (h + 1) * 64],
                                        qb[:, pi * 512 + h * 64:
                                           pi * 512 + (h + 1) * 64],
                                        dsl(h, QR)[:, tt:tt + 1], None, OP.mult)
                        vv = v.tensor_scalar(vdst[:, 512 + h * 64:512 + (h + 1) * 64],
                                             qb[:, pi * 512 + 256 + h * 64:
                                                pi * 512 + 256 + (h + 1) * 64],
                                             dsl(h, QM)[:, tt:tt + 1], None, OP.mult)
                    vv.then_inc(kb.sems["a3v"], 1)
            @blk.gpsimd
            def _(g):
                for tt in range(NTF):
                    pi = tt % 2
                    g.wait_ge(kb.sems["a3v"], tt + 1)
                    vsrc = vart if pi == 0 else ktile
                    g.dma_start(var_d.ap()[tt * 128:(tt + 1) * 128, :],
                                vsrc[:, 0:768])\
                        .then_inc(kb.sems["a3s"], 16)
                    g.wait_ge(kb.sems["a3s"], (tt + 1) * 16)
                    g.sem_inc(kb.sem("a3so"), 1)
        kb.cnt["a3v"] = NTF
        kb.cnt["a3s"] = NTF * 16
        kb.cnt["a3so"] = NTF

    phase('a4')
    if ATTN:
        NB = 9 * 16
        PG = 15
        ED = 23
        NU = NCH * 2
        kb.sem("dl"); kb.sem("dm"); kb.sem("de"); kb.sem("dy")
        KFV, KWV, KRV, QFV, QMV = range(5)

        def kfsl(pi, var, hh=None):
            basec = (pi * 5 + var) * 64
            if hh is None:
                return kfs[:, basec:basec + 64]
            return kfs[hh * 64:(hh + 1) * 64, basec:basec + 64]

        with nc.Block() as blk:
            @blk.sync
            def _(sync):
                for u in range(NU):
                    c, p = u // 2, u % 2
                    pi = u % 2
                    if u >= 2:
                        sync.wait_ge(kb.sems["dm"], PG * (u - 1))
                        sync.wait_ge(kb.sems["de"], ED * (u - 1))
                    for hh in range(2):
                        sync.dma_start(kts[hh * 64:(hh + 1) * 64,
                                           pi * 128:(pi + 1) * 128],
                                       kt_d.ap()[c * 64:(c + 1) * 64,
                                                 p * 128:(p + 1) * 128])\
                            .then_inc(kb.sem(f"dl{pi}"), 16)
                        sync.dma_start(vts[hh * 64:(hh + 1) * 64,
                                           pi * 64:(pi + 1) * 64],
                                       vt_d.ap()[c * 64:(c + 1) * 64,
                                                 (p * 2 + hh) * 64:
                                                 (p * 2 + hh + 1) * 64])\
                            .then_inc(kb.sem(f"dl{pi}"), 16)

            @blk.scalar
            def _(s):
                for u in range(NU):
                    c, p = u // 2, u % 2
                    pi = u % 2
                    if u >= 2:
                        s.wait_ge(kb.sems["dm"], PG * (u - 1))
                        s.wait_ge(kb.sems["de"], ED * (u - 1))
                    srcs = (
                        (KFV, kt_d.ap()[c * 64:(c + 1) * 64,
                                        p * 128:(p + 1) * 128]),
                        (KWV, var_d.ap()[c * 64:(c + 1) * 64,
                                         p * 128:(p + 1) * 128]),
                        (KRV, var_d.ap()[c * 64:(c + 1) * 64,
                                         256 + p * 128:256 + (p + 1) * 128]),
                        (QFV, qt_d.ap()[c * 64:(c + 1) * 64,
                                        p * 128:(p + 1) * 128]),
                        (QMV, var_d.ap()[c * 64:(c + 1) * 64,
                                         512 + p * 128:512 + (p + 1) * 128]),
                    )
                    for var, srca in srcs:
                        s.dma_start(kfsl(pi, var), srca, transpose=True)\
                            .then_inc(kb.sem(f"dl{pi}"), 16)
            @blk.tensor
            def _(te):
                for u in range(NU):
                    c, p = u // 2, u % 2
                    pi = u % 2
                    cpp = c % 2
                    dm0 = PG * u
                    de0 = ED * u
                    te.wait_ge(kb.sem(f"dl{u % 2}"), NB * (u // 2 + 1))
                    if u >= 1:
                        te.wait_ge(kb.sems["de"], ED * u)
                    for hh in range(2):
                        r = slice(hh * 64, hh * 64 + 64)
                        te.matmul(ps[0][r, hh * 64:hh * 64 + 64],
                                  kfsl(pi, KFV, hh), kfsl(pi, KWV, hh),
                                  start=True, stop=True)
                    for hh in range(2):
                        r = slice(hh * 64, hh * 64 + 64)
                        te.matmul(ps[1][r, hh * 64:hh * 64 + 64],
                                  kfsl(pi, KFV, hh), kfsl(pi, KRV, hh),
                                  start=True, stop=True)
                    for hh in range(2):
                        r = slice(hh * 64, hh * 64 + 64)
                        te.matmul(ps[2][r, hh * 64:hh * 64 + 64],
                                  kfsl(pi, KFV, hh), kfsl(pi, QMV, hh),
                                  start=True, stop=True)\
                            .then_inc(kb.sems["dm"], 1) if hh == 1 else \
                            te.matmul(ps[2][r, hh * 64:hh * 64 + 64],
                                      kfsl(pi, KFV, hh), kfsl(pi, QMV, hh),
                                      start=True, stop=True)
                    for hh in range(2):
                        r = slice(hh * 64, hh * 64 + 64)
                        mm = te.matmul(ps[3][r, 0:64], kfsl(pi, KFV, hh),
                                       S0[r, (p * 2 + cpp) * 64:
                                          (p * 2 + cpp) * 64 + 64],
                                       start=True, stop=True)
                        if hh == 1:
                            mm.then_inc(kb.sems["dm"], 1)
                    te.wait_ge(kb.sems["de"], de0 + 4)
                    te.matmul(ps[6][:, 0:64], LTb[:], ywsl(0),
                              start=True, stop=True).then_inc(kb.sems["dm"], 1)
                    for j in range(1, 6):
                        sl_new = j % 2
                        if j == 1:
                            lp, lpt = Lb[:], LTb[:]
                        else:
                            lp = Pw[:, (1 - sl_new) * 128:(2 - sl_new) * 128]
                            lpt = PwT[:, (1 - sl_new) * 128:(2 - sl_new) * 128]
                        if j >= 2:
                            te.wait_ge(kb.sems["de"], de0 + 4 + 3 * (j - 1))
                        if j < 5:
                            te.matmul(ps[4][:, 0:128], lpt, lp, start=True, stop=True)
                            te.matmul(ps[5][:, 0:128], lp, lpt, start=True,
                                      stop=True).then_inc(kb.sems["dm"], 1)
                        else:
                            te.matmul(ps[5][:, 0:128], lp, lpt, start=True,
                                      stop=True).then_inc(kb.sems["dm"], 1)
                        if j < 5:
                            te.wait_ge(kb.sems["de"], de0 + 4 + 3 * j)
                        else:
                            te.wait_ge(kb.sems["de"], de0 + 4 + 3 * 4 + 2)
                        ysl = j % 2
                        te.matmul(ps[6][:, 0:64],
                                  PwT[:, sl_new * 128:(sl_new + 1) * 128],
                                  ywsl(ysl),
                                  start=True, stop=True).then_inc(kb.sems["dm"], 1)
                    te.wait_ge(kb.sems["de"], de0 + 19)
                    for hh in range(2):
                        r = slice(hh * 64, hh * 64 + 64)
                        te.matmul(ps[0][r, 0:64], kfsl(pi, QFV, hh),
                                  S0[r, (p * 2 + cpp) * 64:(p * 2 + cpp) * 64 + 64],
                                  start=True, stop=True)
                    te.matmul(ps[1][:, 0:64], MTb[:], ywsl(0),
                              start=True, stop=True).then_inc(kb.sems["dm"], 1)
                    te.wait_ge(kb.sems["de"], de0 + 20)
                    for hh in range(2):
                        r = slice(hh * 64, hh * 64 + 64)
                        mm = te.matmul(ps[2][r, 0:64],
                                       kts[r, pi * 128 + hh * 64:
                                           pi * 128 + hh * 64 + 64],
                                       Usb[r, :], start=True, stop=True)
                        if hh == 1:
                            mm.then_inc(kb.sems["dm"], 1)

            @blk.vector
            def _(v):
                for u in range(NU):
                    c, p = u // 2, u % 2
                    pi = u % 2
                    cpp = c % 2
                    dm0 = PG * u
                    de0 = ED * u
                    v.wait_ge(kb.sems["dm"], dm0 + 1)
                    sflush(v, v.tensor_tensor(evs[:], ps[0][:, 0:128], mlow[:], OP.mult))
                    v.tensor_scalar(Lb[:], evs[:], stk_col(p, 3, c), None, OP.mult)\
                        .then_inc(kb.sems["de"], 1)
                    sflush(v, v.tensor_tensor(evs2[:], ps[1][:, 0:128], mupp[:], OP.mult))
                    v.tensor_scalar(LTb[:], evs2[:], stk_col(p, 4, c), None, OP.mult)\
                        .then_inc(kb.sems["de"], 1)
                    sflush(v, v.tensor_tensor(evs[:], ps[2][:, 0:128], mupi[:], OP.mult))
                    v.tensor_scalar(MTb[:], evs[:], stk_col(p, 4, c), None, OP.mult)\
                        .then_inc(kb.sems["de"], 1)
                    v.wait_ge(kb.sems["dm"], dm0 + 2)
                    v.wait_ge(kb.sem(f"dl{u % 2}"), NB * (u // 2 + 1))
                    v.scalar_tensor_tensor(ywsl(0), ps[3][:, 0:64],
                                           stk_col(p, 0, c),
                                           vts[:, pi * 64:(pi + 1) * 64],
                                           OP.mult, OP.add)\
                        .then_inc(kb.sems["de"], 1)
                    v.wait_ge(kb.sems["dm"], dm0 + 3)
                    if SCAN:
                        v.wait_ge(kb.sems["de"], de0 + 4)
                    v.tensor_tensor(ywsl(1), ywsl(0), ps[6][:, 0:64],
                                    OP.subtract).then_inc(kb.sems["de"], 1)
                    for j in range(1, 6):
                        sl_new = j % 2
                        v.wait_ge(kb.sems["dm"], dm0 + 3 + 2 * (j - 1) + 1)
                        if j < 5:
                            v.tensor_copy(Pw[:, sl_new * 128:(sl_new + 1) * 128],
                                          ps[4][:, 0:128]).then_inc(kb.sems["de"], 1)
                        v.tensor_copy(PwT[:, sl_new * 128:(sl_new + 1) * 128],
                                      ps[5][:, 0:128]).then_inc(kb.sems["de"], 1)
                        v.wait_ge(kb.sems["dm"], dm0 + 3 + 2 * (j - 1) + 2)
                        if SCAN:
                            v.wait_ge(kb.sems["de"], de0 + 2 + 3 * j)
                        ydst = (j + 1) % 2
                        v.tensor_tensor(ywsl(ydst),
                                        ywsl(j % 2),
                                        ps[6][:, 0:64], OP.add)\
                            .then_inc(kb.sems["de"], 1)
                    if SCAN:
                        v.wait_ge(kb.sems["de"], de0 + 19)
                    v.tensor_scalar(Usb[:], ywsl(0), stk_col(p, 2, c), None,
                                    OP.mult).then_inc(kb.sems["de"], 1)
                    v.wait_ge(kb.sems["dm"], dm0 + 14)
                    if u >= 2:
                        v.wait_ge(kb.sem("dyo"), u - 1)
                    v.tensor_scalar(evs[:, 0:64], ps[0][:, 0:64],
                                    stk_col(p, 1, c), None, OP.mult)\
                        .then_inc(kb.sems["de"], 1)
                    if SCAN:
                        v.wait_ge(kb.sems["de"], de0 + 21)
                    v.tensor_tensor(ytile[:, pi * 64:(pi + 1) * 64], evs[:, 0:64],
                                    ps[1][:, 0:64], OP.add)\
                        .then_inc(kb.sems["de"], 1)
                    v.wait_ge(kb.sems["dm"], dm0 + 15)
                    if SCAN:
                        v.wait_ge(kb.sems["de"], de0)
                    v.scalar_tensor_tensor(
                        S0[:, (p * 2 + 1 - cpp) * 64:(p * 2 + 1 - cpp) * 64 + 64],
                        S0[:, (p * 2 + cpp) * 64:(p * 2 + cpp) * 64 + 64],
                        stk_col(p, 5, c), ps[2][:, 0:64], OP.mult, OP.add)\
                        .then_inc(kb.sems["de"], 1)

            @blk.gpsimd
            def _(g):
                for u in range(NU):
                    c, p = u // 2, u % 2
                    pi = u % 2
                    g.wait_ge(kb.sems["de"], ED * u + 22)
                    for hh in range(2):
                        g.dma_start(y_my.ap()[c * 64:(c + 1) * 64,
                                              (p * 2 + hh) * 64:(p * 2 + hh + 1) * 64],
                                    ytile[hh * 64:(hh + 1) * 64,
                                          pi * 64:(pi + 1) * 64])\
                            .then_inc(kb.sems["dy"], 16)
                    g.wait_ge(kb.sems["dy"], 32 * (u + 1))
                    g.sem_inc(kb.sem("dyo"), 1)
        kb.cnt["dl"] = NB * NU
        kb.cnt["dm"] = PG * NU
        kb.cnt["de"] = ED * NU
        kb.cnt["dy"] = 32 * NU
        kb.cnt["dyo"] = NU

        if DEBUG:
            with nc.Block() as blk:
                @blk.sync
                def _(sync):
                    for ri, (dsrc, ddst) in enumerate(
                            ((kt_d, dbg_kt), (qt_d, dbg_qt), (vt_d, dbg_vt))):
                        for r in range(16):
                            kb.dma(sync, qb[:, 0:256],
                                   dsrc.ap()[r * 128:(r + 1) * 128, 0:256],
                                   name="dbg")
                            kb.wait(sync, "dbg")
                            kb.dma(sync, ddst[r * 128:(r + 1) * 128, 0:256],
                                   qb[:, 0:256], name="dbg")
                            kb.wait(sync, "dbg")
                    for r in range(16):
                        kb.dma(sync, qb[:, 0:256],
                               ag_xnf_o.ap()[0:128, r * 128:r * 128 + 256]
                               if r < 8 else
                               var_d.ap()[(r - 8) * 128:(r - 8) * 128 + 128,
                                          0:256],
                               name="dbg")
                        kb.wait(sync, "dbg")
                        kb.dma(sync, dbg_kf[0:128, r * 256:r * 256 + 256],
                               qb[:, 0:256], name="dbg")
                        kb.wait(sync, "dbg")
                    kb.dma(sync, dbg_kw[0:128, 0:776], wkv_bf[:, 0:776], name="dbg")
                    kb.wait(sync, "dbg")
                    kb.dma(sync, dbg_kw[0:128, 1024:1024 + 1024],
                           ag_xnf_i.ap()[0:128, 0:1024], name="dbg")
                    kb.wait(sync, "dbg")
                    for r in range(NTF):
                        kb.dma(sync, xt[:, 0:256], y_my.ap()[r * 128:(r + 1) * 128, :],
                               name="dbg")
                        kb.wait(sync, "dbg")
                        kb.dma(sync, dbg_y[r * 128:(r + 1) * 128, :], xt[:, 0:256],
                               name="dbg")
                        kb.wait(sync, "dbg")
                    for r in range(NTF):
                        kb.dma(sync, xt[:, 0:16], scal_d.ap()[r * 128:(r + 1) * 128, :],
                               name="dbg")
                        kb.wait(sync, "dbg")
                        kb.dma(sync, dbg_sc[r * 128:(r + 1) * 128, :], xt[:, 0:16],
                               name="dbg")
                        kb.wait(sync, "dbg")
                    kb.dma(sync, dbg_stk[:], stk[:], name="dbg")
                    kb.wait(sync, "dbg")


        phase('b5')
        for kc in range(KC):
            with nc.Block() as blk:
                @blk.sync
                def _(sync, kc=kc):
                    kb.dma(sync, xt[:, 0:256],
                           wg_in.rearrange("(n p) d -> n p d", p=128)[kc])
                    kb.wait(sync, "dml")
            with nc.Block() as blk:
                @blk.vector
                def _(v, kc=kc):
                    v.tensor_copy(wbig[:, kc * 256:(kc + 1) * 256], xt[:, 0:256])
        sg_t2 = sg_d.ap().rearrange("(n p) d -> n p d", p=128)
        kb.sem("b5l"); kb.sem("b5m"); kb.sem("b5e"); kb.sem("b5s")
        with nc.Block() as blk:
            @blk.sync
            def _(sync):
                for tt in range(NTF):
                    if tt >= 2:
                        sync.wait_ge(kb.sems["b5m"], tt - 1)
                    for kc in range(KC):
                        sync.dma_start(
                            xqf_r[:, (tt % 2) * 1024 + kc * 128:
                                  (tt % 2) * 1024 + (kc + 1) * 128],
                            ag_xnf_o.ap()[(tt // 16) * D + kc * 128:
                                          (tt // 16) * D + (kc + 1) * 128,
                                          (tt % 16) * 128:(tt % 16) * 128 + 128])\
                            .then_inc(kb.sem(f"b5l{tt % 2}"), 16)
            @blk.tensor
            def _(te):
                for tt in range(NTF):
                    te.wait_ge(kb.sem(f"b5l{tt % 2}"), (tt // 2 + 1) * KC * 16)
                    if tt >= 2:
                        te.wait_ge(kb.sems["b5e"], tt - 1)
                    for kc in range(KC):
                        mm = te.matmul(ps[tt % 2][:, 0:256],
                                       xqf_r[:, (tt % 2) * 1024 + kc * 128:
                                             (tt % 2) * 1024 + (kc + 1) * 128],
                                       wbig[:, kc * 256:(kc + 1) * 256],
                                       start=(kc == 0), stop=(kc == KC - 1))
                    mm.then_inc(kb.sems["b5m"], 1)
            @blk.scalar
            def _(s):
                def ysq_sq(u):
                    su = u % 4
                    s.wait_ge(kb.sem(f"b6q{su}"), (u // 4 + 1) * 16)
                    if u >= 4:
                        s.wait_ge(kb.sem("b6e2"), u - 3)
                    s.activation(q_fa[:, su * 256:(su + 1) * 256],
                                 q_fb[:, su * 256:(su + 1) * 256],
                                 AF.Square).then_inc(kb.sem("b6a2"), 1)

                def ysq_st(u):
                    su = u % 4
                    s.wait_ge(kb.sem("b6e2"), u + 1)
                    s.dma_start(sq_i.ap()[u * 128:(u + 1) * 128, :],
                                stat[:, su:su + 1])\
                        .then_inc(kb.sem(f"b6st{su}"), 16)
                for tt in range(NTF):
                    s.wait_ge(kb.sems["b5m"], tt + 1)
                    if tt >= 2:
                        s.wait_ge(kb.sem("b5so"), tt - 1)
                    s.activation(xnt[:, (tt % 2) * 256:(tt % 2) * 256 + 256],
                                 ps[tt % 2][:, 0:256], AF.Silu)\
                        .then_inc(kb.sems["b5e"], 1)
                    if tt >= 4:
                        s.wait_ge(kb.sem("b6a2"), tt - 3)
                    s.dma_start(q_fb[:, (tt % 4) * 256:(tt % 4 + 1) * 256],
                                y_my.ap()[tt * 128:(tt + 1) * 128, :])\
                        .then_inc(kb.sem(f"b6q{tt % 4}"), 16)
                    if tt >= 2:
                        ysq_sq(tt - 2)
                    if tt >= 5:
                        ysq_st(tt - 5)
                for u in (NTF - 2, NTF - 1):
                    ysq_sq(u)
                for u in range(NTF - 5, NTF):
                    ysq_st(u)
                for sl in range(4):
                    s.wait_ge(kb.sem(f"b6st{sl}"), (NTF // 4) * 16)
            @blk.gpsimd
            def _(g):
                for tt in range(NTF):
                    g.wait_ge(kb.sems["b5e"], tt + 1)
                    g.dma_start(sg_t2[tt], xnt[:, (tt % 2) * 256:(tt % 2) * 256 + 256])\
                        .then_inc(kb.sems["b5s"], 16)
                    g.wait_ge(kb.sems["b5s"], (tt + 1) * 16)
                    g.sem_inc(kb.sem("b5so"), 1)
            @blk.vector
            def _(v):
                for u in range(NTF):
                    su = u % 4
                    v.wait_ge(kb.sem("b6a2"), u + 1)
                    if u >= 4:
                        v.wait_ge(kb.sem(f"b6st{su}"), (u // 4) * 16)
                    v.tensor_reduce(stat[:, su:su + 1],
                                    q_fa[:, su * 256:(su + 1) * 256],
                                    AX.X, OP.add)\
                        .then_inc(kb.sem("b6e2"), 1)
        kb.cnt["b5l"] = NTF * KC * 16
        kb.cnt["b5m"] = NTF
        kb.cnt["b5e"] = NTF
        kb.cnt["b5s"] = NTF * 16
        kb.cnt["b5so"] = NTF
        phase('b6')
        with nc.Block() as blk:
            @blk.gpsimd
            def _(g):
                cs, _ = kb.inc("coll", 1)
                g.collective_compute("AllReduce", OP.add,
                                     replica_groups=[[0, 1, 2, 3], [4, 5, 6, 7]],
                                     ins=[sq_i.ap().opt()],
                                     outs=[sq_o.ap().opt()]).then_inc(cs, 1)
                g.wait_ge(kb.sems["coll"], kb.cnt["coll"])
        with nc.Block() as blk:
            @blk.sync
            def _(sync):
                kb.dma(sync, q_fb[0:1, 0:256], gn_in)
                kb.wait(sync, "dml")
        with nc.Block() as blk:
            @blk.tensor
            def _(te):
                te.matmul(ps[0][:, 0:256], ones1[:], q_fb[0:1, 0:256],
                          start=True, stop=True)
        with nc.Block() as blk:
            @blk.vector
            def _(v):
                v.tensor_copy(gnbt[:, 0:256], ps[0][:, 0:256])
        for kc2 in range(2):
            with nc.Block() as blk:
                @blk.sync
                def _(sync, kc2=kc2):
                    kb.dma(sync, xt[:, 0:D],
                           wo_in.rearrange("(n p) d -> n p d", p=128)[kc2])
                    kb.wait(sync, "dml")
            with nc.Block() as blk:
                @blk.vector
                def _(v, kc2=kc2):
                    v.tensor_copy(wbig[:, 4096 + kc2 * 1024:4096 + (kc2 + 1) * 1024],
                                  xt[:, 0:D])
        phase('rstd')
        phase('b7')
        atp_t = atp_d.ap().rearrange("(n p) d -> n p d", p=128)
        kb.sem("b7l"); kb.sem("b7z"); kb.sem("b7t"); kb.sem("b7m"); kb.sem("b7e")
        kb.sem("b7s")
        with nc.Block() as blk:
            @blk.sync
            def _(sync):
                def loads7(tt):
                    pi = tt % 2
                    sync.dma_start(yrow[pi][:, 0:256],
                                   y_my.ap()[tt * 128:(tt + 1) * 128, :])\
                        .then_inc(kb.sem(f"b7l{pi}"), 16)
                    sync.dma_start(q_fd[:, pi * 256:pi * 256 + 256], sg_t2[tt])\
                        .then_inc(kb.sem(f"b7l{pi}"), 16)
                    sync.dma_start(stat[:, 4 + pi:5 + pi],
                                   sq_o.ap()[tt * 128:(tt + 1) * 128, :])\
                        .then_inc(kb.sem(f"b7l{pi}"), 16)
                loads7(0)
                for tt in range(NTF):
                    pi = tt % 2
                    if tt + 1 < NTF:
                        if tt >= 1:
                            sync.wait_ge(kb.sems["b7z"], tt)
                        loads7(tt + 1)
                    sync.wait_ge(kb.sems["b7z"], tt + 1)
                    if tt >= 2:
                        sync.wait_ge(kb.sems["b7m"], tt - 1)
                    for j in range(2):
                        sync.dma_start(xqf_r[:, 2048 + pi * 256 + j * 128:
                                             2048 + pi * 256 + (j + 1) * 128],
                                       xnb[:, pi * 256 + j * 128:
                                           pi * 256 + (j + 1) * 128],
                                       transpose=True)\
                            .then_inc(kb.sem(f"b7t{pi}"), 16)
            @blk.vector
            def _(v):
                def zstep(tt):
                    pi = tt % 2
                    v.wait_ge(kb.sem(f"b7l{pi}"), (tt // 2 + 1) * 48)
                    if tt >= 2:
                        v.wait_ge(kb.sem(f"b7t{pi}"), (tt // 2) * 32)
                    v.tensor_scalar(stat[:, 6 + pi:7 + pi],
                                    stat[:, 4 + pi:5 + pi],
                                    1.0 / D, 1e-5, OP.mult, OP.add)\
                        .then_inc(kb.sem("b7v"), 1)
                    kb.flush(v, v.tensor_tensor(q_fa[:, pi * 256:pi * 256 + 256],
                                                q_fd[:, pi * 256:pi * 256 + 256],
                                                gnbt[:, 0:256], OP.mult))
                    v.wait_ge(kb.sem("b7a"), tt + 1)
                    kb.flush(v, v.reciprocal(stat[:, 6 + pi:7 + pi],
                                             stat[:, 6 + pi:7 + pi]))
                    kb.flush(v, v.tensor_scalar(q_fb[:, pi * 256:pi * 256 + 256],
                                                yrow[pi][:, 0:256],
                                                stat[:, 6 + pi:7 + pi],
                                                None, OP.mult))
                    v.tensor_tensor(xnb[:, pi * 256:pi * 256 + 256],
                                    q_fb[:, pi * 256:pi * 256 + 256],
                                    q_fa[:, pi * 256:pi * 256 + 256], OP.mult)\
                        .then_inc(kb.sems["b7z"], 1)

                def estep(tt):
                    pi = tt % 2
                    v.wait_ge(kb.sems["b7m"], tt + 1)
                    if tt >= 2:
                        v.wait_ge(kb.sem("b7so"), tt - 1)
                    v.tensor_copy(q_fc[:, 0:1024].bitcast(BF16)
                                  [:, pi * 1024:pi * 1024 + 512],
                                  ps[4 + 2 * pi][:, 0:512])
                    v.tensor_copy(q_fc[:, 0:1024].bitcast(BF16)
                                  [:, pi * 1024 + 512:pi * 1024 + 1024],
                                  ps[5 + 2 * pi][:, 0:512])\
                        .then_inc(kb.sems["b7e"], 1)
                for tt in range(NTF):
                    zstep(tt)
                    if tt >= 1:
                        estep(tt - 1)
                estep(NTF - 1)
            @blk.tensor
            def _(te):
                for tt in range(NTF):
                    pi = tt % 2
                    te.wait_ge(kb.sem(f"b7t{pi}"), (tt // 2 + 1) * 32)
                    if tt >= 2:
                        te.wait_ge(kb.sems["b7e"], tt - 1)
                    for half in range(2):
                        for j in range(2):
                            mm = te.matmul(ps[4 + 2 * pi + half][:, 0:512],
                                           xqf_r[:, 2048 + pi * 256 + j * 128:
                                                 2048 + pi * 256 + (j + 1) * 128],
                                           wbig[:, 4096 + j * 1024 + half * 512:
                                                4096 + j * 1024 + (half + 1) * 512],
                                           start=(j == 0), stop=(j == 1))
                    mm.then_inc(kb.sems["b7m"], 1)
            @blk.scalar
            def _(s):
                for tt in range(NTF):
                    pi = tt % 2
                    s.wait_ge(kb.sem("b7v"), tt + 1)
                    s.activation(stat[:, 6 + pi:7 + pi], stat[:, 6 + pi:7 + pi],
                                 AF.Sqrt).then_inc(kb.sem("b7a"), 1)
            @blk.gpsimd
            def _(g):
                for tt in range(NTF):
                    pi = tt % 2
                    g.wait_ge(kb.sems["b7e"], tt + 1)
                    g.dma_start(atp_t[tt],
                                q_fc[:, 0:1024].bitcast(BF16)
                                [:, pi * 1024:pi * 1024 + 1024])\
                        .then_inc(kb.sems["b7s"], 16)
                    g.wait_ge(kb.sems["b7s"], (tt + 1) * 16)
                    g.sem_inc(kb.sem("b7so"), 1)
        kb.cnt["b7l"] = NTF * 48
        kb.cnt["b7z"] = NTF
        kb.cnt["b7t"] = NTF * 32
        kb.cnt["b7m"] = NTF
        kb.cnt["b7e"] = NTF
        kb.cnt["b7s"] = NTF * 16
        kb.cnt["b7so"] = NTF
        kb.cnt["b7v"] = NTF
        kb.cnt["b7a"] = NTF
        phase('rs')
        with nc.Block() as blk:
            @blk.gpsimd
            def _(g):
                cs, _ = kb.inc("coll", 1)
                g.collective_compute("ReduceScatter", OP.add,
                                     replica_groups=[[0, 1, 2, 3], [4, 5, 6, 7]],
                                     ins=[atp_d.ap().opt()],
                                     outs=[atr_d.ap().opt()]).then_inc(cs, 1)
                g.wait_ge(kb.sems["coll"], kb.cnt["coll"])
        atr_t = atr_d.ap().rearrange("(n p) d -> n p d", p=128)
        phase('fin')
        for it in range(NT):
            with nc.Block() as blk:
                @blk.sync
                def _(sync, it=it):
                    kb.dma(sync, xt[:, 0:D], x_t[it], name="a5x")
                    kb.dma(sync, yrow[1][:], ffn_t[it], name="a5x")
                    kb.dma(sync, xnb[:, 0:D], atr_t[it], name="a5x")
                    kb.wait(sync, "a5x")
            with nc.Block() as blk:
                @blk.vector
                def _(v):
                    sflush(v, v.tensor_copy(q_fa[:, 0:D], xnb[:, 0:D]))
                    sflush(v, v.tensor_tensor(xnt[:], xt[:, 0:D], yrow[1][:],
                                              OP.add))
                    sflush(v, v.tensor_tensor(xnt[:], xnt[:], q_fa[:, 0:D],
                                              OP.add))
            with nc.Block() as blk:
                @blk.sync
                def _(sync, it=it):
                    kb.dma(sync, out_t[it], xnt[:], name="dst")
                    kb.wait(sync, "dst")

    else:
        for it in range(NT):
            with nc.Block() as blk:
                @blk.sync
                def _(sync, it=it):
                    kb.dma(sync, xt[:, 0:D], x_t[it])
                    kb.dma(sync, yrow[0][:], ffn_t[it])
                    kb.wait(sync, "dml")
            with nc.Block() as blk:
                @blk.vector
                def _(v):
                    sflush(v, v.tensor_tensor(xnt[:], xt[:, 0:D], yrow[0][:],
                                              OP.add))
            with nc.Block() as blk:
                @blk.sync
                def _(sync, it=it):
                    kb.dma(sync, out_t[it], xnt[:], name="dst")
                    kb.wait(sync, "dst")

    phase(None)
    if DEBUG:
        with nc.Block() as blk:
            @blk.sync
            def _(sync):
                kb.dma(sync, dbg_ffn, ffn_d.ap(), name="dbg2")
                kb.dma(sync, dbg_sg, sg_d.ap(), name="dbg2")
                kb.dma(sync, dbg_atr, atr_d.ap(), name="dbg2")
                kb.dma(sync, dbg_hq, hqT_d.ap(), name="dbg2")
                kb.dma(sync, dbg_sq, sq_o.ap(), name="dbg2")
                kb.dma(sync, dbg_atp, atp_d.ap(), name="dbg2")
                kb.wait(sync, "dbg2")

    return kb


def make_consts():
    i = np.arange(128)
    hhp = i // 64
    tp = i % 64
    low = ((hhp[:, None] == hhp[None, :]) & (tp[None, :] < tp[:, None])).astype(np.float32)
    upp = ((hhp[:, None] == hhp[None, :]) & (tp[None, :] > tp[:, None])).astype(np.float32)
    upi = ((hhp[:, None] == hhp[None, :]) & (tp[None, :] >= tp[:, None])).astype(np.float32)
    return low, upp, upi


def kernel(**inputs):
    x = np.ascontiguousarray(np.asarray(inputs["L0_attn_x"], np.float32))
    Wq = np.asarray(inputs["L0_attn_Wq"], np.float32)
    Wk = np.asarray(inputs["L0_attn_Wk"], np.float32)
    Wv = np.asarray(inputs["L0_attn_Wv"], np.float32)
    Wa = np.asarray(inputs["L0_attn_Wa"], np.float32)
    ba = np.asarray(inputs["L0_attn_ba"], np.float32)
    Wb = np.asarray(inputs["L0_attn_Wb"], np.float32)
    bb = np.asarray(inputs["L0_attn_bb"], np.float32)
    Wg = np.ascontiguousarray(np.asarray(inputs["L0_attn_Wg"], np.float32))
    Wo = np.ascontiguousarray(np.asarray(inputs["L0_attn_Wo"], np.float32))
    gn = np.asarray(inputs["L0_attn_g_norm"], np.float32)
    wgate = np.asarray(inputs["L0_ffn_w_gate"], np.float32)
    wup = np.asarray(inputs["L0_ffn_w_up"], np.float32)
    wdown = np.asarray(inputs["L0_ffn_w_down"], np.float32)
    low, upp, upi = make_consts()
    trj = np.tril(np.ones((64, 64), np.float32)).T.copy()
    tri2 = np.ascontiguousarray(np.concatenate([trj, trj], 0))

    kb = build()
    in_maps = []
    for c in range(8):
        b, qq = c // 4, c % 4
        hc = slice(qq * 256, (qq + 1) * 256)
        wkvqab = np.concatenate([Wv[:, hc], Wk[:, hc], Wq[:, hc],
                                 Wa[:, 4 * qq:4 * qq + 4], Wb[:, 4 * qq:4 * qq + 4]], 1)
        bab = np.concatenate([ba[4 * qq:4 * qq + 4], bb[4 * qq:4 * qq + 4]])[None, :]
        in_maps.append({
            "x": np.ascontiguousarray(x[b, qq * T:(qq + 1) * T]),
            "wkvqab": np.ascontiguousarray(wkvqab),
            "bab": np.ascontiguousarray(bab),
            "wgm": np.ascontiguousarray(Wg[:, hc]),
            "wom": np.ascontiguousarray(Wo[qq * 256:(qq + 1) * 256, :]),
            "gnm": np.ascontiguousarray(gn[None, qq * 256:(qq + 1) * 256]),
            "wgate": np.ascontiguousarray(wgate[c * (I3 // 8):(c + 1) * (I3 // 8)]),
            "wup": np.ascontiguousarray(wup[c * (I3 // 8):(c + 1) * (I3 // 8)]),
            "wdown": np.ascontiguousarray(wdown[c * (D // 8):(c + 1) * (D // 8)]),
            "mlow": low, "mupp": upp, "mupi": upi, "tri": tri2,
        })
    res = run_bass_kernel_spmd(kb.nc, in_maps, core_ids=list(range(8)))
    out = np.zeros((2, S, D), np.float32)
    for c in range(8):
        b, qq = c // 4, c % 4
        out[b, qq * T:(qq + 1) * T] = res.results[c]["out"]
    return out




# version 16
# speedup vs baseline: 1.3555x; 1.0815x over previous
import numpy as np
import concourse.bass as bass
import concourse.mybir as mybir
from concourse.bass_utils import run_bass_kernel_spmd

F32 = mybir.dt.float32
BF16 = mybir.dt.bfloat16
U8 = mybir.dt.uint8
U32 = mybir.dt.uint32
AF = mybir.ActivationFunctionType
OP = mybir.AluOpType
AX = mybir.AxisListType

S, D = 8192, 1024
T = 2048
NT = T // 128
NTF = S // 128
KC = D // 128
I3 = 2816
ISL = 352
NIS = I3 // ISL
NI128 = I3 // 128
DH = 64
C = 64
NCH = S // C
CLIP = 60.0

ATTN = True
DEBUG = False
SCAN = False


class KB:
    def __init__(self):
        self.nc = bass.Bass("TRN2", target_bir_lowering=False)
        self.sems = {}
        self.cnt = {}

    def sem(self, name):
        if name not in self.sems:
            self.sems[name] = self.nc.alloc_semaphore(name)
            self.cnt[name] = 0
        return self.sems[name]

    def val(self, name):
        return self.cnt.get(name, 0)

    def inc(self, name, by=16):
        self.sem(name)
        self.cnt[name] += by
        return self.sems[name], self.cnt[name]

    def dma(self, eng, out, in_, name="dml", transpose=False):
        s, _ = self.inc(name)
        eng.dma_start(out, in_, transpose=transpose).then_inc(s, 16)

    def wait(self, eng, name, val=None):
        if name in self.sems:
            eng.wait_ge(self.sems[name], self.cnt[name] if val is None else val)

    def flush(self, eng, inst, name="esync"):
        s, n = self.inc(name, 1)
        inst.then_inc(s, 1)
        eng.wait_ge(s, n)


def build():
    kb = KB()
    nc = kb.nc
    _scope = [None, 0]

    def sflush(eng, inst):
        if SCAN:
            kb.flush(eng, inst)
        return inst

    def phase(name):
        if _scope[0] is not None:
            nc.leave_named_scope(_scope[0], _scope[1], False)
            _scope[0] = None
        if name is not None:
            _scope[1], _ = nc.enter_named_scope(name, False)
            _scope[0] = name

    x_in = nc.dram_tensor("x", [T, D], F32, kind="ExternalInput").ap()
    wkvqab_in = nc.dram_tensor("wkvqab", [D, 776], F32, kind="ExternalInput").ap()
    bab_in = nc.dram_tensor("bab", [1, 8], F32, kind="ExternalInput").ap()
    wg_in = nc.dram_tensor("wgm", [D, 256], F32, kind="ExternalInput").ap()
    wo_in = nc.dram_tensor("wom", [256, D], F32, kind="ExternalInput").ap()
    gn_in = nc.dram_tensor("gnm", [1, 256], F32, kind="ExternalInput").ap()
    wgate_in = nc.dram_tensor("wgate", [I3 // 8, D], F32, kind="ExternalInput").ap()
    wup_in = nc.dram_tensor("wup", [I3 // 8, D], F32, kind="ExternalInput").ap()
    wdown_in = nc.dram_tensor("wdown", [D // 8, I3], F32, kind="ExternalInput").ap()
    mlow_in = nc.dram_tensor("mlow", [128, 128], F32, kind="ExternalInput").ap()
    tri_in = nc.dram_tensor("tri", [128, 64], F32, kind="ExternalInput").ap()
    mupp_in = nc.dram_tensor("mupp", [128, 128], F32, kind="ExternalInput").ap()
    mupi_in = nc.dram_tensor("mupi", [128, 128], F32, kind="ExternalInput").ap()
    out_ext = nc.dram_tensor("out", [T, D], F32, kind="ExternalOutput").ap()
    if DEBUG:
        dbg_kt = nc.dram_tensor("dbg_kt", [S, 256], BF16, kind="ExternalOutput").ap()
        dbg_qt = nc.dram_tensor("dbg_qt", [S, 256], BF16, kind="ExternalOutput").ap()
        dbg_vt = nc.dram_tensor("dbg_vt", [S, 256], BF16, kind="ExternalOutput").ap()
        dbg_sc = nc.dram_tensor("dbg_sc", [S, 16], F32, kind="ExternalOutput").ap()
        dbg_stk = nc.dram_tensor("dbg_stk", [128, 2 * 6 * NCH], F32,
                                 kind="ExternalOutput").ap()
        dbg_y = nc.dram_tensor("dbg_y", [S, 256], F32, kind="ExternalOutput").ap()
        dbg_kf = nc.dram_tensor("dbg_kf", [256, S], BF16, kind="ExternalOutput").ap()
        dbg_kw = nc.dram_tensor("dbg_kw", [128, 4096], BF16, kind="ExternalOutput").ap()
        dbg_ffn = nc.dram_tensor("dbg_ffn", [T, D], F32, kind="ExternalOutput").ap()
        dbg_sg = nc.dram_tensor("dbg_sg", [S, 256], F32, kind="ExternalOutput").ap()
        dbg_rstd = nc.dram_tensor("dbg_rstd", [S, 1], F32, kind="ExternalOutput").ap()
        dbg_atr = nc.dram_tensor("dbg_atr", [T, D], BF16, kind="ExternalOutput").ap()
        dbg_hq = nc.dram_tensor("dbg_hq", [I3, T], BF16, kind="ExternalOutput").ap()
        dbg_sq = nc.dram_tensor("dbg_sq", [S, 1], F32, kind="ExternalOutput").ap()
        dbg_atp = nc.dram_tensor("dbg_atp", [S, D], BF16, kind="ExternalOutput").ap()

    dt = nc.dram_tensor
    ag_xnf_i = dt("ag_xnf_i", [D, T], BF16)
    ag_xnf_o = dt("ag_xnf_o", [4 * D, T], BF16)
    wqg_i = dt("wqg_i", [I3 // 8, D], BF16)
    wqu_i = dt("wqu_i", [I3 // 8, D], BF16)
    wqd_i = dt("wqd_i", [D // 8, I3], BF16)
    wqg_o = dt("wqg_o", [I3, D], BF16)
    wqu_o = dt("wqu_o", [I3, D], BF16)
    wqd_o = dt("wqd_o", [D, I3], BF16)
    hqT_d = dt("hqT_d", [I3, T], BF16)
    ffn_d = dt("ffn_d", [T, D], F32)
    sg_d = dt("sg_d", [S, 256], F32)
    sq_i = dt("sq_i", [S, 1], F32)
    sq_o = dt("sq_o", [S, 1], F32)
    atp_d = dt("atp_d", [S, D], BF16)
    atr_d = dt("atr_d", [T, D], BF16)
    kt_d = dt("kt_d", [S, 256], BF16)
    qt_d = dt("qt_d", [S, 256], BF16)
    vt_d = dt("vt_d", [S, 256], BF16)
    var_d = dt("var_d", [S, 768], BF16)
    scal_d = dt("scal_d", [S, 16], F32)
    scal2_d = dt("scal2_d", [S, 32], F32)
    gct_d = dt("gct_d", [128, 8], F32)
    y_my = dt("y_my", [S, 256], F32)


    A = nc.alloc_sbuf_tensor
    xt = A("xt", [128, I3], F32)
    xnt = A("xnt", [128, D], F32)
    xnb = A("xnb", [128, D], BF16)
    stat = A("stat", [128, 8], F32)
    q_fa = A("q_fa", [128, 1024], F32)
    q_fb = A("q_fb", [128, 1024], F32)
    q_fc = A("q_fc", [128, 1024], F32)
    q_fd = A("q_fd", [128, 1024], F32)
    q_fs = A("q_fs", [128, 1024], F32)
    q_m8 = A("q_m8", [128, 1024], U8)
    q_s1 = A("q_s1", [128, 64], F32)
    q_s2 = A("q_s2", [128, 64], F32)
    q_s3 = A("q_s3", [128, 64], F32)
    qb = A("qb", [128, 1024], BF16)
    tp1 = A("tp1", [128, 128], BF16)
    tp2 = A("tp2", [128, 128], BF16)
    xqf_r = A("xqf_r", [128, KC * T], BF16)
    wgu = A("wgu", [128, KC * 2 * ISL], BF16)
    wdq = A("wdq", [128, NI128 * 1024], BF16)
    wbig = A("wbig", [128, KC * 1024], BF16)
    hqf_t = A("hqf", [128, 2 * 128], BF16)
    hqf = [hqf_t[:, 0:128], hqf_t[:, 128:256]]
    gu_s = A("gu_s", [128, 2 * ISL], F32)
    hfa = A("hfa", [128, 2 * ISL], F32)
    yrow_t = A("yrow", [128, 2 * D], F32)
    yrow = [yrow_t[:, 0:D], yrow_t[:, D:2 * D]]
    ps = [nc.alloc_psum_tensor(f"ps{i}", [128, 512], F32) for i in range(8)]
    if ATTN:
        wkv_bf = A("wkv_bf", [128, KC * 776], BF16)
        bab_bf = A("bab_bf", [1, 8], BF16)
        ktile = A("ktile", [128, 776], BF16)
        abt = A("abt", [128, 8], F32)
        sctile = A("sctile", [128, 16], F32)
        vart = A("vart", [128, 768], BF16)
        rawh = A("rawh", [128, 1024], F32)
        der = A("der", [128, 4 * 8 * 64], F32)
        tmp8 = A("tmp8", [128, 8 * 64], F32)
        rowm = A("rowm", [1, 256], F32)
        tri64 = A("tri64", [128, 64], F32)
        ones1 = A("ones1", [1, 128], F32)
        mlow = A("mlows", [128, 128], F32)
        mupp = A("mupps", [128, 128], F32)
        mupi = A("mupis", [128, 128], F32)
        onesb = A("onesb", [1, 128], BF16)
        trow = A("trow", [128, 10 * 128], BF16)
        stk = A("stk", [128, 2 * 6 * NCH], F32)
        kfs = A("kfs", [128, 2 * 5 * C], BF16)
        kts = A("kts", [128, 2 * 128], BF16)
        vts = A("vts", [128, 2 * C], BF16)
        Lb = A("Lb", [128, 128], BF16)
        LTb = A("LTb", [128, 128], BF16)
        MTb = A("MTb", [128, 128], BF16)
        Pw = A("Pw", [128, 2 * 128], BF16)
        PwT = A("PwT", [128, 2 * 128], BF16)
        Yw = A("Yw", [128, 2 * C], BF16)
        if SCAN:
            _yws = [A("Ywa", [128, C], BF16), A("Ywb", [128, C], BF16)]

            def ywsl(i):
                return _yws[i][:]
        else:
            def ywsl(i):
                return Yw[:, i * 64:(i + 1) * 64]
        Usb = A("Usb", [128, C], BF16)
        S0 = A("S0", [128, 4 * C], BF16)
        evs = A("evs", [128, 128], F32)
        evs2 = A("evs2", [128, 128], F32)
        ytile = A("ytile", [128, 128], F32)
        n2sq = q_fa

        gnbt = gu_s
        sgt2 = q_fd

    x_t = x_in.rearrange("(n p) d -> n p d", p=128)
    xnf_stage = qb[:, 0:KC * 128]
    xnf_stage2 = hfa.bitcast(BF16)[:, 0:KC * 128]
    ffn_t = ffn_d.ap().rearrange("(n p) d -> n p d", p=128)
    sg_t = sg_d.ap().rearrange("(n p) d -> n p d", p=128)
    out_t = out_ext.rearrange("(n p) d -> n p d", p=128)

    def q_sign(s, src_ap, rows, cols):
        s.activation(q_fs[0:rows, 0:cols], src_ap, AF.Sign)

    retq = [None]

    def q_main_nosign(v, src_ap, out_ap, rows, cols):
        q_main(v, src_ap, out_ap, rows, cols)

    def q_main(v, src_ap, out_ap, rows, cols):
        nb = cols // 16
        a3 = src_ap.rearrange("p (g k) -> p g k", k=16)
        kb.flush(v, v.tensor_reduce(q_s1[0:rows, 0:nb], a3, AX.X, OP.max,
                                    apply_absolute_value=True))
        kb.flush(v, v.tensor_scalar(q_s2[0:rows, 0:nb], q_s1[0:rows, 0:nb],
                                    1.0 / 6.0, 1e-12, OP.mult, OP.max))
        kb.flush(v, v.reciprocal(q_s3[0:rows, 0:nb], q_s2[0:rows, 0:nb]))
        sflush(v, v.tensor_scalar(q_fa[0:rows, 0:cols].bitcast(U32), src_ap.bitcast(U32),
                        0x7FFFFFFF, None, OP.bitwise_and))
        sflush(v, v.tensor_tensor(q_fb[0:rows, 0:cols].rearrange("p (g k) -> p g k", k=16),
                        q_fa[0:rows, 0:cols].rearrange("p (g k) -> p g k", k=16),
                        q_s3[0:rows, 0:nb].unsqueeze(2).broadcast_to([rows, nb, 16]),
                        OP.mult))
        sflush(v, v.tensor_scalar(q_fa[0:rows, 0:cols].bitcast(U32),
                        q_fb[0:rows, 0:cols].bitcast(U32), 0x001FFFFF, None, OP.add))
        sflush(v, v.tensor_scalar(q_fc[0:rows, 0:cols].bitcast(U32),
                        q_fa[0:rows, 0:cols].bitcast(U32), 0xFFC00000, None,
                        OP.bitwise_and))
        sflush(v, v.tensor_scalar(q_fa[0:rows, 0:cols].bitcast(U32),
                        q_fb[0:rows, 0:cols].bitcast(U32), 0x003FFFFF, None, OP.add))
        sflush(v, v.tensor_scalar(q_fd[0:rows, 0:cols].bitcast(U32),
                        q_fa[0:rows, 0:cols].bitcast(U32), 0xFF800000, None,
                        OP.bitwise_and))
        sflush(v, v.tensor_scalar(q_fa[0:rows, 0:cols], q_fd[0:rows, 0:cols], 0.5, None, OP.max))
        sflush(v, v.tensor_scalar(q_m8[0:rows, 0:cols], q_fb[0:rows, 0:cols], 1.0, None, OP.is_lt))
        sflush(v, v.select(q_fd[0:rows, 0:cols], q_m8[0:rows, 0:cols],
                 q_fa[0:rows, 0:cols], q_fc[0:rows, 0:cols], add_drain=True))
        sflush(v, v.tensor_scalar(q_fa[0:rows, 0:cols], q_fb[0:rows, 0:cols], 0.25, None, OP.is_gt))
        sflush(v, v.tensor_tensor(q_fb[0:rows, 0:cols], q_fd[0:rows, 0:cols],
                        q_fa[0:rows, 0:cols], OP.mult))
        sflush(v, v.tensor_tensor(q_fa[0:rows, 0:cols], q_fb[0:rows, 0:cols],
                        q_fs[0:rows, 0:cols], OP.mult))
        retq[0] = v.tensor_tensor(
            out_ap.rearrange("p (g k) -> p g k", k=16),
            q_fa[0:rows, 0:cols].rearrange("p (g k) -> p g k", k=16),
            q_s2[0:rows, 0:nb].unsqueeze(2).broadcast_to([rows, nb, 16]),
            OP.mult)

    phase('p1')
    for it in range(NT):
        with nc.Block() as blk:
            @blk.sync
            def _(sync, it=it):
                kb.dma(sync, xt[:, 0:D], x_t[it])
                kb.wait(sync, "dml")
        with nc.Block() as blk:
            @blk.vector
            def _(v):
                kb.flush(v, v.tensor_reduce(stat[:, 0:1], xt[:, 0:D], AX.X, OP.add))
                kb.flush(v, v.tensor_scalar(stat[:, 0:1], stat[:, 0:1], 1.0 / D,
                                            None, OP.mult))
            @blk.scalar
            def _(s):
                s.activation(q_fa[:, 0:D], xt[:, 0:D], AF.Square)
        with nc.Block() as blk:
            @blk.vector
            def _(v):
                kb.flush(v, v.tensor_reduce(stat[:, 1:2], q_fa[:, 0:D], AX.X, OP.add))
                kb.flush(v, v.tensor_scalar(stat[:, 1:2], stat[:, 1:2], 1.0 / D,
                                            None, OP.mult))
                kb.flush(v, v.tensor_tensor(stat[:, 2:3], stat[:, 0:1], stat[:, 0:1],
                                            OP.mult))
                kb.flush(v, v.tensor_tensor(stat[:, 3:4], stat[:, 1:2], stat[:, 2:3],
                                            OP.subtract))
                kb.flush(v, v.tensor_scalar(stat[:, 3:4], stat[:, 3:4], 1e-5,
                                            None, OP.add))
        with nc.Block() as blk:
            @blk.scalar
            def _(s):
                s.activation(stat[:, 4:5], stat[:, 3:4], AF.Sqrt)
        with nc.Block() as blk:
            @blk.vector
            def _(v):
                kb.flush(v, v.reciprocal(stat[:, 5:6], stat[:, 4:5]))
                kb.flush(v, v.tensor_tensor(stat[:, 6:7], stat[:, 0:1], stat[:, 5:6],
                                            OP.mult))
                kb.flush(v, v.tensor_scalar(stat[:, 6:7], stat[:, 6:7], -1.0,
                                            None, OP.mult))
        with nc.Block() as blk:
            @blk.vector
            def _(v):
                kb.flush(v, v.tensor_scalar(stat[:, 7:8], stat[:, 0:1], -1.0,
                                            None, OP.mult))
                v.tensor_scalar(xnt[:], xt[:, 0:D], stat[:, 7:8], stat[:, 5:6],
                                OP.add, OP.mult)
        with nc.Block() as blk:
            @blk.vector
            def _(v):
                v.tensor_copy(xnb[:], xnt[:])
            @blk.scalar
            def _(s):
                q_sign(s, xnt[:], 128, D)
        with nc.Block() as blk:
            @blk.vector
            def _(v):
                q_main(v, xnt[:], qb[:, 0:D], 128, D)
        with nc.Block() as blk:
            @blk.sync
            def _(sync, it=it):
                for kc in range(KC):
                    kb.dma(sync, tp1[:], xnb[:, kc * 128:(kc + 1) * 128],
                           name="dtr", transpose=True)
                    kb.wait(sync, "dtr")
                    kb.dma(sync, ag_xnf_i.ap()[kc * 128:(kc + 1) * 128,
                                               it * 128:(it + 1) * 128], tp1[:])
                    kb.wait(sync, "dml")
                    kb.dma(sync, xqf_r[:, kc * T + it * 128:kc * T + (it + 1) * 128],
                           qb[:, kc * 128:(kc + 1) * 128], name="dtr", transpose=True)
                    kb.wait(sync, "dtr")

    phase('wq')
    with nc.Block() as blk:
        @blk.gpsimd
        def _(g):
            cs, _ = kb.inc("coll", 1)
            g.collective_compute("AllGather", OP.bypass,
                                 replica_groups=[[0, 1, 2, 3], [4, 5, 6, 7]],
                                 ins=[ag_xnf_i.ap().opt()],
                                 outs=[ag_xnf_o.ap().opt()]).then_inc(cs, 1)

    for (win, wout, rows, cols) in ((wgate_in, wqg_i, I3 // 8, D),
                                    (wup_in, wqu_i, I3 // 8, D),
                                    (wdown_in, wqd_i, D // 8, I3)):
        nrt = (rows + 127) // 128
        ncs = (cols + 1023) // 1024
        for rt in range(nrt):
            r0 = rt * 128
            pr = min(128, rows - r0)
            with nc.Block() as blk:
                @blk.sync
                def _(sync, win=win, r0=r0, pr=pr, cols=cols):
                    kb.dma(sync, xt[0:pr, 0:cols], win[r0:r0 + pr, :])
                    kb.wait(sync, "dml")
            for csl in range(ncs):
                c0 = csl * 1024
                pc = min(1024, cols - c0)
                with nc.Block() as blk:
                    @blk.scalar
                    def _(s, r0=r0, pr=pr, c0=c0, pc=pc):
                        q_sign(s, xt[0:pr, c0:c0 + pc], pr, pc)
                with nc.Block() as blk:
                    @blk.vector
                    def _(v, r0=r0, pr=pr, c0=c0, pc=pc):
                        q_main(v, xt[0:pr, c0:c0 + pc], qb[0:pr, 0:pc], pr, pc)
                with nc.Block() as blk:
                    @blk.sync
                    def _(sync, wout=wout, r0=r0, pr=pr, c0=c0, pc=pc):
                        kb.dma(sync, wout.ap()[r0:r0 + pr, c0:c0 + pc], qb[0:pr, 0:pc])
                        kb.wait(sync, "dml")
    with nc.Block() as blk:
        @blk.gpsimd
        def _(g):
            for (i_, o_) in ((wqg_i, wqg_o), (wqu_i, wqu_o), (wqd_i, wqd_o)):
                cs, _ = kb.inc("coll", 1)
                g.collective_compute("AllGather", OP.bypass,
                                     replica_groups=[list(range(8))],
                                     ins=[i_.ap().opt()],
                                     outs=[o_.ap().opt()]).then_inc(cs, 1)
            g.wait_ge(kb.sems["coll"], kb.cnt["coll"])

    phase('f1')
    with nc.Block() as blk:
        @blk.gpsimd
        def _(g):
            g.wait_ge(kb.sems["coll"], kb.cnt["coll"])

    kb.sem("f1m"); kb.sem("f1a"); kb.sem("f1h"); kb.sem("f1g"); kb.sem("f1q")
    kb.sem("f1s")
    for isl in range(NIS):
        i0 = isl * ISL
        with nc.Block() as blk:
            @blk.sync
            def _(sync, i0=i0):
                for kc in range(KC):
                    for (srcw, off) in ((wqg_o, 0), (wqu_o, ISL)):
                        for j in range(3):
                            w0 = j * 128
                            wN = min(128, ISL - w0)
                            kb.dma(sync, wgu[:, kc * 2 * ISL + off + w0:
                                            kc * 2 * ISL + off + w0 + wN],
                                   srcw.ap()[i0 + w0:i0 + w0 + wN,
                                             kc * 128:(kc + 1) * 128],
                                   name="p4w", transpose=True)
                kb.wait(sync, "p4w")
        kb.sem("f1so"); kb.sem("f1t0"); kb.sem("f1t1"); kb.sem("f1t2")
        c0 = {k: kb.cnt[k] for k in ("f1m", "f1a", "f1h", "f1g", "f1q", "f1s",
                                     "f1so", "f1t0", "f1t1", "f1t2")}
        with nc.Block() as blk:
            @blk.tensor
            def _(te, isl=isl, c0=c0):
                for it in range(NT):
                    pi = it % 2
                    if it >= 2:
                        te.wait_ge(kb.sems["f1h"], c0["f1h"] + it - 1)
                    for kc in range(KC):
                        lhs = xqf_r[:, kc * T + it * 128:kc * T + (it + 1) * 128]
                        te.matmul(ps[pi][:, 0:ISL], lhs,
                                  wgu[:, kc * 2 * ISL:kc * 2 * ISL + ISL],
                                  start=(kc == 0), stop=(kc == KC - 1))
                        mm = te.matmul(ps[2 + pi][:, 0:ISL], lhs,
                                       wgu[:, kc * 2 * ISL + ISL:
                                           kc * 2 * ISL + 2 * ISL],
                                       start=(kc == 0), stop=(kc == KC - 1))
                    mm.then_inc(kb.sems["f1m"], 1)
            @blk.scalar
            def _(s, isl=isl, c0=c0):
                for it in range(NT):
                    pi = it % 2
                    s.wait_ge(kb.sems["f1m"], c0["f1m"] + it + 1)
                    if it >= 2:
                        s.wait_ge(kb.sems["f1h"], c0["f1h"] + it - 1)
                    s.activation(gu_s[:, pi * ISL:(pi + 1) * ISL], ps[pi][:, 0:ISL],
                                 AF.Silu).then_inc(kb.sems["f1a"], 1)
                    s.wait_ge(kb.sems["f1h"], c0["f1h"] + it + 1)
                    s.activation(q_fs[:, 0:ISL], hfa[:, pi * ISL:(pi + 1) * ISL],
                                 AF.Sign).then_inc(kb.sems["f1g"], 1)
            @blk.vector
            def _(v, isl=isl, c0=c0):
                for it in range(NT):
                    pi = it % 2
                    v.wait_ge(kb.sems["f1a"], c0["f1a"] + it + 1)
                    v.tensor_tensor(hfa[:, pi * ISL:(pi + 1) * ISL],
                                    gu_s[:, pi * ISL:(pi + 1) * ISL],
                                    ps[2 + pi][:, 0:ISL], OP.mult)\
                        .then_inc(kb.sems["f1h"], 1)
                    v.wait_ge(kb.sems["f1g"], c0["f1g"] + it + 1)
                    if it >= 2:
                        v.wait_ge(kb.sem("f1so"), c0["f1so"] + it - 1)
                    q_main_nosign(v, hfa[:, pi * ISL:(pi + 1) * ISL],
                                  qb[:, pi * 512:pi * 512 + ISL], 128, ISL)
                    retq[0].then_inc(kb.sems["f1q"], 1)
            @blk.sync
            def _(sync, isl=isl, c0=c0):
                for it in range(NT):
                    pi = it % 2
                    st = it % 3
                    sync.wait_ge(kb.sems["f1q"], c0["f1q"] + it + 1)
                    if it >= 3:
                        sync.wait_ge(kb.sem("f1so"), c0["f1so"] + it - 2)
                    for c in range(3):
                        sync.dma_start(
                            trow[:, st * 384 + c * 128:st * 384 + (c + 1) * 128],
                            qb[:, pi * 512 + c * 128:pi * 512 + (c + 1) * 128],
                            transpose=True)\
                            .then_inc(kb.sem(f"f1t{st}"), 16)
            @blk.gpsimd
            def _(g, isl=isl, i0=i0, c0=c0):
                for it in range(NT):
                    st = it % 3
                    g.wait_ge(kb.sem(f"f1t{st}"), c0[f"f1t{st}"] + (it // 3 + 1) * 48)
                    for c in range(3):
                        rr = 96 if c == 2 else 128
                        g.dma_start(hqT_d.ap()[i0 + c * 128:i0 + c * 128 + rr,
                                               it * 128:(it + 1) * 128],
                                    trow[0:rr, st * 384 + c * 128:
                                         st * 384 + c * 128 + 128])\
                            .then_inc(kb.sems["f1s"], 16)
                    g.wait_ge(kb.sems["f1s"], c0["f1s"] + (it + 1) * 48)
                    g.sem_inc(kb.sem("f1so"), 1)
        nt3 = [(NT + 2 - i) // 3 for i in range(3)]
        for k, d in (("f1m", NT), ("f1a", NT), ("f1h", NT), ("f1g", NT),
                     ("f1q", NT), ("f1s", NT * 48), ("f1so", NT),
                     ("f1t0", nt3[0] * 48), ("f1t1", nt3[1] * 48),
                     ("f1t2", nt3[2] * 48)):
            kb.cnt[k] = c0[k] + d

    phase('p5')
    with nc.Block() as blk:
        @blk.sync
        def _(sync):
            for kc2 in range(NI128):
                for j in range(KC):
                    kb.dma(sync, wdq[:, kc2 * 1024 + j * 128:kc2 * 1024 + (j + 1) * 128],
                           wqd_o.ap()[j * 128:(j + 1) * 128,
                                      kc2 * 128:(kc2 + 1) * 128],
                           name="p5w", transpose=True)
            kb.wait(sync, "p5w")

    kb.sem("p5mm"); kb.sem("p5ev"); kb.sem("p5st")
    with nc.Block() as blk:
        @blk.sync
        def _(sync):
            for it in range(NT):
                for kc2 in range(NI128):
                    i = it * NI128 + kc2
                    if i >= 4:
                        kb.wait(sync, "p5mm", (i - 3))
                    sync.dma_start(qb[:, (i % 4) * 128:(i % 4 + 1) * 128],
                                   hqT_d.ap()[kc2 * 128:(kc2 + 1) * 128,
                                              it * 128:(it + 1) * 128])\
                        .then_inc(kb.sem(f"p5l{i % 4}"), 16)
        @blk.tensor
        def _(te):
            for it in range(NT):
                for kc2 in range(NI128):
                    i = it * NI128 + kc2
                    te.wait_ge(kb.sem(f"p5l{i % 4}"), 16 * (i // 4 + 1))
                    lhs = qb[:, (i % 4) * 128:(i % 4 + 1) * 128]
                    te.matmul(ps[2][:, 0:512], lhs, wdq[:, kc2 * 1024:kc2 * 1024 + 512],
                              start=(kc2 == 0), stop=(kc2 == NI128 - 1))
                    te.matmul(ps[3][:, 0:512], lhs,
                              wdq[:, kc2 * 1024 + 512:(kc2 + 1) * 1024],
                              start=(kc2 == 0), stop=(kc2 == NI128 - 1))\
                        .then_inc(kb.sems["p5mm"], 1)
                te.wait_ge(kb.sems["p5ev"], it + 1)
        @blk.vector
        def _(v):
            for it in range(NT):
                v.wait_ge(kb.sems["p5mm"], (it + 1) * NI128)
                if it >= 2:
                    v.wait_ge(kb.sem(f"p5st{it % 2}"), 16 * (it // 2))
                v.tensor_copy(yrow[it % 2][:, 0:512], ps[2][:, 0:512])
                v.tensor_copy(yrow[it % 2][:, 512:1024], ps[3][:, 0:512])\
                    .then_inc(kb.sems["p5ev"], 1)
        @blk.scalar
        def _(s):
            for it in range(NT):
                s.wait_ge(kb.sems["p5ev"], it + 1)
                s.dma_start(ffn_t[it], yrow[it % 2][:])\
                    .then_inc(kb.sem(f"p5st{it % 2}"), 16)
            s.wait_ge(kb.sem("p5st0"), (NT // 2) * 16)
            s.wait_ge(kb.sem("p5st1"), (NT // 2) * 16)
    kb.cnt["p5mm"] = NT * NI128
    kb.cnt["p5ev"] = NT
    kb.cnt["p5st"] = NT * 16

    if ATTN:
        phase('a0')
        with nc.Block() as blk:
            @blk.sync
            def _(sync):
                kb.dma(sync, mlow[:], mlow_in)
                kb.dma(sync, mupp[:], mupp_in)
                kb.dma(sync, mupi[:], mupi_in)
                kb.wait(sync, "dml")
            @blk.vector
            def _(v):
                v.memset(onesb[:], 1.0)
                v.memset(S0[:], 0.0)
        for kc in range(KC):
            with nc.Block() as blk:
                @blk.sync
                def _(sync, kc=kc):
                    kb.dma(sync, xt[:, 0:776],
                           wkvqab_in.rearrange("(n p) d -> n p d", p=128)[kc])
                    kb.wait(sync, "dml")
            with nc.Block() as blk:
                @blk.vector
                def _(v, kc=kc):
                    v.tensor_copy(wkv_bf[:, kc * 776:(kc + 1) * 776], xt[:, 0:776])
        with nc.Block() as blk:
            @blk.sync
            def _(sync):
                kb.dma(sync, stat[0:1, 0:8], bab_in)
                kb.wait(sync, "dml")
        with nc.Block() as blk:
            @blk.vector
            def _(v):
                v.tensor_copy(bab_bf[:], stat[0:1, 0:8])

        phase('a1')
        kb.sem("a1l"); kb.sem("a1m"); kb.sem("a1e"); kb.sem("a1s"); kb.sem("a1g")
        with nc.Block() as blk:
            @blk.sync
            def _(sync):
                for tt in range(NTF):
                    if tt >= 2:
                        sync.wait_ge(kb.sems["a1m"], tt - 1)
                    buf = xnf_stage if tt % 2 == 0 else xnf_stage2
                    for kc in range(KC):
                        sync.dma_start(
                            buf[:, kc * 128:(kc + 1) * 128],
                            ag_xnf_o.ap()[(tt // 16) * D + kc * 128:
                                          (tt // 16) * D + (kc + 1) * 128,
                                          (tt % 16) * 128:(tt % 16) * 128 + 128]
                        ).then_inc(kb.sem(f"a1l{tt % 2}"), 16)
            @blk.scalar
            def _(s):
                for tt in range(NTF):
                    s.wait_ge(kb.sems["a1m"], tt + 1)
                    if tt >= 1:
                        s.wait_ge(kb.sems["a1e"], tt)
                    s.activation(abt[:, 0:8], ps[1][:, 256:264], AF.Sigmoid)\
                        .then_inc(kb.sems["a1g"], 1)
                    s.activation(n2sq[:, 0:256], ps[0][:, 256:512], AF.Square)
                    s.activation(n2sq[:, 256:512], ps[1][:, 0:256], AF.Square)\
                        .then_inc(kb.sems["a1g"], 1)
            @blk.tensor
            def _(te):
                for tt in range(NTF):
                    te.wait_ge(kb.sem(f"a1l{tt % 2}"), (tt // 2 + 1) * KC * 16)
                    if tt >= 1:
                        te.wait_ge(kb.sems["a1e"], tt)
                    buf = xnf_stage if tt % 2 == 0 else xnf_stage2
                    for kc in range(KC):
                        lhs = buf[:, kc * 128:(kc + 1) * 128]
                        te.matmul(ps[0][:, 0:512], lhs,
                                  wkv_bf[:, kc * 776:kc * 776 + 512],
                                  start=(kc == 0), stop=(kc == KC - 1))
                        te.matmul(ps[1][:, 0:264], lhs,
                                  wkv_bf[:, kc * 776 + 512:kc * 776 + 776],
                                  start=(kc == 0), stop=False)
                    te.matmul(ps[1][:, 256:264], onesb[:], bab_bf[:],
                              start=False, stop=True).then_inc(kb.sems["a1m"], 1)
            @blk.vector
            def _(v):
                for tt in range(NTF):
                    v.wait_ge(kb.sems["a1m"], tt + 1)
                    v.wait_ge(kb.sems["a1g"], tt * 2 + 1)
                    if tt >= 1:
                        v.wait_ge(kb.sem("a1so"), tt)
                    for h in range(4):
                        v.tensor_scalar(ktile[:, h * 64:(h + 1) * 64],
                                        ps[0][:, h * 64:(h + 1) * 64],
                                        abt[:, 4 + h:5 + h], None, OP.mult)
                    v.tensor_copy(ktile[:, 256:512], ps[0][:, 256:512])
                    v.tensor_copy(ktile[:, 512:768], ps[1][:, 0:256])
                    v.wait_ge(kb.sems["a1g"], tt * 2 + 2)
                    kb.flush(v, v.tensor_reduce(
                        sctile[:, 8:12],
                        n2sq[:, 0:256].rearrange("p (h e) -> p h e", e=64),
                        AX.X, OP.add))
                    kb.flush(v, v.tensor_reduce(
                        sctile[:, 12:16],
                        n2sq[:, 256:512].rearrange("p (h e) -> p h e", e=64),
                        AX.X, OP.add))
                    v.tensor_copy(sctile[:, 0:8], abt[:, 0:8])\
                        .then_inc(kb.sems["a1e"], 1)
            @blk.gpsimd
            def _(g):
                for tt in range(NTF):
                    g.wait_ge(kb.sems["a1e"], tt + 1)
                    r0 = tt * 128
                    g.dma_start(vt_d.ap()[r0:r0 + 128, :], ktile[:, 0:256])\
                        .then_inc(kb.sems["a1s"], 16)
                    g.dma_start(kt_d.ap()[r0:r0 + 128, :], ktile[:, 256:512])\
                        .then_inc(kb.sems["a1s"], 16)
                    g.dma_start(qt_d.ap()[r0:r0 + 128, :], ktile[:, 512:768])\
                        .then_inc(kb.sems["a1s"], 16)
                    g.dma_start(scal_d.ap()[r0:r0 + 128, :], sctile[:])\
                        .then_inc(kb.sems["a1s"], 16)
                    g.wait_ge(kb.sems["a1s"], (tt + 1) * 64)
                    g.sem_inc(kb.sem("a1so"), 1)
        kb.cnt["a1l"] = NTF * KC * 16
        kb.cnt["a1m"] = NTF
        kb.cnt["a1g"] = NTF * 2
        kb.cnt["a1e"] = NTF
        kb.cnt["a1s"] = NTF * 64
        kb.cnt["a1so"] = NTF

        phase('a2')
        with nc.Block() as blk:
            @blk.sync
            def _(sync):
                kb.dma(sync, tri64[:], tri_in)
                kb.wait(sync, "dml")
            @blk.vector
            def _(v):
                v.memset(ones1[:], 1.0)
        QW, QR, QM, QSR, QSO, QSS, QGC = range(7)

        def dsl(h, q):
            return der[:, (h * 8 + q) * 64:(h * 8 + q) * 64 + 64]

        def tsl(q):
            return tmp8[:, q * 64:(q + 1) * 64]
        TLA, TLB, TLK, TLQ, TLG, TLG0, TREF, TLGC = range(8)
        sc2 = scal_d.ap().rearrange("(cp cpar tok) col -> cpar tok cp col",
                                    cpar=2, tok=64)
        for h in range(4):
            with nc.Block() as blk:
                @blk.sync
                def _(sync, h=h):
                    for cpar in range(2):
                        kb.dma(sync, rawh[cpar * 64:(cpar + 1) * 64, :]
                               .rearrange("p (cp col) -> p cp col", col=16),
                               sc2[cpar], name="a2l")
                    kb.wait(sync, "a2l")
            with nc.Block() as blk:
                @blk.scalar
                def _(s, h=h):
                    rw = rawh[:].rearrange("p (cp col) -> p cp col", col=16)
                    s.activation(tsl(TLA).rearrange("p c -> p c 1" ) if False else
                                 tsl(TLA).rearrange("p (c o) -> p c o", o=1),
                                 rw[:, :, 0 + h:1 + h], AF.Ln)
                    s.activation(tsl(TLB).rearrange("p (c o) -> p c o", o=1),
                                 rw[:, :, 4 + h:5 + h], AF.Ln)
            with nc.Block() as blk:
                @blk.vector
                def _(v, h=h):
                    rw = rawh[:].rearrange("p (cp col) -> p cp col", col=16)
                    sflush(v, v.tensor_scalar(tsl(TLK).rearrange("p (c o) -> p c o", o=1),
                                    rw[:, :, 8 + h:9 + h], 1e-24, None, OP.max))
                    sflush(v, v.tensor_scalar(tsl(TLQ).rearrange("p (c o) -> p c o", o=1),
                                    rw[:, :, 12 + h:13 + h], 1e-24, None, OP.max))
            with nc.Block() as blk:
                @blk.scalar
                def _(s, h=h):
                    s.activation(tsl(TLK), tsl(TLK), AF.Ln)
                    s.activation(tsl(TLQ), tsl(TLQ), AF.Ln)
            with nc.Block() as blk:
                @blk.vector
                def _(v, h=h):
                    sflush(v, v.tensor_scalar(tsl(TLK), tsl(TLK), -0.5, None, OP.mult))
                    sflush(v, v.tensor_scalar(tsl(TLQ), tsl(TLQ), -0.5, None, OP.mult))
            with nc.Block() as blk:
                @blk.tensor
                def _(te, h=h):
                    for cpar in range(2):
                        te.matmul(ps[0][cpar * 64:(cpar + 1) * 64, 0:64],
                                  tri64[cpar * 64:(cpar + 1) * 64, :],
                                  tsl(TLA)[cpar * 64:(cpar + 1) * 64, :],
                                  start=True, stop=True)
            with nc.Block() as blk:
                @blk.vector
                def _(v, h=h):
                    sflush(v, v.tensor_copy(tsl(TLG), ps[0][:, 0:64]))
                    sflush(v, v.tensor_tensor(tsl(TLG0), tsl(TLG), tsl(TLA), OP.subtract))
            with nc.Block() as blk:
                @blk.sync
                def _(sync, h=h):
                    kb.dma(sync, rowm[0:1, 0:64], tsl(TLG0)[32:33, :], name="a2r")
                    kb.dma(sync, rowm[0:1, 64:128], tsl(TLG0)[96:97, :], name="a2r")
                    kb.dma(sync, rowm[0:1, 128:192], tsl(TLG)[63:64, :], name="a2r")
                    kb.dma(sync, rowm[0:1, 192:256], tsl(TLG)[127:128, :], name="a2r")
                    kb.wait(sync, "a2r")
            with nc.Block() as blk:
                @blk.tensor
                def _(te, h=h):
                    for cpar in range(2):
                        te.matmul(ps[1][cpar * 64:(cpar + 1) * 64, 0:64],
                                  ones1[:, 0:64], rowm[0:1, cpar * 64:(cpar + 1) * 64],
                                  start=True, stop=True)
                        te.matmul(ps[2][cpar * 64:(cpar + 1) * 64, 0:64],
                                  ones1[:, 0:64],
                                  rowm[0:1, 128 + cpar * 64:192 + cpar * 64],
                                  start=True, stop=True)
            with nc.Block() as blk:
                @blk.vector
                def _(v, h=h):
                    sflush(v, v.tensor_copy(tsl(TREF), ps[1][:, 0:64]))
                    sflush(v, v.tensor_copy(tsl(TLGC), ps[2][:, 0:64]))
            with nc.Block() as blk:
                @blk.vector
                def _(v, h=h):
                    sflush(v, v.tensor_tensor(dsl(h, QW), tsl(TLK), tsl(TLG), OP.subtract))
                    sflush(v, v.tensor_tensor(dsl(h, QW), dsl(h, QW), tsl(TREF), OP.add))
                    sflush(v, v.tensor_tensor(dsl(h, QSR), tsl(TLG0), tsl(TLB), OP.add))
                    sflush(v, v.tensor_tensor(dsl(h, QSR), dsl(h, QSR), tsl(TLK), OP.add))
                    sflush(v, v.tensor_tensor(dsl(h, QR), dsl(h, QSR), tsl(TREF), OP.subtract))
                    sflush(v, v.tensor_tensor(dsl(h, QSO), tsl(TLG), tsl(TLQ), OP.add))
                    sflush(v, v.tensor_tensor(dsl(h, QM), dsl(h, QSO), tsl(TREF), OP.subtract))
                    sflush(v, v.tensor_tensor(dsl(h, QSS), tsl(TLGC), tsl(TLG), OP.subtract))
                    sflush(v, v.tensor_tensor(dsl(h, QSS), dsl(h, QSS), tsl(TLK), OP.add))
                    sflush(v, v.tensor_copy(dsl(h, QGC), tsl(TLGC)))
                    for q in range(7):
                        sflush(v, v.tensor_scalar(dsl(h, q), dsl(h, q), CLIP, None, OP.min))
            with nc.Block() as blk:
                @blk.scalar
                def _(s, h=h):
                    for q in range(7):
                        s.activation(dsl(h, q), dsl(h, q), AF.Exp)
            with nc.Block() as blk:
                @blk.vector
                def _(v, h=h):
                    sflush(v, v.tensor_scalar(dsl(h, QSR), dsl(h, QSR), -1.0, None, OP.mult))
        with nc.Block() as blk:
            @blk.sync
            def _(sync):
                qlist = (QSR, QSO, QSS, QR, QW, QGC)
                for pr in range(2):
                    for qi, qq in enumerate(qlist):
                        for hh in range(2):
                            h = pr * 2 + hh
                            for cpar in range(2):
                                kb.dma(sync,
                                       stk[hh * 64:(hh + 1) * 64,
                                           pr * 768 + qi * 128 + cpar * 64:
                                           pr * 768 + qi * 128 + (cpar + 1) * 64],
                                       dsl(h, qq)[cpar * 64:(cpar + 1) * 64, :],
                                       name="a2k")
                kb.wait(sync, "a2k")

        def stk_col(pr, qi, c):
            return stk[:, pr * 768 + qi * 128 + (c % 2) * 64 + c // 2:
                       pr * 768 + qi * 128 + (c % 2) * 64 + c // 2 + 1]

        phase('a3')
        kb.sem("a3v"); kb.sem("a3s")
        with nc.Block() as blk:
            @blk.sync
            def _(sync):
                def loads(tt):
                    pi = tt % 2
                    sync.dma_start(qb[:, pi * 512:pi * 512 + 256],
                                   kt_d.ap()[tt * 128:(tt + 1) * 128, :])\
                        .then_inc(kb.sem(f"a3l{pi}"), 16)
                    sync.dma_start(qb[:, pi * 512 + 256:pi * 512 + 512],
                                   qt_d.ap()[tt * 128:(tt + 1) * 128, :])\
                        .then_inc(kb.sem(f"a3l{pi}"), 16)
                loads(0)
                for tt in range(NTF):
                    if tt + 1 < NTF:
                        if tt >= 1:
                            sync.wait_ge(kb.sems["a3v"], tt)
                        loads(tt + 1)
            @blk.vector
            def _(v):
                for tt in range(NTF):
                    pi = tt % 2
                    v.wait_ge(kb.sem(f"a3l{pi}"), (tt // 2 + 1) * 32)
                    if tt >= 2:
                        v.wait_ge(kb.sem("a3so"), tt - 1)
                    vdst = vart if pi == 0 else ktile
                    for h in range(4):
                        v.tensor_scalar(vdst[:, h * 64:(h + 1) * 64],
                                        qb[:, pi * 512 + h * 64:
                                           pi * 512 + (h + 1) * 64],
                                        dsl(h, QW)[:, tt:tt + 1], None, OP.mult)
                        v.tensor_scalar(vdst[:, 256 + h * 64:256 + (h + 1) * 64],
                                        qb[:, pi * 512 + h * 64:
                                           pi * 512 + (h + 1) * 64],
                                        dsl(h, QR)[:, tt:tt + 1], None, OP.mult)
                        vv = v.tensor_scalar(vdst[:, 512 + h * 64:512 + (h + 1) * 64],
                                             qb[:, pi * 512 + 256 + h * 64:
                                                pi * 512 + 256 + (h + 1) * 64],
                                             dsl(h, QM)[:, tt:tt + 1], None, OP.mult)
                    vv.then_inc(kb.sems["a3v"], 1)
            @blk.gpsimd
            def _(g):
                for tt in range(NTF):
                    pi = tt % 2
                    g.wait_ge(kb.sems["a3v"], tt + 1)
                    vsrc = vart if pi == 0 else ktile
                    g.dma_start(var_d.ap()[tt * 128:(tt + 1) * 128, :],
                                vsrc[:, 0:768])\
                        .then_inc(kb.sems["a3s"], 16)
                    g.wait_ge(kb.sems["a3s"], (tt + 1) * 16)
                    g.sem_inc(kb.sem("a3so"), 1)
        kb.cnt["a3v"] = NTF
        kb.cnt["a3s"] = NTF * 16
        kb.cnt["a3so"] = NTF

    phase('a4')
    if ATTN:
        NB = 9 * 16
        PG = 15
        ED = 23
        NU = NCH * 2
        kb.sem("dl"); kb.sem("dm"); kb.sem("de"); kb.sem("dy")
        KFV, KWV, KRV, QFV, QMV = range(5)

        def kfsl(pi, var, hh=None):
            basec = (pi * 5 + var) * 64
            if hh is None:
                return kfs[:, basec:basec + 64]
            return kfs[hh * 64:(hh + 1) * 64, basec:basec + 64]

        with nc.Block() as blk:
            @blk.sync
            def _(sync):
                for u in range(NU):
                    c, p = u // 2, u % 2
                    pi = u % 2
                    if u >= 2:
                        sync.wait_ge(kb.sems["dm"], PG * (u - 1))
                        sync.wait_ge(kb.sems["de"], ED * (u - 1))
                    for hh in range(2):
                        sync.dma_start(kts[hh * 64:(hh + 1) * 64,
                                           pi * 128:(pi + 1) * 128],
                                       kt_d.ap()[c * 64:(c + 1) * 64,
                                                 p * 128:(p + 1) * 128])\
                            .then_inc(kb.sem(f"dl{pi}"), 16)
                        sync.dma_start(vts[hh * 64:(hh + 1) * 64,
                                           pi * 64:(pi + 1) * 64],
                                       vt_d.ap()[c * 64:(c + 1) * 64,
                                                 (p * 2 + hh) * 64:
                                                 (p * 2 + hh + 1) * 64])\
                            .then_inc(kb.sem(f"dl{pi}"), 16)

            @blk.scalar
            def _(s):
                for u in range(NU):
                    c, p = u // 2, u % 2
                    pi = u % 2
                    if u >= 2:
                        s.wait_ge(kb.sems["dm"], PG * (u - 1))
                        s.wait_ge(kb.sems["de"], ED * (u - 1))
                    srcs = (
                        (KFV, kt_d.ap()[c * 64:(c + 1) * 64,
                                        p * 128:(p + 1) * 128]),
                        (KWV, var_d.ap()[c * 64:(c + 1) * 64,
                                         p * 128:(p + 1) * 128]),
                        (KRV, var_d.ap()[c * 64:(c + 1) * 64,
                                         256 + p * 128:256 + (p + 1) * 128]),
                        (QFV, qt_d.ap()[c * 64:(c + 1) * 64,
                                        p * 128:(p + 1) * 128]),
                        (QMV, var_d.ap()[c * 64:(c + 1) * 64,
                                         512 + p * 128:512 + (p + 1) * 128]),
                    )
                    for var, srca in srcs:
                        s.dma_start(kfsl(pi, var), srca, transpose=True)\
                            .then_inc(kb.sem(f"dl{pi}"), 16)
            @blk.tensor
            def _(te):
                for u in range(NU):
                    c, p = u // 2, u % 2
                    pi = u % 2
                    cpp = c % 2
                    dm0 = PG * u
                    de0 = ED * u
                    te.wait_ge(kb.sem(f"dl{u % 2}"), NB * (u // 2 + 1))
                    if u >= 1:
                        te.wait_ge(kb.sems["de"], ED * u)
                    for hh in range(2):
                        r = slice(hh * 64, hh * 64 + 64)
                        te.matmul(ps[0][r, hh * 64:hh * 64 + 64],
                                  kfsl(pi, KFV, hh), kfsl(pi, KWV, hh),
                                  start=True, stop=True)
                    for hh in range(2):
                        r = slice(hh * 64, hh * 64 + 64)
                        te.matmul(ps[1][r, hh * 64:hh * 64 + 64],
                                  kfsl(pi, KFV, hh), kfsl(pi, KRV, hh),
                                  start=True, stop=True)
                    for hh in range(2):
                        r = slice(hh * 64, hh * 64 + 64)
                        te.matmul(ps[2][r, hh * 64:hh * 64 + 64],
                                  kfsl(pi, KFV, hh), kfsl(pi, QMV, hh),
                                  start=True, stop=True)\
                            .then_inc(kb.sems["dm"], 1) if hh == 1 else \
                            te.matmul(ps[2][r, hh * 64:hh * 64 + 64],
                                      kfsl(pi, KFV, hh), kfsl(pi, QMV, hh),
                                      start=True, stop=True)
                    for hh in range(2):
                        r = slice(hh * 64, hh * 64 + 64)
                        mm = te.matmul(ps[3][r, 0:64], kfsl(pi, KFV, hh),
                                       S0[r, (p * 2 + cpp) * 64:
                                          (p * 2 + cpp) * 64 + 64],
                                       start=True, stop=True)
                        if hh == 1:
                            mm.then_inc(kb.sems["dm"], 1)
                    te.wait_ge(kb.sems["de"], de0 + 4)
                    te.matmul(ps[6][:, 0:64], LTb[:], ywsl(0),
                              start=True, stop=True).then_inc(kb.sems["dm"], 1)
                    for j in range(1, 6):
                        sl_new = j % 2
                        if j == 1:
                            lp, lpt = Lb[:], LTb[:]
                        else:
                            lp = Pw[:, (1 - sl_new) * 128:(2 - sl_new) * 128]
                            lpt = PwT[:, (1 - sl_new) * 128:(2 - sl_new) * 128]
                        if j >= 2:
                            te.wait_ge(kb.sems["de"], de0 + 4 + 3 * (j - 1))
                        if j < 5:
                            te.matmul(ps[4][:, 0:128], lpt, lp, start=True, stop=True)
                            te.matmul(ps[5][:, 0:128], lp, lpt, start=True,
                                      stop=True).then_inc(kb.sems["dm"], 1)
                        else:
                            te.matmul(ps[5][:, 0:128], lp, lpt, start=True,
                                      stop=True).then_inc(kb.sems["dm"], 1)
                        if j < 5:
                            te.wait_ge(kb.sems["de"], de0 + 4 + 3 * j)
                        else:
                            te.wait_ge(kb.sems["de"], de0 + 4 + 3 * 4 + 2)
                        ysl = j % 2
                        te.matmul(ps[6][:, 0:64],
                                  PwT[:, sl_new * 128:(sl_new + 1) * 128],
                                  ywsl(ysl),
                                  start=True, stop=True).then_inc(kb.sems["dm"], 1)
                    te.wait_ge(kb.sems["de"], de0 + 19)
                    for hh in range(2):
                        r = slice(hh * 64, hh * 64 + 64)
                        te.matmul(ps[0][r, 0:64], kfsl(pi, QFV, hh),
                                  S0[r, (p * 2 + cpp) * 64:(p * 2 + cpp) * 64 + 64],
                                  start=True, stop=True)
                    te.matmul(ps[1][:, 0:64], MTb[:], ywsl(0),
                              start=True, stop=True).then_inc(kb.sems["dm"], 1)
                    te.wait_ge(kb.sems["de"], de0 + 20)
                    for hh in range(2):
                        r = slice(hh * 64, hh * 64 + 64)
                        mm = te.matmul(ps[2][r, 0:64],
                                       kts[r, pi * 128 + hh * 64:
                                           pi * 128 + hh * 64 + 64],
                                       Usb[r, :], start=True, stop=True)
                        if hh == 1:
                            mm.then_inc(kb.sems["dm"], 1)

            @blk.vector
            def _(v):
                for u in range(NU):
                    c, p = u // 2, u % 2
                    pi = u % 2
                    cpp = c % 2
                    dm0 = PG * u
                    de0 = ED * u
                    v.wait_ge(kb.sems["dm"], dm0 + 1)
                    sflush(v, v.tensor_tensor(evs[:], ps[0][:, 0:128], mlow[:], OP.mult))
                    v.tensor_scalar(Lb[:], evs[:], stk_col(p, 3, c), None, OP.mult)\
                        .then_inc(kb.sems["de"], 1)
                    sflush(v, v.tensor_tensor(evs2[:], ps[1][:, 0:128], mupp[:], OP.mult))
                    v.tensor_scalar(LTb[:], evs2[:], stk_col(p, 4, c), None, OP.mult)\
                        .then_inc(kb.sems["de"], 1)
                    sflush(v, v.tensor_tensor(evs[:], ps[2][:, 0:128], mupi[:], OP.mult))
                    v.tensor_scalar(MTb[:], evs[:], stk_col(p, 4, c), None, OP.mult)\
                        .then_inc(kb.sems["de"], 1)
                    v.wait_ge(kb.sems["dm"], dm0 + 2)
                    v.wait_ge(kb.sem(f"dl{u % 2}"), NB * (u // 2 + 1))
                    v.scalar_tensor_tensor(ywsl(0), ps[3][:, 0:64],
                                           stk_col(p, 0, c),
                                           vts[:, pi * 64:(pi + 1) * 64],
                                           OP.mult, OP.add)\
                        .then_inc(kb.sems["de"], 1)
                    v.wait_ge(kb.sems["dm"], dm0 + 3)
                    if SCAN:
                        v.wait_ge(kb.sems["de"], de0 + 4)
                    v.tensor_tensor(ywsl(1), ywsl(0), ps[6][:, 0:64],
                                    OP.subtract).then_inc(kb.sems["de"], 1)
                    for j in range(1, 6):
                        sl_new = j % 2
                        v.wait_ge(kb.sems["dm"], dm0 + 3 + 2 * (j - 1) + 1)
                        if j < 5:
                            v.tensor_copy(Pw[:, sl_new * 128:(sl_new + 1) * 128],
                                          ps[4][:, 0:128]).then_inc(kb.sems["de"], 1)
                        v.tensor_copy(PwT[:, sl_new * 128:(sl_new + 1) * 128],
                                      ps[5][:, 0:128]).then_inc(kb.sems["de"], 1)
                        v.wait_ge(kb.sems["dm"], dm0 + 3 + 2 * (j - 1) + 2)
                        if SCAN:
                            v.wait_ge(kb.sems["de"], de0 + 2 + 3 * j)
                        ydst = (j + 1) % 2
                        v.tensor_tensor(ywsl(ydst),
                                        ywsl(j % 2),
                                        ps[6][:, 0:64], OP.add)\
                            .then_inc(kb.sems["de"], 1)
                    if SCAN:
                        v.wait_ge(kb.sems["de"], de0 + 19)
                    v.tensor_scalar(Usb[:], ywsl(0), stk_col(p, 2, c), None,
                                    OP.mult).then_inc(kb.sems["de"], 1)
                    v.wait_ge(kb.sems["dm"], dm0 + 14)
                    if u >= 2:
                        v.wait_ge(kb.sem("dyo"), u - 1)
                    v.tensor_scalar(evs[:, 0:64], ps[0][:, 0:64],
                                    stk_col(p, 1, c), None, OP.mult)\
                        .then_inc(kb.sems["de"], 1)
                    if SCAN:
                        v.wait_ge(kb.sems["de"], de0 + 21)
                    v.tensor_tensor(ytile[:, pi * 64:(pi + 1) * 64], evs[:, 0:64],
                                    ps[1][:, 0:64], OP.add)\
                        .then_inc(kb.sems["de"], 1)
                    v.wait_ge(kb.sems["dm"], dm0 + 15)
                    if SCAN:
                        v.wait_ge(kb.sems["de"], de0)
                    v.scalar_tensor_tensor(
                        S0[:, (p * 2 + 1 - cpp) * 64:(p * 2 + 1 - cpp) * 64 + 64],
                        S0[:, (p * 2 + cpp) * 64:(p * 2 + cpp) * 64 + 64],
                        stk_col(p, 5, c), ps[2][:, 0:64], OP.mult, OP.add)\
                        .then_inc(kb.sems["de"], 1)

            @blk.gpsimd
            def _(g):
                for u in range(NU):
                    c, p = u // 2, u % 2
                    pi = u % 2
                    g.wait_ge(kb.sems["de"], ED * u + 22)
                    for hh in range(2):
                        g.dma_start(y_my.ap()[c * 64:(c + 1) * 64,
                                              (p * 2 + hh) * 64:(p * 2 + hh + 1) * 64],
                                    ytile[hh * 64:(hh + 1) * 64,
                                          pi * 64:(pi + 1) * 64])\
                            .then_inc(kb.sems["dy"], 16)
                    g.wait_ge(kb.sems["dy"], 32 * (u + 1))
                    g.sem_inc(kb.sem("dyo"), 1)
        kb.cnt["dl"] = NB * NU
        kb.cnt["dm"] = PG * NU
        kb.cnt["de"] = ED * NU
        kb.cnt["dy"] = 32 * NU
        kb.cnt["dyo"] = NU

        if DEBUG:
            with nc.Block() as blk:
                @blk.sync
                def _(sync):
                    for ri, (dsrc, ddst) in enumerate(
                            ((kt_d, dbg_kt), (qt_d, dbg_qt), (vt_d, dbg_vt))):
                        for r in range(16):
                            kb.dma(sync, qb[:, 0:256],
                                   dsrc.ap()[r * 128:(r + 1) * 128, 0:256],
                                   name="dbg")
                            kb.wait(sync, "dbg")
                            kb.dma(sync, ddst[r * 128:(r + 1) * 128, 0:256],
                                   qb[:, 0:256], name="dbg")
                            kb.wait(sync, "dbg")
                    for r in range(16):
                        kb.dma(sync, qb[:, 0:256],
                               ag_xnf_o.ap()[0:128, r * 128:r * 128 + 256]
                               if r < 8 else
                               var_d.ap()[(r - 8) * 128:(r - 8) * 128 + 128,
                                          0:256],
                               name="dbg")
                        kb.wait(sync, "dbg")
                        kb.dma(sync, dbg_kf[0:128, r * 256:r * 256 + 256],
                               qb[:, 0:256], name="dbg")
                        kb.wait(sync, "dbg")
                    kb.dma(sync, dbg_kw[0:128, 0:776], wkv_bf[:, 0:776], name="dbg")
                    kb.wait(sync, "dbg")
                    kb.dma(sync, dbg_kw[0:128, 1024:1024 + 1024],
                           ag_xnf_i.ap()[0:128, 0:1024], name="dbg")
                    kb.wait(sync, "dbg")
                    for r in range(NTF):
                        kb.dma(sync, xt[:, 0:256], y_my.ap()[r * 128:(r + 1) * 128, :],
                               name="dbg")
                        kb.wait(sync, "dbg")
                        kb.dma(sync, dbg_y[r * 128:(r + 1) * 128, :], xt[:, 0:256],
                               name="dbg")
                        kb.wait(sync, "dbg")
                    for r in range(NTF):
                        kb.dma(sync, xt[:, 0:16], scal_d.ap()[r * 128:(r + 1) * 128, :],
                               name="dbg")
                        kb.wait(sync, "dbg")
                        kb.dma(sync, dbg_sc[r * 128:(r + 1) * 128, :], xt[:, 0:16],
                               name="dbg")
                        kb.wait(sync, "dbg")
                    kb.dma(sync, dbg_stk[:], stk[:], name="dbg")
                    kb.wait(sync, "dbg")


        phase('b5')
        for kc in range(KC):
            with nc.Block() as blk:
                @blk.sync
                def _(sync, kc=kc):
                    kb.dma(sync, xt[:, 0:256],
                           wg_in.rearrange("(n p) d -> n p d", p=128)[kc])
                    kb.wait(sync, "dml")
            with nc.Block() as blk:
                @blk.vector
                def _(v, kc=kc):
                    v.tensor_copy(wbig[:, kc * 256:(kc + 1) * 256], xt[:, 0:256])
        sg_t2 = sg_d.ap().rearrange("(n p) d -> n p d", p=128)
        kb.sem("b5l"); kb.sem("b5m"); kb.sem("b5e"); kb.sem("b5s")
        with nc.Block() as blk:
            @blk.sync
            def _(sync):
                for tt in range(NTF):
                    if tt >= 2:
                        sync.wait_ge(kb.sems["b5m"], tt - 1)
                    for kc in range(KC):
                        sync.dma_start(
                            xqf_r[:, (tt % 2) * 1024 + kc * 128:
                                  (tt % 2) * 1024 + (kc + 1) * 128],
                            ag_xnf_o.ap()[(tt // 16) * D + kc * 128:
                                          (tt // 16) * D + (kc + 1) * 128,
                                          (tt % 16) * 128:(tt % 16) * 128 + 128])\
                            .then_inc(kb.sem(f"b5l{tt % 2}"), 16)
            @blk.tensor
            def _(te):
                for tt in range(NTF):
                    te.wait_ge(kb.sem(f"b5l{tt % 2}"), (tt // 2 + 1) * KC * 16)
                    if tt >= 2:
                        te.wait_ge(kb.sems["b5e"], tt - 1)
                    for kc in range(KC):
                        mm = te.matmul(ps[tt % 2][:, 0:256],
                                       xqf_r[:, (tt % 2) * 1024 + kc * 128:
                                             (tt % 2) * 1024 + (kc + 1) * 128],
                                       wbig[:, kc * 256:(kc + 1) * 256],
                                       start=(kc == 0), stop=(kc == KC - 1))
                    mm.then_inc(kb.sems["b5m"], 1)
            @blk.scalar
            def _(s):
                def ysq_sq(u):
                    su = u % 4
                    s.wait_ge(kb.sem(f"b6q{su}"), (u // 4 + 1) * 16)
                    if u >= 4:
                        s.wait_ge(kb.sem("b6e2"), u - 3)
                    s.activation(q_fa[:, su * 256:(su + 1) * 256],
                                 q_fb[:, su * 256:(su + 1) * 256],
                                 AF.Square).then_inc(kb.sem("b6a2"), 1)

                def ysq_st(u):
                    su = u % 4
                    s.wait_ge(kb.sem("b6e2"), u + 1)
                    s.dma_start(sq_i.ap()[u * 128:(u + 1) * 128, :],
                                stat[:, su:su + 1])\
                        .then_inc(kb.sem(f"b6st{su}"), 16)
                for tt in range(NTF):
                    s.wait_ge(kb.sems["b5m"], tt + 1)
                    if tt >= 2:
                        s.wait_ge(kb.sem("b5so"), tt - 1)
                    s.activation(xnt[:, (tt % 2) * 256:(tt % 2) * 256 + 256],
                                 ps[tt % 2][:, 0:256], AF.Silu)\
                        .then_inc(kb.sems["b5e"], 1)
                    if tt >= 4:
                        s.wait_ge(kb.sem("b6a2"), tt - 3)
                    s.dma_start(q_fb[:, (tt % 4) * 256:(tt % 4 + 1) * 256],
                                y_my.ap()[tt * 128:(tt + 1) * 128, :])\
                        .then_inc(kb.sem(f"b6q{tt % 4}"), 16)
                    if tt >= 2:
                        ysq_sq(tt - 2)
                    if tt >= 5:
                        ysq_st(tt - 5)
                for u in (NTF - 2, NTF - 1):
                    ysq_sq(u)
                for u in range(NTF - 5, NTF):
                    ysq_st(u)
                for sl in range(4):
                    s.wait_ge(kb.sem(f"b6st{sl}"), (NTF // 4) * 16)
            @blk.gpsimd
            def _(g):
                for tt in range(NTF):
                    g.wait_ge(kb.sems["b5e"], tt + 1)
                    g.dma_start(sg_t2[tt], xnt[:, (tt % 2) * 256:(tt % 2) * 256 + 256])\
                        .then_inc(kb.sems["b5s"], 16)
                    g.wait_ge(kb.sems["b5s"], (tt + 1) * 16)
                    g.sem_inc(kb.sem("b5so"), 1)
            @blk.vector
            def _(v):
                for u in range(NTF):
                    su = u % 4
                    v.wait_ge(kb.sem("b6a2"), u + 1)
                    if u >= 4:
                        v.wait_ge(kb.sem(f"b6st{su}"), (u // 4) * 16)
                    v.tensor_reduce(stat[:, su:su + 1],
                                    q_fa[:, su * 256:(su + 1) * 256],
                                    AX.X, OP.add)\
                        .then_inc(kb.sem("b6e2"), 1)
        kb.cnt["b5l"] = NTF * KC * 16
        kb.cnt["b5m"] = NTF
        kb.cnt["b5e"] = NTF
        kb.cnt["b5s"] = NTF * 16
        kb.cnt["b5so"] = NTF
        phase('b6')
        with nc.Block() as blk:
            @blk.gpsimd
            def _(g):
                cs, _ = kb.inc("coll", 1)
                g.collective_compute("AllReduce", OP.add,
                                     replica_groups=[[0, 1, 2, 3], [4, 5, 6, 7]],
                                     ins=[sq_i.ap().opt()],
                                     outs=[sq_o.ap().opt()]).then_inc(cs, 1)
                g.wait_ge(kb.sems["coll"], kb.cnt["coll"])
        with nc.Block() as blk:
            @blk.sync
            def _(sync):
                kb.dma(sync, q_fb[0:1, 0:256], gn_in)
                kb.wait(sync, "dml")
        with nc.Block() as blk:
            @blk.tensor
            def _(te):
                te.matmul(ps[0][:, 0:256], ones1[:], q_fb[0:1, 0:256],
                          start=True, stop=True)
        with nc.Block() as blk:
            @blk.vector
            def _(v):
                v.tensor_copy(gnbt[:, 0:256], ps[0][:, 0:256])
        for kc2 in range(2):
            with nc.Block() as blk:
                @blk.sync
                def _(sync, kc2=kc2):
                    kb.dma(sync, xt[:, 0:D],
                           wo_in.rearrange("(n p) d -> n p d", p=128)[kc2])
                    kb.wait(sync, "dml")
            with nc.Block() as blk:
                @blk.vector
                def _(v, kc2=kc2):
                    v.tensor_copy(wbig[:, 4096 + kc2 * 1024:4096 + (kc2 + 1) * 1024],
                                  xt[:, 0:D])
        phase('rstd')
        phase('b7')
        atp_t = atp_d.ap().rearrange("(n p) d -> n p d", p=128)
        kb.sem("b7l"); kb.sem("b7z"); kb.sem("b7t"); kb.sem("b7m"); kb.sem("b7e")
        kb.sem("b7s")
        with nc.Block() as blk:
            @blk.sync
            def _(sync):
                def loads7(tt):
                    pi = tt % 2
                    sync.dma_start(yrow[pi][:, 0:256],
                                   y_my.ap()[tt * 128:(tt + 1) * 128, :])\
                        .then_inc(kb.sem(f"b7l{pi}"), 16)
                    sync.dma_start(q_fd[:, pi * 256:pi * 256 + 256], sg_t2[tt])\
                        .then_inc(kb.sem(f"b7l{pi}"), 16)
                    sync.dma_start(stat[:, 4 + pi:5 + pi],
                                   sq_o.ap()[tt * 128:(tt + 1) * 128, :])\
                        .then_inc(kb.sem(f"b7l{pi}"), 16)
                loads7(0)
                for tt in range(NTF):
                    pi = tt % 2
                    if tt + 1 < NTF:
                        if tt >= 1:
                            sync.wait_ge(kb.sems["b7z"], tt)
                        loads7(tt + 1)
                    sync.wait_ge(kb.sems["b7z"], tt + 1)
                    if tt >= 2:
                        sync.wait_ge(kb.sems["b7m"], tt - 1)
                    for j in range(2):
                        sync.dma_start(xqf_r[:, 2048 + pi * 256 + j * 128:
                                             2048 + pi * 256 + (j + 1) * 128],
                                       xnb[:, pi * 256 + j * 128:
                                           pi * 256 + (j + 1) * 128],
                                       transpose=True)\
                            .then_inc(kb.sem(f"b7t{pi}"), 16)
            @blk.vector
            def _(v):
                def zstep(tt):
                    pi = tt % 2
                    v.wait_ge(kb.sem(f"b7l{pi}"), (tt // 2 + 1) * 48)
                    if tt >= 2:
                        v.wait_ge(kb.sem(f"b7t{pi}"), (tt // 2) * 32)
                    v.tensor_scalar(stat[:, 6 + pi:7 + pi],
                                    stat[:, 4 + pi:5 + pi],
                                    1.0 / D, 1e-5, OP.mult, OP.add)\
                        .then_inc(kb.sem("b7v"), 1)
                    kb.flush(v, v.tensor_tensor(q_fa[:, pi * 256:pi * 256 + 256],
                                                q_fd[:, pi * 256:pi * 256 + 256],
                                                gnbt[:, 0:256], OP.mult))
                    v.wait_ge(kb.sem("b7a"), tt + 1)
                    kb.flush(v, v.reciprocal(stat[:, 6 + pi:7 + pi],
                                             stat[:, 6 + pi:7 + pi]))
                    kb.flush(v, v.tensor_scalar(q_fb[:, pi * 256:pi * 256 + 256],
                                                yrow[pi][:, 0:256],
                                                stat[:, 6 + pi:7 + pi],
                                                None, OP.mult))
                    v.tensor_tensor(xnb[:, pi * 256:pi * 256 + 256],
                                    q_fb[:, pi * 256:pi * 256 + 256],
                                    q_fa[:, pi * 256:pi * 256 + 256], OP.mult)\
                        .then_inc(kb.sems["b7z"], 1)

                def estep(tt):
                    pi = tt % 2
                    v.wait_ge(kb.sems["b7m"], tt + 1)
                    if tt >= 2:
                        v.wait_ge(kb.sem("b7so"), tt - 1)
                    v.tensor_copy(q_fc[:, 0:1024].bitcast(BF16)
                                  [:, pi * 1024:pi * 1024 + 512],
                                  ps[4 + 2 * pi][:, 0:512])
                    v.tensor_copy(q_fc[:, 0:1024].bitcast(BF16)
                                  [:, pi * 1024 + 512:pi * 1024 + 1024],
                                  ps[5 + 2 * pi][:, 0:512])\
                        .then_inc(kb.sems["b7e"], 1)
                for tt in range(NTF):
                    zstep(tt)
                    if tt >= 1:
                        estep(tt - 1)
                estep(NTF - 1)
            @blk.tensor
            def _(te):
                for tt in range(NTF):
                    pi = tt % 2
                    te.wait_ge(kb.sem(f"b7t{pi}"), (tt // 2 + 1) * 32)
                    if tt >= 2:
                        te.wait_ge(kb.sems["b7e"], tt - 1)
                    for half in range(2):
                        for j in range(2):
                            mm = te.matmul(ps[4 + 2 * pi + half][:, 0:512],
                                           xqf_r[:, 2048 + pi * 256 + j * 128:
                                                 2048 + pi * 256 + (j + 1) * 128],
                                           wbig[:, 4096 + j * 1024 + half * 512:
                                                4096 + j * 1024 + (half + 1) * 512],
                                           start=(j == 0), stop=(j == 1))
                    mm.then_inc(kb.sems["b7m"], 1)
            @blk.scalar
            def _(s):
                for tt in range(NTF):
                    pi = tt % 2
                    s.wait_ge(kb.sem("b7v"), tt + 1)
                    s.activation(stat[:, 6 + pi:7 + pi], stat[:, 6 + pi:7 + pi],
                                 AF.Sqrt).then_inc(kb.sem("b7a"), 1)
            @blk.gpsimd
            def _(g):
                for tt in range(NTF):
                    pi = tt % 2
                    g.wait_ge(kb.sems["b7e"], tt + 1)
                    g.dma_start(atp_t[tt],
                                q_fc[:, 0:1024].bitcast(BF16)
                                [:, pi * 1024:pi * 1024 + 1024])\
                        .then_inc(kb.sems["b7s"], 16)
                    g.wait_ge(kb.sems["b7s"], (tt + 1) * 16)
                    g.sem_inc(kb.sem("b7so"), 1)
        kb.cnt["b7l"] = NTF * 48
        kb.cnt["b7z"] = NTF
        kb.cnt["b7t"] = NTF * 32
        kb.cnt["b7m"] = NTF
        kb.cnt["b7e"] = NTF
        kb.cnt["b7s"] = NTF * 16
        kb.cnt["b7so"] = NTF
        kb.cnt["b7v"] = NTF
        kb.cnt["b7a"] = NTF
        phase('rs')
        with nc.Block() as blk:
            @blk.gpsimd
            def _(g):
                cs, _ = kb.inc("coll", 1)
                g.collective_compute("ReduceScatter", OP.add,
                                     replica_groups=[[0, 1, 2, 3], [4, 5, 6, 7]],
                                     ins=[atp_d.ap().opt()],
                                     outs=[atr_d.ap().opt()]).then_inc(cs, 1)
                g.wait_ge(kb.sems["coll"], kb.cnt["coll"])
        atr_t = atr_d.ap().rearrange("(n p) d -> n p d", p=128)
        phase('fin')
        for it in range(NT):
            with nc.Block() as blk:
                @blk.sync
                def _(sync, it=it):
                    kb.dma(sync, xt[:, 0:D], x_t[it], name="a5x")
                    kb.dma(sync, yrow[1][:], ffn_t[it], name="a5x")
                    kb.dma(sync, xnb[:, 0:D], atr_t[it], name="a5x")
                    kb.wait(sync, "a5x")
            with nc.Block() as blk:
                @blk.vector
                def _(v):
                    sflush(v, v.tensor_copy(q_fa[:, 0:D], xnb[:, 0:D]))
                    sflush(v, v.tensor_tensor(xnt[:], xt[:, 0:D], yrow[1][:],
                                              OP.add))
                    sflush(v, v.tensor_tensor(xnt[:], xnt[:], q_fa[:, 0:D],
                                              OP.add))
            with nc.Block() as blk:
                @blk.sync
                def _(sync, it=it):
                    kb.dma(sync, out_t[it], xnt[:], name="dst")
                    kb.wait(sync, "dst")

    else:
        for it in range(NT):
            with nc.Block() as blk:
                @blk.sync
                def _(sync, it=it):
                    kb.dma(sync, xt[:, 0:D], x_t[it])
                    kb.dma(sync, yrow[0][:], ffn_t[it])
                    kb.wait(sync, "dml")
            with nc.Block() as blk:
                @blk.vector
                def _(v):
                    sflush(v, v.tensor_tensor(xnt[:], xt[:, 0:D], yrow[0][:],
                                              OP.add))
            with nc.Block() as blk:
                @blk.sync
                def _(sync, it=it):
                    kb.dma(sync, out_t[it], xnt[:], name="dst")
                    kb.wait(sync, "dst")

    phase(None)
    if DEBUG:
        with nc.Block() as blk:
            @blk.sync
            def _(sync):
                kb.dma(sync, dbg_ffn, ffn_d.ap(), name="dbg2")
                kb.dma(sync, dbg_sg, sg_d.ap(), name="dbg2")
                kb.dma(sync, dbg_atr, atr_d.ap(), name="dbg2")
                kb.dma(sync, dbg_hq, hqT_d.ap(), name="dbg2")
                kb.dma(sync, dbg_sq, sq_o.ap(), name="dbg2")
                kb.dma(sync, dbg_atp, atp_d.ap(), name="dbg2")
                kb.wait(sync, "dbg2")

    return kb


def make_consts():
    i = np.arange(128)
    hhp = i // 64
    tp = i % 64
    low = ((hhp[:, None] == hhp[None, :]) & (tp[None, :] < tp[:, None])).astype(np.float32)
    upp = ((hhp[:, None] == hhp[None, :]) & (tp[None, :] > tp[:, None])).astype(np.float32)
    upi = ((hhp[:, None] == hhp[None, :]) & (tp[None, :] >= tp[:, None])).astype(np.float32)
    return low, upp, upi


def kernel(**inputs):
    x = np.ascontiguousarray(np.asarray(inputs["L0_attn_x"], np.float32))
    Wq = np.asarray(inputs["L0_attn_Wq"], np.float32)
    Wk = np.asarray(inputs["L0_attn_Wk"], np.float32)
    Wv = np.asarray(inputs["L0_attn_Wv"], np.float32)
    Wa = np.asarray(inputs["L0_attn_Wa"], np.float32)
    ba = np.asarray(inputs["L0_attn_ba"], np.float32)
    Wb = np.asarray(inputs["L0_attn_Wb"], np.float32)
    bb = np.asarray(inputs["L0_attn_bb"], np.float32)
    Wg = np.ascontiguousarray(np.asarray(inputs["L0_attn_Wg"], np.float32))
    Wo = np.ascontiguousarray(np.asarray(inputs["L0_attn_Wo"], np.float32))
    gn = np.asarray(inputs["L0_attn_g_norm"], np.float32)
    wgate = np.asarray(inputs["L0_ffn_w_gate"], np.float32)
    wup = np.asarray(inputs["L0_ffn_w_up"], np.float32)
    wdown = np.asarray(inputs["L0_ffn_w_down"], np.float32)
    low, upp, upi = make_consts()
    trj = np.tril(np.ones((64, 64), np.float32)).T.copy()
    tri2 = np.ascontiguousarray(np.concatenate([trj, trj], 0))

    kb = build()
    in_maps = []
    for c in range(8):
        b, qq = c // 4, c % 4
        hc = slice(qq * 256, (qq + 1) * 256)
        wkvqab = np.concatenate([Wv[:, hc], Wk[:, hc], Wq[:, hc],
                                 Wa[:, 4 * qq:4 * qq + 4], Wb[:, 4 * qq:4 * qq + 4]], 1)
        bab = np.concatenate([ba[4 * qq:4 * qq + 4], bb[4 * qq:4 * qq + 4]])[None, :]
        in_maps.append({
            "x": np.ascontiguousarray(x[b, qq * T:(qq + 1) * T]),
            "wkvqab": np.ascontiguousarray(wkvqab),
            "bab": np.ascontiguousarray(bab),
            "wgm": np.ascontiguousarray(Wg[:, hc]),
            "wom": np.ascontiguousarray(Wo[qq * 256:(qq + 1) * 256, :]),
            "gnm": np.ascontiguousarray(gn[None, qq * 256:(qq + 1) * 256]),
            "wgate": np.ascontiguousarray(wgate[c * (I3 // 8):(c + 1) * (I3 // 8)]),
            "wup": np.ascontiguousarray(wup[c * (I3 // 8):(c + 1) * (I3 // 8)]),
            "wdown": np.ascontiguousarray(wdown[c * (D // 8):(c + 1) * (D // 8)]),
            "mlow": low, "mupp": upp, "mupi": upi, "tri": tri2,
        })
    res = run_bass_kernel_spmd(kb.nc, in_maps, core_ids=list(range(8)))
    out = np.zeros((2, S, D), np.float32)
    for c in range(8):
        b, qq = c // 4, c % 4
        out[b, qq * T:(qq + 1) * T] = res.results[c]["out"]
    return out


